# version 42
# baseline (speedup 1.0000x reference)
import numpy as np
import ml_dtypes
import concourse.bass as bass
import concourse.mybir as mybir
from concourse.bass_utils import run_bass_kernel_spmd

F32 = mybir.dt.float32
BF16 = mybir.dt.bfloat16
U8 = mybir.dt.uint8
AF = mybir.ActivationFunctionType
ALU = mybir.AluOpType
AX = mybir.AxisListType

B = 8
S = 2048
D = 1024
NT = 16
NSC = 4
H = 16
E = 8
SOFF = [0, 512, 896, 1152]


def build_nc(dbg=False):
    nc = bass.Bass("TRN2", target_bir_lowering=False,
                   detect_race_conditions=False, num_swdge_queues=1)

    x_d = nc.dram_tensor("x", [S, D], F32, kind="ExternalInput")
    wqT_d = nc.dram_tensor("wqT", [D, D], F32, kind="ExternalInput")
    wkT_d = nc.dram_tensor("wkT", [D, D], F32, kind="ExternalInput")
    wvT_d = nc.dram_tensor("wvT", [D, D], F32, kind="ExternalInput")
    woT_d = nc.dram_tensor("woT", [D, D], F32, kind="ExternalInput")
    bqc_d = nc.dram_tensor("bq_col", [128, 8], F32, kind="ExternalInput")
    bkc_d = nc.dram_tensor("bk_col", [128, 8], F32, kind="ExternalInput")
    bkr_d = nc.dram_tensor("bk_row", [1, D], F32, kind="ExternalInput")
    bvr_d = nc.dram_tensor("bv_row", [1, D], F32, kind="ExternalInput")
    bor_d = nc.dram_tensor("bo_row", [1, D], F32, kind="ExternalInput")
    gwT_d = nc.dram_tensor("gwT", [D, E], F32, kind="ExternalInput")
    gup_d = nc.dram_tensor("gup", [E, D, 2 * D], F32, kind="ExternalInput")
    dwn_d = nc.dram_tensor("dwn", [E, D, D], F32, kind="ExternalInput")
    mask_d = nc.dram_tensor("mask", [128, 512], F32, kind="ExternalInput")
    idn_d = nc.dram_tensor("idn", [128, 128], BF16, kind="ExternalInput")
    onesr_d = nc.dram_tensor("ones_row", [1, 128], BF16, kind="ExternalInput")
    epsc_d = nc.dram_tensor("epsc", [128, 1], F32, kind="ExternalInput")
    y_d = nc.dram_tensor("y", [S, D], F32, kind="ExternalOutput")

    x_t = nc.alloc_sbuf_tensor("x_t", [128, NT, D], F32)
    h_t = nc.alloc_sbuf_tensor("h_t", [128, 4, D], BF16)
    stat = nc.alloc_sbuf_tensor("stat", [128, NT, 12], F32)
    rstat = nc.alloc_sbuf_tensor("rstat", [128, 16, 8], F32)
    recip = nc.alloc_sbuf_tensor("recip", [128, 2, 4], F32)
    rt_e = nc.alloc_sbuf_tensor("rt_e", [128, 4, 8], F32)
    msk1 = nc.alloc_sbuf_tensor("msk1", [128, 4, 8], F32)
    tmp8 = nc.alloc_sbuf_tensor("tmp8", [128, 4, 8], F32)
    silu_sb = nc.alloc_sbuf_tensor("silu_sb", [128, 2, 512], BF16)
    mask_s = nc.alloc_sbuf_tensor("mask_s", [128, 512], F32)
    idn_s = nc.alloc_sbuf_tensor("idn_s", [128, 128], BF16)
    onesr_s = nc.alloc_sbuf_tensor("onesr_s", [1, 128], BF16)
    bqc_s = nc.alloc_sbuf_tensor("bqc_s", [128, 8], F32)
    bkc_s = nc.alloc_sbuf_tensor("bkc_s", [128, 8], F32)
    bkr_s = nc.alloc_sbuf_tensor("bkr_s", [1, D], BF16)
    bvr_s = nc.alloc_sbuf_tensor("bvr_s", [1, D], BF16)
    bor_s = nc.alloc_sbuf_tensor("bor_s", [1, D], BF16)
    epsc_s = nc.alloc_sbuf_tensor("epsc_s", [128, 1], F32)
    routed = nc.alloc_sbuf_tensor("routed", [128, NT, E], F32)
    gw_bf = nc.alloc_sbuf_tensor("gw_bf", [128, 8, E], BF16)

    def al32(v):
        return (v + 31) & ~31

    a_specs = [
        ("q_f", [128, 8, 512], BF16), ("k_f", [128, 8, 512], BF16),
        ("k_t", [128, 4, D], BF16), ("v_t", [128, 4, H * 65], BF16),
        ("s_sb", [128, 2, 1280], BF16), ("ao_t", [128, 4, D], BF16),
        ("ao_f", [128, 8, 128], BF16), ("M32", [128, 8, 65], F32),
        ("M_bf", [128, 8, 65], BF16), ("h_f", [128, 8, 512], BF16),
        ("wq_bf", [128, 8, D], BF16), ("wk_bf", [128, 8, D], BF16),
        ("wv_bf", [128, 8, D], BF16), ("wo_bf", [128, 8, D], BF16),
    ]
    offs, cur = {}, 0
    for name, shp, dt in a_specs:
        offs[name] = cur
        cur = al32(cur + int(np.prod(shp[1:])) * mybir.dt.size(dt))
    pool_bytes = cur
    b_specs = [
        ("gup_bf", [128, 8, 2 * D], BF16, offs["q_f"]),
        ("dwn_bf", [128, 8, D], BF16, offs["s_sb"]),
        ("z_sb", [128, 8, 2 * D], BF16, offs["wq_bf"]),
        ("h2_f", [128, 8, S], BF16, offs["wv_bf"]),
    ]
    assert offs["s_sb"] - offs["q_f"] >= 8 * 2 * D * 2
    assert offs["h_f"] - offs["s_sb"] >= 8 * D * 2
    assert offs["wv_bf"] - offs["wq_bf"] >= 8 * 2 * D * 2
    assert pool_bytes - offs["wv_bf"] >= 8 * S * 2
    pool = nc.alloc_sbuf_tensor("pool", [128, pool_bytes], U8)
    base = nc.lookup_mloc(pool).addr
    TT = {}
    for name, shp, dt in a_specs:
        TT[name] = nc.alloc_sbuf_tensor_at(name, shp, dt, offset=base + offs[name])
    for name, shp, dt, off in b_specs:
        TT[name] = nc.alloc_sbuf_tensor_at(name, shp, dt, offset=base + off)
    TT["ao_f2"] = nc.alloc_sbuf_tensor_at("ao_f2", [128, 8, 128], BF16,
                                          offset=base + offs["h_f"])
    q_f, k_f, k_t, v_t = TT["q_f"], TT["k_f"], TT["k_t"], TT["v_t"]
    s_sb, ao_t, ao_f = TT["s_sb"], TT["ao_t"], TT["ao_f"]
    ao_f2 = TT["ao_f2"]
    M32, M_bf, h_f = TT["M32"], TT["M_bf"], TT["h_f"]
    wq_bf, wk_bf, wv_bf, wo_bf = TT["wq_bf"], TT["wk_bf"], TT["wv_bf"], TT["wo_bf"]
    gup_bf, dwn_bf, z_sb, h2_f = TT["gup_bf"], TT["dwn_bf"], TT["z_sb"], TT["h2_f"]

    pb = [nc.alloc_psum_tensor(f"pb{i}", [128, 512], F32) for i in range(6)]
    tr = [nc.alloc_psum_tensor(f"tr{i}", [128, 1024], BF16) for i in range(2)]

    sems = {n: nc.alloc_semaphore(name=n)
            for n in ["dIN", "dW", "dOUT", "tS", "vS", "sS", "gS"]}

    dbg_d = {}
    if dbg:
        for name, shp, dt in [
            ("d_hf", [128, 8, 512], BF16), ("d_qf", [128, 8, 512], BF16),
            ("d_kf", [128, 8, 512], BF16), ("d_kt", [128, 4, D], BF16),
            ("d_vt", [128, 4, H * 65], BF16), ("d_aot", [128, 4, D], BF16),
            ("d_m32", [128, 8, 65], F32), ("d_xa", [128, NT, D], F32),
            ("d_h2f", [128, 8, S], BF16), ("d_routed", [128, NT, E], F32),
            ("d_z0", [128, 8, 2 * D], BF16),
        ]:
            dbg_d[name] = nc.dram_tensor(name, shp, dt, kind="ExternalOutput")

    def v_head(tc, h):
        return v_t[:, tc].rearrange("p (h c) -> p h c", c=65)[:, h, :]

    def walk(which, eng):
        c = {"dIN": 0, "dW": 0, "dOUT": 0, "tS": 0, "vS": 0, "sS": 0, "gS": 0}

        def op(engine, fn, inc=(), wait=()):
            if which == engine:
                for sn, v in wait:
                    if v > 0:
                        eng.wait_ge(sems[sn], v)
                ins = fn(eng)
                for sn, n in inc:
                    ins.then_inc(sems[sn], n)
            for sn, n in inc:
                c[sn] += n

        def tap(name, src_ap, wait):
            if not dbg:
                return
            op("sync", lambda e: e.dma_start(dbg_d[name].ap(), src_ap),
               wait=wait, inc=[("dOUT", 16)])
            tgt = c["dOUT"]
            if which in ("tensor", "vector", "scalar", "gpsimd"):
                eng.wait_ge(sems["dOUT"], tgt)

        op("vector", lambda e: e.memset(stat[:, 0, 11:12], 1.0), inc=[("vS", 1)])
        V_WARM = c["vS"]
        op("scalar", lambda e: e.activation(
            stat[:, 0, 10:11], stat[:, 0, 11:12], AF.Square),
            wait=[("vS", V_WARM)])
        op("scalar", lambda e: e.activation(
            stat[:, 0, 10:11], stat[:, 0, 11:12], AF.Ln,
            bias=stat[:, 0, 11:12]))
        op("scalar", lambda e: e.activation(
            stat[:, 0, 10:11], stat[:, 0, 11:12], AF.Exp, scale=-0.5))

        op("sync", lambda e: e.dma_start(
            x_t[:, 0:4, :],
            x_d.ap().rearrange("(c p) d -> p c d", p=128)[:, 0:4, :]),
            inc=[("dIN", 16)])
        IN_X = c["dIN"]
        for dst, src in [(mask_s[:, :], mask_d[:, :]), (idn_s[:, :], idn_d[:, :]),
                         (onesr_s[:, :], onesr_d[:, :]), (bqc_s[:, :], bqc_d[:, :]),
                         (bkc_s[:, :], bkc_d[:, :]), (epsc_s[:, :], epsc_d[:, :])]:
            op("sync", lambda e, d=dst, s=src: e.dma_start(d, s), inc=[("dIN", 16)])
        IN_CONST = c["dIN"]
        op("sync", lambda e: e.dma_start(
            x_t[:, 4:8, :],
            x_d.ap().rearrange("(c p) d -> p c d", p=128)[:, 4:8, :]),
            inc=[("dIN", 16)])
        IN_47 = c["dIN"]
        IN_ALL_BOX = [0]

        op("gpsimd", lambda e: e.memset(M32[:, :, :], 0.0), inc=[("gS", 1)])
        op("gpsimd", lambda e: e.memset(
            v_t[:, :, :].rearrange("p t (h c) -> p t h c", c=65)[:, :, :, 64:65], 1.0),
            inc=[("gS", 1)])
        MEMSET = c["gS"]
        for dst, src in [(bkr_s, bkr_d), (bvr_s, bvr_d), (bor_s, bor_d)]:
            op("gpsimd", lambda e, a=dst, b=src: e.dma_start(a[:, :], b[:, :]),
               inc=[("dW", 16)])
        DW_W = {}
        for nm, wbf, wd in [("v", wv_bf, wvT_d), ("k", wk_bf, wkT_d),
                            ("q", wq_bf, wqT_d), ("o", wo_bf, woT_d)]:
            for ch in range(2):
                op("gpsimd", lambda e, a=wbf, b=wd, ch=ch: e.dma_start(
                    a[:, ch * 4:(ch + 1) * 4, :],
                    b.ap().rearrange("(c p) n -> p c n", p=128)[
                        :, ch * 4:(ch + 1) * 4, :]),
                    wait=([("dIN", IN_X)] if (nm == "v" and ch == 0) else []),
                    inc=[("dW", 16)])
            DW_W[nm] = c["dW"]
        op("sync", lambda e: e.dma_start(
            x_t[:, 8:16, :],
            x_d.ap().rearrange("(c p) d -> p c d", p=128)[:, 8:16, :]),
            wait=[("dW", DW_W["k"])], inc=[("dIN", 16)])
        IN_ALL = c["dIN"]
        IN_ALL_BOX[0] = IN_ALL

        TR_FREE = [("sS", 0), ("sS", 0)]
        T_tr = [0] * NT
        HF_EVAC = [0] * NT
        ST_READY = {}
        V_HT = {}
        H_T_FREE = [0]
        LN2_JUNK = [0]
        LN2_VAR = {}
        VS1 = {}

        def ln_stats(tiles, xw):
            sts = [stat[:, tc] for tc in tiles]
            for i, tc in enumerate(tiles):
                op("vector", lambda e, st=sts[i], tc=tc: e.reduce_sum(
                    st[:, 0:1], x_t[:, tc, :], AX.X),
                    wait=(xw if i == 0 else []),
                    inc=([("vS", 1)] if i == 3 else []))
            v_s1 = c["vS"]
            for i, tc in enumerate(tiles):
                op("scalar", lambda e, st=sts[i], tc=tc: e.activation(
                    silu_sb[:, 0, :], x_t[:, tc, 0:512], AF.Square,
                    accum_out=st[:, 1:2]),
                    wait=(xw if i == 0 else []))
                op("scalar", lambda e, st=sts[i], tc=tc: e.activation(
                    silu_sb[:, 0, :], x_t[:, tc, 512:1024], AF.Square,
                    accum_out=st[:, 2:3]))
            for i in range(4):
                op("scalar", lambda e, st=sts[i]: e.mul(st[:, 4:5], st[:, 0:1], 1.0 / D),
                   wait=([("vS", v_s1)] if i == 0 else []))
            for i in range(4):
                op("scalar", lambda e, st=sts[i]: e.add(st[:, 3:4], st[:, 1:2], st[:, 2:3]))
            for i in range(4):
                op("scalar", lambda e, st=sts[i]: e.mul(st[:, 5:6], st[:, 4:5], st[:, 4:5]))
            for i in range(4):
                op("scalar", lambda e, st=sts[i]: e.mul(st[:, 6:7], st[:, 5:6], -1.0))
            for i in range(4):
                op("scalar", lambda e, st=sts[i]: e.mul(st[:, 11:12], st[:, 3:4], 1.0 / D))
            for i in range(4):
                op("scalar", lambda e, st=sts[i]: e.add(st[:, 7:8], st[:, 11:12], st[:, 6:7]))
            for i in range(4):
                op("scalar", lambda e, st=sts[i]: e.activation(
                    st[:, 8:9], st[:, 7:8], AF.Ln, bias=epsc_s[:, 0:1]))
            for i in range(4):
                op("scalar", lambda e, st=sts[i]: e.activation(
                    st[:, 9:10], st[:, 8:9], AF.Exp, scale=-0.5))
            for i in range(4):
                op("scalar", lambda e, st=sts[i]: e.mul(st[:, 10:11], st[:, 4:5], st[:, 9:10]),
                   inc=([("sS", 1)] if i == 3 else []))
            ST_READY[tiles[0]] = ("sS", c["sS"])

        def ln2_stats_v(tiles, sc):
            sts = [stat[:, tc] for tc in tiles]
            for i, tc in enumerate(tiles):
                op("vector", lambda e, st=sts[i], tc=tc: e.reduce_sum(
                    st[:, 0:1], x_t[:, tc, :], AX.X))
            for i, tc in enumerate(tiles):
                op("vector", lambda e, st=sts[i], tc=tc, i=i: e.scalar_tensor_tensor(
                    ao_t[:, i, :], x_t[:, tc, :], 1.0, x_t[:, tc, :],
                    ALU.mult, ALU.mult, accum_out=st[:, 1:2]),
                    wait=([("tS", H_T_FREE[0])] if i == 0 else []),
                    inc=([("vS", 1)] if i == 3 else []))
            LN2_JUNK[0] = c["vS"]
            for i in range(4):
                op("vector", lambda e, st=sts[i]: e.tensor_scalar(
                    st[:, 4:5], st[:, 0:1], 1.0 / D, None, ALU.mult))
            for i in range(4):
                op("vector", lambda e, st=sts[i]: e.tensor_mul(st[:, 5:6], st[:, 4:5], st[:, 4:5]))
            for i in range(4):
                op("vector", lambda e, st=sts[i]: e.tensor_scalar(
                    st[:, 11:12], st[:, 1:2], 1.0 / D, None, ALU.mult))
            for i in range(4):
                op("vector", lambda e, st=sts[i]: e.tensor_sub(st[:, 7:8], st[:, 11:12], st[:, 5:6]),
                   inc=([("vS", 1)] if i == 3 else []))
            LN2_VAR[tiles[0]] = c["vS"]

        def ln2_stats_fin(tiles):
            sts = [stat[:, tc] for tc in tiles]
            v_var = LN2_VAR[tiles[0]]
            for i in range(4):
                op("scalar", lambda e, st=sts[i]: e.activation(
                    st[:, 8:9], st[:, 7:8], AF.Ln, bias=epsc_s[:, 0:1]),
                    wait=([("vS", v_var)] if i == 0 else []))
            for i in range(4):
                op("scalar", lambda e, st=sts[i]: e.activation(
                    st[:, 9:10], st[:, 8:9], AF.Exp, scale=-0.5),
                    inc=([("sS", 1)] if i == 3 else []))
            s_exp = c["sS"]
            for i in range(4):
                op("vector", lambda e, st=sts[i]: e.tensor_mul(st[:, 10:11], st[:, 4:5], st[:, 9:10]),
                   wait=([("sS", s_exp)] if i == 0 else []),
                   inc=([("vS", 1)] if i == 3 else []))
            ST_READY[tiles[0]] = ("vS", c["vS"])

        def ln2_s_pieces(tiles):
            sts = [stat[:, tc] for tc in tiles]

            def sq(i0):
                for i in (i0, i0 + 1):
                    op("scalar", lambda e, st=sts[i], tc=tiles[i], i=i: e.activation(
                        ao_t[:, i, :], x_t[:, tc, :], AF.Square,
                        accum_out=st[:, 1:2]),
                        wait=([("vS", V_wo_evac[NSC - 1])] if i == 0 else []))

            def cp(i0):
                for i in (i0, i0 + 1):
                    op("scalar", lambda e, st=sts[i], tc=tiles[i], i=i: e.activation(
                        ao_t[:, i, :], x_t[:, tc, :], AF.Copy,
                        accum_out=st[:, 0:1]),
                        inc=([("sS", 1)] if i == 3 else []))
                if i0 == 2:
                    LN2_JUNK[0] = c["sS"]

            def mid1():
                for i in range(4):
                    op("scalar", lambda e, st=sts[i]: e.mul(st[:, 4:5], st[:, 0:1], 1.0 / D))
                for i in range(4):
                    op("scalar", lambda e, st=sts[i]: e.mul(st[:, 5:6], st[:, 4:5], st[:, 4:5]))

            def mid2():
                for i in range(4):
                    op("scalar", lambda e, st=sts[i]: e.mul(st[:, 6:7], st[:, 5:6], -1.0))
                for i in range(4):
                    op("scalar", lambda e, st=sts[i]: e.mul(st[:, 11:12], st[:, 1:2], 1.0 / D))

            def mid3():
                for i in range(4):
                    op("scalar", lambda e, st=sts[i]: e.add(st[:, 7:8], st[:, 11:12], st[:, 6:7]))

            def fin1():
                for i in range(4):
                    op("scalar", lambda e, st=sts[i]: e.activation(
                        st[:, 8:9], st[:, 7:8], AF.Ln, bias=epsc_s[:, 0:1]))

            def fin2():
                for i in range(4):
                    op("scalar", lambda e, st=sts[i]: e.activation(
                        st[:, 9:10], st[:, 8:9], AF.Exp, scale=-0.5))
                for i in range(4):
                    op("scalar", lambda e, st=sts[i]: e.mul(
                        st[:, 10:11], st[:, 4:5], st[:, 9:10]),
                       inc=([("sS", 1)] if i == 3 else []))
                ST_READY[tiles[0]] = ("sS", c["sS"])

            return {0: lambda: sq(0), 1: lambda: sq(2), 2: lambda: cp(0),
                    3: lambda: cp(2), 4: mid1, 5: mid2, 6: mid3,
                    7: lambda: (fin1(), fin2())}

        def ln_ts(tiles):
            sts = [stat[:, tc] for tc in tiles]
            for i, tc in enumerate(tiles):
                op("vector", lambda e, st=sts[i], tc=tc, i=i: e.tensor_scalar(
                    h_t[:, i, :], x_t[:, tc, :],
                    st[:, 9:10], st[:, 10:11], ALU.mult, ALU.subtract),
                    wait=([ST_READY[tiles[0]], ("tS", H_T_FREE[0])]
                          if i == 0 else []),
                    inc=([("vS", 1)] if i == 3 else []))
            V_HT[tiles[0]] = c["vS"]

        def ln_tr(tiles, dest_f, dst_cols, ev="scalar"):
            for i, tc in enumerate(tiles):
                trb = tr[i % 2]
                evi = ("scalar" if i % 2 == 0 else "vector") if ev == "mix" else ev
                es = "sS" if evi == "scalar" else "vS"
                for cc in range(8):
                    op("tensor", lambda e, cc=cc, i=i, trb=trb: e.transpose(
                        trb[:, cc * 128:(cc + 1) * 128],
                        h_t[:, i, cc * 128:(cc + 1) * 128], idn_s[:, :]),
                        wait=([("vS", V_HT[tiles[0]]), TR_FREE[i % 2],
                               ("dIN", IN_CONST)] if cc == 0 else []),
                        inc=([("tS", 1)] if cc == 7 else []))
                T_tr[tc] = c["tS"]
                cpf = ((lambda e, o, s: e.copy(o, s)) if evi == "scalar"
                       else (lambda e, o, s: e.tensor_copy(o, s)))
                for cc in range(8):
                    op(evi, lambda e, cc=cc, trb=trb, dc=dst_cols[i]: cpf(
                        e, dest_f[:, cc, dc * 128:(dc + 1) * 128],
                        trb[:, cc * 128:(cc + 1) * 128]),
                        wait=([("tS", T_tr[tc])] if cc == 0 else []),
                        inc=([(es, 1)] if cc == 7 else []))
                HF_EVAC[tc] = (es, c[es])
                TR_FREE[i % 2] = HF_EVAC[tc]
            H_T_FREE[0] = T_tr[tiles[3]]

        def ln_stats_pieces(tiles, xw):
            sts = [stat[:, tc] for tc in tiles]

            def pA():
                for i, tc in enumerate(tiles):
                    op("vector", lambda e, st=sts[i], tc=tc: e.reduce_sum(
                        st[:, 0:1], x_t[:, tc, :], AX.X),
                        wait=(xw if i == 0 else []),
                        inc=([("vS", 1)] if i == 3 else []))
                self_v = c["vS"]
                for i, tc in enumerate(tiles):
                    op("scalar", lambda e, st=sts[i], tc=tc: e.activation(
                        silu_sb[:, 0, :], x_t[:, tc, 0:512], AF.Square,
                        accum_out=st[:, 1:2]),
                        wait=(xw if i == 0 else []))
                    op("scalar", lambda e, st=sts[i], tc=tc: e.activation(
                        silu_sb[:, 0, :], x_t[:, tc, 512:1024], AF.Square,
                        accum_out=st[:, 2:3]))
                VS1[tiles[0]] = self_v

            def pB():
                for i in range(4):
                    op("scalar", lambda e, st=sts[i]: e.mul(st[:, 4:5], st[:, 0:1], 1.0 / D),
                       wait=([("vS", VS1[tiles[0]])] if i == 0 else []))
                for i in range(4):
                    op("scalar", lambda e, st=sts[i]: e.add(st[:, 3:4], st[:, 1:2], st[:, 2:3]))
                for i in range(4):
                    op("scalar", lambda e, st=sts[i]: e.mul(st[:, 5:6], st[:, 4:5], st[:, 4:5]))
                for i in range(4):
                    op("scalar", lambda e, st=sts[i]: e.mul(st[:, 6:7], st[:, 5:6], -1.0))

            def pC():
                for i in range(4):
                    op("scalar", lambda e, st=sts[i]: e.mul(st[:, 11:12], st[:, 3:4], 1.0 / D))
                for i in range(4):
                    op("scalar", lambda e, st=sts[i]: e.add(st[:, 7:8], st[:, 11:12], st[:, 6:7]))
                for i in range(4):
                    op("scalar", lambda e, st=sts[i]: e.activation(
                        st[:, 8:9], st[:, 7:8], AF.Ln, bias=epsc_s[:, 0:1]))
                for i in range(4):
                    op("scalar", lambda e, st=sts[i]: e.activation(
                        st[:, 9:10], st[:, 8:9], AF.Exp, scale=-0.5))

            def pD():
                for i in range(4):
                    op("scalar", lambda e, st=sts[i]: e.mul(
                        st[:, 10:11], st[:, 4:5], st[:, 9:10]),
                        inc=([("sS", 1)] if i == 3 else []))
                ST_READY[tiles[0]] = ("sS", c["sS"])
            return [pA, pB, pC, pD]

        def ln2v_pieces(tiles):
            sts = [stat[:, tc] for tc in tiles]

            def pE():
                for i, tc in enumerate(tiles):
                    op("vector", lambda e, st=sts[i], tc=tc: e.reduce_sum(
                        st[:, 0:1], x_t[:, tc, :], AX.X))
                for i, tc in enumerate(tiles):
                    op("vector", lambda e, st=sts[i], tc=tc, i=i: e.scalar_tensor_tensor(
                        h_t[:, i, :], x_t[:, tc, :], 1.0, x_t[:, tc, :],
                        ALU.mult, ALU.mult, accum_out=st[:, 1:2]),
                        wait=([("tS", H_T_FREE[0])] if i == 0 else []))

            def pF():
                for i in range(4):
                    op("vector", lambda e, st=sts[i]: e.tensor_scalar(
                        st[:, 4:5], st[:, 0:1], 1.0 / D, None, ALU.mult))
                for i in range(4):
                    op("vector", lambda e, st=sts[i]: e.tensor_mul(st[:, 5:6], st[:, 4:5], st[:, 4:5]))
                for i in range(4):
                    op("vector", lambda e, st=sts[i]: e.tensor_scalar(
                        st[:, 11:12], st[:, 1:2], 1.0 / D, None, ALU.mult))
                for i in range(4):
                    op("vector", lambda e, st=sts[i]: e.tensor_sub(
                        st[:, 7:8], st[:, 11:12], st[:, 5:6]),
                        inc=([("vS", 1)] if i == 3 else []))
                LN2_VAR[tiles[0]] = c["vS"]

            def pG():
                v_var = LN2_VAR[tiles[0]]
                for i in range(4):
                    op("scalar", lambda e, st=sts[i]: e.activation(
                        st[:, 8:9], st[:, 7:8], AF.Ln, bias=epsc_s[:, 0:1]),
                        wait=([("vS", v_var)] if i == 0 else []))
                for i in range(4):
                    op("scalar", lambda e, st=sts[i]: e.activation(
                        st[:, 9:10], st[:, 8:9], AF.Exp, scale=-0.5),
                        inc=([("sS", 1)] if i == 3 else []))
                s_exp = c["sS"]
                for i in range(4):
                    op("vector", lambda e, st=sts[i]: e.tensor_mul(
                        st[:, 10:11], st[:, 4:5], st[:, 9:10]),
                        wait=([("sS", s_exp)] if i == 0 else []),
                        inc=([("vS", 1)] if i == 3 else []))
                ST_READY[tiles[0]] = ("vS", c["vS"])
            return [pE, pF, pG]

        NB_FREE = [0] * H
        S_AO = [0] * H
        V_M32ADD = [0] * H
        S_mbf = [0] * H
        T_SCONS = [0] * H
        T_proj = [0] * NSC
        T_wo = [0] * NSC
        V_wo_evac = [0] * NSC
        V_PSUM23_FREE = 0
        T_attn_done = 0
        T_MID_BOX = [0]
        AOF_EVAC = [0] * 4
        WO_EVAC_V = [0] * (NSC * 8)

        PREV_SEVAC = [0] * 4
        V_RECV = [0] * H
        for sc in range(NSC):
            tc0 = sc * 4
            tiles = [tc0 + i for i in range(4)]
            last_sc = (sc == NSC - 1)
            if sc == 0:
                ln_stats(tiles, [("dIN", IN_X)])
                ln_ts(tiles)
            ln_tr(tiles, h_f, [0, 1, 2, 3], ev="mix")
            S_HF_W = [HF_EVAC[tc0 + 2], HF_EVAC[tc0 + 3]]
            S_hf = c["sS"]
            if sc == 0:
                tap("d_hf", h_f[:, :, :], [("sS", S_hf)])

            gidx = [0]
            PROJ_EVAC_S = [0] * 40
            pieces = {}
            pi = 6 if sc < NSC - 1 else 4
            step = 4 if sc < NSC - 1 else 3
            if sc + 1 < NSC:
                for p in ln_stats_pieces([tc0 + 4 + i for i in range(4)],
                                         [("dIN", IN_47 if sc == 0 else IN_ALL)]):
                    pieces[pi] = p
                    pi += step
            if sc > 0:
                for p in ln2v_pieces([tc0 - 4 + i for i in range(4)]):
                    pieces[pi] = p
                    pi += step

            def proj_group(mms, evac_fn, hf_wait=None, dw=None):
                g = gidx[0]
                if g in pieces:
                    pieces.pop(g)()
                if sc > 0 and g < 2:
                    bank = pb[2 + g]
                else:
                    bank = pb[g % 2]
                w = (S_HF_W if hf_wait is None else [hf_wait]) + [
                     ("dW", DW_W["q"] if dw is None else dw), ("dIN", IN_CONST)]
                if sc > 0 and g < 2:
                    w.append(("vS", PREV_SEVAC[2 + g]))
                elif sc > 0 and g < 4:
                    w.append(("vS", V_wo_evac[sc - 1]))
                elif g >= 2:
                    w.append(("sS", PROJ_EVAC_S[g - 2]))
                n = len(mms)
                for i, (lf, rf) in enumerate(mms):
                    op("tensor", lambda e, lf=lf, rf=rf, i=i, n=n, bank=bank: e.matmul(
                        bank[:, 0:512], lf(), rf(),
                        start=(i == 0), stop=(i == n - 1)),
                        wait=(w if i == 0 else []),
                        inc=([("tS", 1)] if i == n - 1 else []))
                evac_fn(bank, c["tS"], g)
                PROJ_EVAC_S[g] = c["sS"]
                gidx[0] += 1

            def ev_act(dest_ap, bias_ap):
                def f(bank, td, g):
                    op("scalar", lambda e: e.activation(dest_ap, bank[:, 0:512],
                                                        AF.Exp, bias=bias_ap),
                       wait=[("tS", td)], inc=[("sS", 1)])
                return f

            def ev_exp(dest_ap):
                def f(bank, td, g):
                    op("scalar", lambda e: e.activation(dest_ap, bank[:, 0:512], AF.Exp),
                       wait=[("tS", td)], inc=[("sS", 1)])
                return f

            def ev_v(tcm, hh):
                def f(bank, td, g):
                    dst = v_t[:, tcm].rearrange("p (h c) -> p h c", c=65)[
                        :, hh * 8:(hh + 1) * 8, 0:64]
                    src = bank[:, 0:512].rearrange("p (h c) -> p h c", c=64)
                    op("scalar", lambda e: e.copy(dst, src),
                       wait=[("tS", td), ("gS", MEMSET)], inc=[("sS", 1)])
                return f

            wsel = [(wv_bf, bvr_s, "v")] if last_sc else \
                   [(wv_bf, bvr_s, "v"), (wk_bf, bkr_s, "k")]
            for tcm in range(4):
                for wbf, brow, evf in wsel:
                    for hh in range(2):
                        mms = [(lambda: onesr_s[:, :],
                                lambda hh=hh, brow=brow: brow[:, hh * 512:(hh + 1) * 512])]
                        mms += [(lambda dc=dc, tcm=tcm: h_f[:, dc, tcm * 128:(tcm + 1) * 128],
                                 lambda dc=dc, hh=hh, wbf=wbf: wbf[:, dc, hh * 512:(hh + 1) * 512])
                                for dc in range(8)]
                        if evf == "k":
                            proj_group(mms, ev_exp(k_t[:, tcm, hh * 512:(hh + 1) * 512]),
                                       hf_wait=HF_EVAC[tc0 + tcm], dw=DW_W["k"])
                        else:
                            proj_group(mms, ev_v(tcm, hh),
                                       hf_wait=HF_EVAC[tc0 + tcm], dw=DW_W["v"])
            for wbf, dest, bcol in [(wq_bf, q_f, bqc_s), (wk_bf, k_f, bkc_s)]:
                for oc in range(8):
                    mms = [(lambda dc=dc, wbf=wbf, oc=oc: wbf[:, dc, oc * 128:(oc + 1) * 128],
                            lambda dc=dc: h_f[:, dc, :]) for dc in range(8)]
                    proj_group(mms, ev_act(dest[:, oc, :], bcol[:, oc:oc + 1]))
            for g in sorted(pieces):
                pieces.pop(g)()
            T_proj[sc] = c["tS"]
            PROJ_LAST_S = c["sS"]
            if sc == 0:
                tap("d_qf", q_f[:, :, :], [("sS", PROJ_LAST_S)])
                tap("d_kf", k_f[:, :, :], [("sS", PROJ_LAST_S)])
                tap("d_kt", k_t[:, :, :], [("sS", PROJ_LAST_S)])
                tap("d_vt", v_t[:, :, :], [("sS", PROJ_LAST_S)])


            V_sevac = [[0] * 4 for _ in range(H)]
            T_SCONS_NEW = [0] * H

            def emit_scores(h):
                p64 = 64 * (h % 2)
                hc = h // 2
                for j in range(4):
                    nj = 512 - j * 128
                    sbk = pb[j]
                    w = [("sS", PROJ_LAST_S)]
                    if h >= 1:
                        w.append(("vS", V_sevac[h - 1][j]))
                    elif sc > 0:
                        w.append(("vS", V_wo_evac[sc - 1] if j < 2 else PREV_SEVAC[j]))
                    op("tensor", lambda e, j=j, nj=nj, sbk=sbk, p64=p64, hc=hc: e.matmul(
                        sbk[:, 0:nj],
                        k_f[p64:p64 + 64, hc, j * 128:(j + 1) * 128],
                        q_f[p64:p64 + 64, hc, j * 128:512],
                        start=True, stop=True), wait=w, inc=[("tS", 1)])
                    t_sj = c["tS"]
                    wv = [("tS", t_sj)]
                    if h >= 2:
                        wv.append(("tS", T_SCONS_NEW[h - 2]))
                    elif sc > 0:
                        wv.append(("tS", T_SCONS[H - 2 + h]))
                    op("vector", lambda e, j=j, nj=nj, sbk=sbk, h=h: e.tensor_tensor(
                        s_sb[:, h % 2, SOFF[j]:SOFF[j] + nj],
                        sbk[:, 0:nj], mask_s[:, 0:nj], ALU.mult),
                        wait=wv, inc=[("vS", 1)])
                    V_sevac[h][j] = c["vS"]

            emit_scores(0)
            for h in range(H):
                p64 = 64 * (h % 2)
                hc = h // 2
                nb = pb[4 + (h % 2)]
                if h + 1 < H:
                    emit_scores(h + 1)
                first_w = [("sS", S_AO[h - 2] if h >= 2 else S_AO[H - 2 + h]),
                           ("vS", V_RECV[h - 2] if h >= 2 else V_RECV[H - 2 + h])]
                nmm = 0
                if sc > 0:
                    for tcx in range(4):
                        w = (first_w + [("sS", S_mbf[h])]) if tcx == 0 else []
                        op("tensor", lambda e, tcx=tcx, nb=nb, p64=p64, hc=hc: e.matmul(
                            nb[:, tcx * 65:tcx * 65 + 65],
                            q_f[p64:p64 + 64, hc, tcx * 128:(tcx + 1) * 128],
                            M_bf[p64:p64 + 64, hc, :],
                            start=(tcx == 0), stop=False), wait=w)
                        nmm += 1
                intra_list = [(j, tcx) for j in range(4) for tcx in range(j, 4)]
                for j, tcx in intra_list[:-1]:
                    if tcx == j:
                        w = [("vS", V_sevac[h][j])]
                    else:
                        w = []
                    if nmm == 0:
                        w = first_w + w
                    op("tensor", lambda e, j=j, tcx=tcx, stf=(nmm == 0), nb=nb, h=h: e.matmul(
                        nb[:, tcx * 65:tcx * 65 + 65],
                        s_sb[:, h % 2,
                             SOFF[j] + (tcx - j) * 128:SOFF[j] + (tcx - j) * 128 + 128],
                        v_head(j, h),
                        start=stf, stop=False),
                        wait=w)
                    nmm += 1
                if not last_sc:
                    for j in range(4):
                        op("tensor", lambda e, j=j, nb=nb, p64=p64, h=h: e.matmul(
                            nb[p64:p64 + 64, 260:325],
                            k_t[:, j, h * 64:(h + 1) * 64],
                            v_head(j, h),
                            start=False, stop=False))
                op("tensor", lambda e, nb=nb, h=h: e.matmul(
                    nb[:, 3 * 65:3 * 65 + 65],
                    s_sb[:, h % 2, SOFF[3]:SOFF[3] + 128],
                    v_head(3, h),
                    start=False, stop=True),
                    wait=[("vS", V_sevac[h][3])], inc=[("tS", 1)])
                T_SCONS_NEW[h] = c["tS"]
                t_upd = c["tS"]
                op("vector", lambda e, nb=nb, h=h: e.reciprocal(
                    recip[:, h % 2, :],
                    nb[:, 0:260].rearrange("p (t c) -> p t c", c=65)[:, :, 64:65]
                      .rearrange("p t o -> p (t o)")),
                    wait=[("tS", t_upd),
                          ("sS", S_AO[h - 2] if h >= 2 else S_AO[H - 2 + h])],
                    inc=[("vS", 1)])
                v_recip = c["vS"]
                if not last_sc:
                    op("vector", lambda e, nb=nb, p64=p64, hc=hc: e.tensor_tensor(
                        M32[p64:p64 + 64, hc, :], M32[p64:p64 + 64, hc, :],
                        nb[p64:p64 + 64, 260:325], ALU.add),
                        inc=[("vS", 1)])
                    V_M32ADD[h] = c["vS"]
                V_RECV[h] = c["vS"]
                for tcx in range(4):
                    op("scalar", lambda e, tcx=tcx, nb=nb, h=h: e.activation(
                        ao_t[:, tcx, h * 64:(h + 1) * 64],
                        nb[:, tcx * 65:tcx * 65 + 64],
                        AF.Copy, scale=recip[:, h % 2, tcx:tcx + 1]),
                        wait=([("vS", v_recip)] if tcx == 0 else []),
                        inc=([("sS", 1)] if tcx == 3 else []))
                S_AO[h] = c["sS"]
                if not last_sc:
                    op("scalar", lambda e, p64=p64, hc=hc: e.copy(
                        M_bf[p64:p64 + 64, hc, :], M32[p64:p64 + 64, hc, :]),
                        wait=[("vS", V_M32ADD[h])], inc=[("sS", 1)])
                    S_mbf[h] = c["sS"]
                if sc == NSC - 1 and h == 7:
                    T_MID_BOX[0] = c["tS"]
            PREV_SEVAC = list(V_sevac[H - 1])
            T_SCONS = [0] * (H - 2) + [T_SCONS_NEW[H - 2], T_SCONS_NEW[H - 1]]
            S_AO_DONE = c["sS"]
            T_attn_done = c["tS"]
            if sc + 1 < NSC:
                ln_ts([tc0 + 4 + i for i in range(4)])
            if sc == 0:
                tap("d_aot", ao_t[:, :, :], [("sS", S_AO_DONE)])
                tap("d_m32", M32[:, :, :], [("vS", V_RECV[H - 1])])

            def a4_trs(tcm):
                trb = tr[tcm % 2]
                for cc in range(8):
                    w = [("sS", S_AO_DONE), TR_FREE[tcm % 2]] if cc == 0 else []
                    op("tensor", lambda e, cc=cc, tcm=tcm, trb=trb: e.transpose(
                        trb[:, cc * 128:(cc + 1) * 128],
                        ao_t[:, tcm, cc * 128:(cc + 1) * 128], idn_s[:, :]),
                        wait=w, inc=([("tS", 1)] if cc == 7 else []))
                t_tra = c["tS"]
                for cc in range(8):
                    op("scalar", lambda e, cc=cc, trb=trb,
                       dst=(ao_f if tcm % 2 == 0 else ao_f2): e.copy(
                        dst[:, cc, :], trb[:, cc * 128:(cc + 1) * 128]),
                        wait=([("tS", t_tra)] if cc == 0 else []),
                        inc=([("sS", 1)] if cc == 7 else []))
                AOF_EVAC[tcm] = c["sS"]
                TR_FREE[tcm % 2] = ("sS", AOF_EVAC[tcm])

            def a4_wo(tcm):
                for hh in range(2):
                    bank = pb[hh]
                    gix = sc * 8 + tcm * 2 + hh
                    w0 = [("sS", AOF_EVAC[tcm])]
                    if sc == 0 and gix == 0:
                        w0.append(("dW", DW_W["o"]))
                    if gix >= 2:
                        w0.append(("vS", WO_EVAC_V[gix - 2]))
                    else:
                        w0.append(("vS", V_sevac[H - 1][hh]))
                    op("tensor", lambda e, hh=hh, bank=bank: e.matmul(
                        bank[:, 0:512], onesr_s[:, :],
                        bor_s[:, hh * 512:(hh + 1) * 512],
                        start=True, stop=False), wait=w0)
                    for dc in range(8):
                        op("tensor", lambda e, dc=dc, hh=hh, bank=bank,
                           src=(ao_f if tcm % 2 == 0 else ao_f2): e.matmul(
                            bank[:, 0:512], src[:, dc, :],
                            wo_bf[:, dc, hh * 512:(hh + 1) * 512],
                            start=False, stop=(dc == 7)),
                            inc=([("tS", 1)] if dc == 7 else []))
                    t_wo = c["tS"]
                    op("vector", lambda e, tcm=tcm, hh=hh, bank=bank: e.tensor_tensor(
                        x_t[:, tc0 + tcm, hh * 512:(hh + 1) * 512],
                        x_t[:, tc0 + tcm, hh * 512:(hh + 1) * 512],
                        bank[:, 0:512], ALU.add),
                        wait=[("tS", t_wo)], inc=[("vS", 1)])
                    WO_EVAC_V[gix] = c["vS"]

            a4_trs(0)
            a4_trs(1)
            a4_wo(0)
            a4_trs(2)
            a4_wo(1)
            a4_trs(3)
            a4_wo(2)
            a4_wo(3)
            T_wo[sc] = c["tS"]
            V_wo_evac[sc] = c["vS"]
            if sc == NSC - 1:
                tap("d_xa", x_t[:, :, :], [("vS", V_wo_evac[sc])])


        op("gpsimd", lambda e: e.dma_start(
            gw_bf[:, :, :], gwT_d.ap().rearrange("(c p) e -> p c e", p=128)),
            wait=[("tS", T_proj[NSC - 1])], inc=[("dW", 16)])
        DW_GW = c["dW"]
        DW_GUP = [0] * E
        DW_DWN = [0] * E
        op("gpsimd", lambda e: e.dma_start(
            gup_bf[:, 4:6, :],
            gup_d[0].rearrange("(c p) q -> p c q", p=128)[:, 4:6, :]),
            wait=[("tS", T_MID_BOX[0])], inc=[("dW", 16)])
        op("gpsimd", lambda e: e.dma_start(
            gup_bf[:, 0:4, :],
            gup_d[0].rearrange("(c p) q -> p c q", p=128)[:, 0:4, :]),
            wait=[("tS", T_attn_done)], inc=[("dW", 16)])
        op("gpsimd", lambda e: e.dma_start(
            gup_bf[:, 6:8, :],
            gup_d[0].rearrange("(c p) q -> p c q", p=128)[:, 6:8, :]),
            inc=[("dW", 16)])
        DW_GUP[0] = c["dW"]


        T_tr = [0] * NT
        HF_EVAC = [0] * NT
        S_exp = [0] * 4
        V_z = [0] * (E * 32)
        V_de = [0] * (E * 32)
        V_out_h = [0] * (NT * 2)
        zi_box = [0]
        DW_GUP_CUR = [DW_GUP[0]]
        DW_GUP_CUR_A = [DW_GUP[0]]
        T_GATE_QP3 = [0]

        def emit_gate(e_i, tq, pieces=None):
            for qp in range(8):
                idx = tq * 8 + qp
                zi = zi_box[0]
                bp = (pb[2], pb[3]) if idx % 2 == 0 else (pb[0], pb[1])
                w = [("dW", DW_GUP_CUR_A[0] if qp < 4 else DW_GUP_CUR[0])]
                if e_i == 0:
                    w += [HF_EVAC[tq * 4 + 2], HF_EVAC[tq * 4 + 3]]
                else:
                    w += [("vS", H2F_V_BOX[0]), ("sS", H2F_S_BOX[0])]
                if zi >= 2:
                    w.append(("vS", V_z[zi - 2]))
                if e_i == 0 and idx == 0:
                    w.append(("vS", PREV_SEVAC[3]))
                if e_i == 0 and idx == 1:
                    w.append(("vS", V_wo_evac[NSC - 1]))
                for half, bank in enumerate(bp):
                    qb = qp * 128 + half * 1024
                    for dc in range(8):
                        op("tensor", lambda e, dc=dc, qb=qb, tq=tq, bank=bank: e.matmul(
                            bank[:, 0:512],
                            gup_bf[:, dc, qb:qb + 128],
                            h2_f[:, dc, tq * 512:(tq + 1) * 512],
                            start=(dc == 0), stop=(dc == 7)),
                            wait=(w if (half == 0 and dc == 0) else []),
                            inc=([("tS", 1)] if (half == 1 and dc == 7) else []))
                t_gu = c["tS"]
                if qp == 3:
                    T_GATE_QP3[0] = t_gu
                ws = [("tS", t_gu)]
                if zi >= 2:
                    ws.append(("vS", V_z[zi - 2]))
                op("scalar", lambda e, bp=bp, idx=idx: e.activation(
                    silu_sb[:, idx % 2, :], bp[0][:, 0:512], AF.Silu),
                    wait=ws, inc=[("sS", 1)])
                s_si = c["sS"]
                if pieces and qp in pieces:
                    pieces.pop(qp)()
                op("vector", lambda e, bp=bp, idx=idx, qp=qp, tq=tq: e.tensor_tensor(
                    z_sb[:, qp, tq * 512:(tq + 1) * 512],
                    silu_sb[:, idx % 2, :], bp[1][:, 0:512], ALU.mult),
                    wait=[("sS", s_si)], inc=[("vS", 1)])
                V_z[zi] = c["vS"]
                zi_box[0] = zi + 1

        H2F_DONE_BOX = [0]
        H2F_V_BOX = [0]
        H2F_S_BOX = [0]
        ln_ts([0, 1, 2, 3])
        for g4 in range(4):
            tiles = [g4 * 4 + i for i in range(4)]
            ln_tr(tiles, h2_f, tiles, ev="vector")
            rb = pb[4 + (g4 % 2)]
            w = [HF_EVAC[tiles[2]], HF_EVAC[tiles[3]], ("dW", DW_GW)]
            if g4 >= 2:
                w.append(("sS", S_exp[g4 - 2]))
            else:
                w.append(("sS", S_AO[H - 2 + g4]))
                w.append(("vS", V_RECV[H - 2 + g4]))
            nmm = 0
            for i, tc in enumerate(tiles):
                for dc in range(8):
                    op("tensor", lambda e, dc=dc, tc=tc, rb=rb, i=i, nmm=nmm: e.matmul(
                        rb[:, i * 8:(i + 1) * 8], h2_f[:, dc, tc * 128:(tc + 1) * 128],
                        gw_bf[:, dc, :], start=(nmm == 0), stop=(nmm == 31)),
                        wait=(w if nmm == 0 else []),
                        inc=([("tS", 1)] if nmm == 31 else []))
                    nmm += 1
            t_rout = c["tS"]
            for i in range(4):
                op("vector", lambda e, i=i, rb=rb, rs=rstat[:, g4 * 4 + i]: e.reduce_max(
                    rs[:, 0:1], rb[:, i * 8:(i + 1) * 8], AX.X, negate=True),
                    wait=([("tS", t_rout)] if i == 0 else []),
                    inc=([("vS", 1)] if i == 3 else []))
            v_nmx = c["vS"]
            for i in range(4):
                op("scalar", lambda e, i=i, rb=rb, rs=rstat[:, g4 * 4 + i]: e.activation(
                    rt_e[:, i, :], rb[:, i * 8:(i + 1) * 8], AF.Exp, bias=rs[:, 0:1]),
                    wait=([("vS", v_nmx)] if i == 0 else []),
                    inc=([("sS", 1)] if i == 3 else []))
            S_exp[g4] = c["sS"]

            def R(fn, wait_first=None, inc_last=False):
                for i in range(4):
                    op("vector", lambda e, i=i, rs=rstat[:, g4 * 4 + i],
                       ee=rt_e[:, i], mk=msk1[:, i], tp=tmp8[:, i]: fn(e, i, rs, ee, mk, tp),
                       wait=(wait_first if (i == 0 and wait_first) else []),
                       inc=([("vS", 1)] if (i == 3 and inc_last) else []))

            R(lambda e, i, rs, ee, mk, tp: e.reduce_sum(rs[:, 1:2], ee, AX.X),
              wait_first=[("sS", S_exp[g4])])
            R(lambda e, i, rs, ee, mk, tp: e.reciprocal(rs[:, 2:3], rs[:, 1:2]))
            R(lambda e, i, rs, ee, mk, tp: e.tensor_scalar(
                ee, ee, rs[:, 2:3], None, ALU.mult))
            R(lambda e, i, rs, ee, mk, tp: e.reduce_max(rs[:, 3:4], ee, AX.X))
            R(lambda e, i, rs, ee, mk, tp: e.tensor_scalar(
                mk, ee, rs[:, 3:4], None, ALU.is_equal))
            R(lambda e, i, rs, ee, mk, tp: e.tensor_mul(tp, ee, mk))
            R(lambda e, i, rs, ee, mk, tp: e.tensor_sub(tp, ee, tp))
            R(lambda e, i, rs, ee, mk, tp: e.reduce_max(rs[:, 4:5], tp, AX.X))
            R(lambda e, i, rs, ee, mk, tp: e.tensor_scalar(
                tp, tp, rs[:, 4:5], None, ALU.is_equal))
            R(lambda e, i, rs, ee, mk, tp: e.tensor_add(mk, mk, tp))
            R(lambda e, i, rs, ee, mk, tp: e.scalar_tensor_tensor(
                rs[:, 5:6], rs[:, 3:4], 1e-9, rs[:, 4:5], ALU.add, ALU.add))
            R(lambda e, i, rs, ee, mk, tp: e.reciprocal(rs[:, 6:7], rs[:, 5:6]))
            R(lambda e, i, rs, ee, mk, tp: e.tensor_mul(ee, ee, mk))
            R(lambda e, i, rs, ee, mk, tp, g4=g4: e.tensor_scalar(
                routed[:, g4 * 4 + i, :], ee, rs[:, 6:7], None, ALU.mult),
              inc_last=True)
            if g4 + 1 < 4:
                ln_ts([g4 * 4 + 4 + i for i in range(4)])
            emit_gate(0, g4, pieces=(
                ln2_s_pieces([(NSC - 1) * 4 + i for i in range(4)])
                if g4 == 0 else None))
            if g4 == 0:
                op("gpsimd", lambda e: e.dma_start(
                    dwn_bf[:, :, :], dwn_d[0].rearrange("(c p) q -> p c q", p=128)),
                    wait=[("tS", T_wo[NSC - 1]), ("sS", LN2_JUNK[0])],
                    inc=[("dW", 16)])
                DW_DWN[0] = c["dW"]
        H2F_DONE = c["sS"]
        H2F_V = c["vS"]
        H2F_DONE_BOX[0] = H2F_DONE
        H2F_V_BOX[0] = H2F_V
        H2F_S_BOX[0] = H2F_DONE
        V_ROUTED_ALL = c["vS"]
        tap("d_h2f", h2_f[:, :, :], [("sS", H2F_DONE)])
        tap("d_routed", routed[:, :, :], [("vS", V_ROUTED_ALL)])

        di = 0
        for e_i in range(E):
            if e_i > 0:
                for tq in range(4):
                    emit_gate(e_i, tq)
            T_gu_last = c["tS"]
            V_z_all = c["vS"]
            if e_i == 0:
                tap("d_z0", z_sb[:, :, :], [("vS", V_z_all)])
            if e_i + 1 < E:
                t_qp3 = T_GATE_QP3[0]
                for j, (lo, hi) in enumerate([(0, 512), (1024, 1536),
                                              (512, 1024), (1536, 2048)]):
                    op("gpsimd", lambda e, e_i=e_i, lo=lo, hi=hi: e.dma_start(
                        gup_bf[:, :, lo:hi],
                        gup_d[e_i + 1].rearrange(
                            "(c p) q -> p c q", p=128)[:, :, lo:hi]),
                        wait=([("tS", t_qp3)] if j == 0 else
                              ([("tS", T_gu_last)] if j == 2 else [])),
                        inc=[("dW", 16)])
                    if j == 1:
                        DW_GUP_A = c["dW"]
                DW_GUP[e_i + 1] = c["dW"]
                DW_GUP_CUR[0] = c["dW"]
                DW_GUP_CUR_A[0] = DW_GUP_A
            for tcm in range(NT):
                for hh in range(2):
                    idx = tcm * 2 + hh
                    bank = pb[4 + (idx % 2)]
                    w = [("dW", DW_DWN[e_i]),
                         ("vS", V_z[e_i * 32 + (tcm // 4) * 8 + 7])]
                    if di >= 2:
                        w.append(("vS", V_de[di - 2]))
                    if e_i == 0:
                        w.append(("vS", V_ROUTED_ALL))
                        if idx < 2:
                            w.append(("sS", S_exp[2 + idx]))
                    for qc in range(8):
                        op("tensor", lambda e, qc=qc, tcm=tcm, hh=hh, bank=bank: e.matmul(
                            bank[:, 0:512],
                            z_sb[:, qc, tcm * 128:(tcm + 1) * 128],
                            dwn_bf[:, qc, hh * 512:(hh + 1) * 512],
                            start=(qc == 0), stop=(qc == 7)),
                            wait=(w if qc == 0 else []),
                            inc=([("tS", 1)] if qc == 7 else []))
                    t_d = c["tS"]
                    op("vector",
                       lambda e, tcm=tcm, hh=hh, bank=bank, e_i=e_i: e.scalar_tensor_tensor(
                           x_t[:, tcm, hh * 512:(hh + 1) * 512],
                           bank[:, 0:512], routed[:, tcm, e_i:e_i + 1],
                           x_t[:, tcm, hh * 512:(hh + 1) * 512],
                           ALU.mult, ALU.add),
                       wait=[("tS", t_d)], inc=[("vS", 1)])
                    V_de[di] = c["vS"]
                    di += 1
                    if e_i == E - 1:
                        V_out_h[tcm * 2 + hh] = c["vS"]
            T_dwn_last = c["tS"]
            if e_i + 1 < E:
                op("gpsimd", lambda e, e_i=e_i: e.dma_start(
                    dwn_bf[:, :, :],
                    dwn_d[e_i + 1].rearrange("(c p) q -> p c q", p=128)),
                    wait=[("tS", T_dwn_last)], inc=[("dW", 16)])
                DW_DWN[e_i + 1] = c["dW"]

        for tcm in range(NT):
            for hh in range(2):
                op("sync" if hh == 0 else "scalar",
                   lambda e, tcm=tcm, hh=hh: e.dma_start(
                    y_d[tcm * 128:(tcm + 1) * 128, hh * 512:(hh + 1) * 512],
                    x_t[:, tcm, hh * 512:(hh + 1) * 512]),
                   wait=[("vS", V_out_h[tcm * 2 + hh])], inc=[("dOUT", 16)])
        if which == "sync":
            eng.wait_ge(sems["dOUT"], c["dOUT"])

    with nc.Block() as block:
        @block.sync
        def _(e):
            walk("sync", e)

        @block.gpsimd
        def _(e):
            walk("gpsimd", e)

        @block.tensor
        def _(e):
            walk("tensor", e)

        @block.vector
        def _(e):
            walk("vector", e)

        @block.scalar
        def _(e):
            walk("scalar", e)

    return nc


_NC_CACHE = [None]


def kernel(**inputs):
    x = np.ascontiguousarray(np.asarray(inputs["L0_attn_x"], dtype=np.float32))
    wqT = np.ascontiguousarray(np.asarray(inputs["L0_attn_Wq"], dtype=np.float32).T)
    wkT = np.ascontiguousarray(np.asarray(inputs["L0_attn_Wk"], dtype=np.float32).T)
    wvT = np.ascontiguousarray(np.asarray(inputs["L0_attn_Wv"], dtype=np.float32).T)
    woT = np.ascontiguousarray(np.asarray(inputs["L0_attn_Wo"], dtype=np.float32).T)
    bq = np.asarray(inputs["L0_attn_bq"], dtype=np.float32)
    bk = np.asarray(inputs["L0_attn_bk"], dtype=np.float32)
    bv = np.asarray(inputs["L0_attn_bv"], dtype=np.float32)
    bo = np.asarray(inputs["L0_attn_bo"], dtype=np.float32)
    gw = np.asarray(inputs["L0_ffn_gate_w"], dtype=np.float32)
    gup = np.ascontiguousarray(np.asarray(inputs["L0_ffn_gate_up_proj"], dtype=np.float32))
    dwn = np.ascontiguousarray(np.asarray(inputs["L0_ffn_down_proj"], dtype=np.float32))

    common = dict(
        wqT=wqT, wkT=wkT, wvT=wvT, woT=woT,
        bq_col=np.ascontiguousarray(bq.reshape(8, 128).T),
        bk_col=np.ascontiguousarray(bk.reshape(8, 128).T),
        bk_row=np.ascontiguousarray(bk.reshape(1, D)),
        bv_row=np.ascontiguousarray(bv.reshape(1, D)),
        bo_row=np.ascontiguousarray(bo.reshape(1, D)),
        gwT=np.ascontiguousarray(gw.T), gup=gup, dwn=dwn,
        mask=np.concatenate([np.triu(np.ones((128, 128), np.float32)),
                             np.ones((128, 384), np.float32)], axis=1),
        idn=np.eye(128, dtype=np.float32).astype(ml_dtypes.bfloat16),
        ones_row=np.ones((1, 128), np.float32).astype(ml_dtypes.bfloat16),
        epsc=np.full((128, 1), 1e-5, np.float32),
    )
    if _NC_CACHE[0] is None:
        _NC_CACHE[0] = build_nc()
    nc = _NC_CACHE[0]
    in_maps = [dict(x=np.ascontiguousarray(x[b]), **common) for b in range(B)]
    res = run_bass_kernel_spmd(nc, in_maps, core_ids=list(range(B)))
    return np.stack([r["y"] for r in res.results], axis=0)


# version 44
# speedup vs baseline: 1.0022x; 1.0022x over previous
import numpy as np
import ml_dtypes
import concourse.bass as bass
import concourse.mybir as mybir
from concourse.bass_utils import run_bass_kernel_spmd

F32 = mybir.dt.float32
BF16 = mybir.dt.bfloat16
U8 = mybir.dt.uint8
AF = mybir.ActivationFunctionType
ALU = mybir.AluOpType
AX = mybir.AxisListType

B = 8
S = 2048
D = 1024
NT = 16
NSC = 4
H = 16
E = 8
SOFF = [0, 512, 896, 1152]


def build_nc(dbg=False):
    nc = bass.Bass("TRN2", target_bir_lowering=False,
                   detect_race_conditions=False, num_swdge_queues=1)

    x_d = nc.dram_tensor("x", [S, D], F32, kind="ExternalInput")
    wqT_d = nc.dram_tensor("wqT", [D, D], F32, kind="ExternalInput")
    wkT_d = nc.dram_tensor("wkT", [D, D], F32, kind="ExternalInput")
    wvT_d = nc.dram_tensor("wvT", [D, D], F32, kind="ExternalInput")
    woT_d = nc.dram_tensor("woT", [D, D], F32, kind="ExternalInput")
    bqc_d = nc.dram_tensor("bq_col", [128, 8], F32, kind="ExternalInput")
    bkc_d = nc.dram_tensor("bk_col", [128, 8], F32, kind="ExternalInput")
    bkr_d = nc.dram_tensor("bk_row", [1, D], F32, kind="ExternalInput")
    bvr_d = nc.dram_tensor("bv_row", [1, D], F32, kind="ExternalInput")
    bor_d = nc.dram_tensor("bo_row", [1, D], F32, kind="ExternalInput")
    gwT_d = nc.dram_tensor("gwT", [D, E], F32, kind="ExternalInput")
    gup_d = nc.dram_tensor("gup", [E, D, 2 * D], F32, kind="ExternalInput")
    dwn_d = nc.dram_tensor("dwn", [E, D, D], F32, kind="ExternalInput")
    mask_d = nc.dram_tensor("mask", [128, 512], F32, kind="ExternalInput")
    idn_d = nc.dram_tensor("idn", [128, 128], BF16, kind="ExternalInput")
    onesr_d = nc.dram_tensor("ones_row", [1, 128], BF16, kind="ExternalInput")
    epsc_d = nc.dram_tensor("epsc", [128, 1], F32, kind="ExternalInput")
    y_d = nc.dram_tensor("y", [S, D], F32, kind="ExternalOutput")

    x_t = nc.alloc_sbuf_tensor("x_t", [128, NT, D], F32)
    h_t = nc.alloc_sbuf_tensor("h_t", [128, 4, D], BF16)
    stat = nc.alloc_sbuf_tensor("stat", [128, NT, 12], F32)
    rstat = nc.alloc_sbuf_tensor("rstat", [128, 16, 8], F32)
    recip = nc.alloc_sbuf_tensor("recip", [128, 2, 4], F32)
    rt_e = nc.alloc_sbuf_tensor("rt_e", [128, 4, 8], F32)
    msk1 = nc.alloc_sbuf_tensor("msk1", [128, 4, 8], F32)
    tmp8 = nc.alloc_sbuf_tensor("tmp8", [128, 4, 8], F32)
    silu_sb = nc.alloc_sbuf_tensor("silu_sb", [128, 2, 512], BF16)
    mask_s = nc.alloc_sbuf_tensor("mask_s", [128, 512], F32)
    idn_s = nc.alloc_sbuf_tensor("idn_s", [128, 128], BF16)
    onesr_s = nc.alloc_sbuf_tensor("onesr_s", [1, 128], BF16)
    bqc_s = nc.alloc_sbuf_tensor("bqc_s", [128, 8], F32)
    bkc_s = nc.alloc_sbuf_tensor("bkc_s", [128, 8], F32)
    bkr_s = nc.alloc_sbuf_tensor("bkr_s", [1, D], BF16)
    bvr_s = nc.alloc_sbuf_tensor("bvr_s", [1, D], BF16)
    bor_s = nc.alloc_sbuf_tensor("bor_s", [1, D], BF16)
    epsc_s = nc.alloc_sbuf_tensor("epsc_s", [128, 1], F32)
    routed = nc.alloc_sbuf_tensor("routed", [128, NT, E], F32)
    gw_bf = nc.alloc_sbuf_tensor("gw_bf", [128, 8, E], BF16)

    def al32(v):
        return (v + 31) & ~31

    a_specs = [
        ("q_f", [128, 8, 512], BF16), ("k_f", [128, 8, 512], BF16),
        ("k_t", [128, 4, D], BF16), ("v_t", [128, 4, H * 65], BF16),
        ("s_sb", [128, 2, 1280], BF16), ("ao_t", [128, 4, D], BF16),
        ("ao_f", [128, 8, 128], BF16), ("M32", [128, 8, 65], F32),
        ("M_bf", [128, 8, 65], BF16), ("h_f", [128, 8, 512], BF16),
        ("wq_bf", [128, 8, D], BF16), ("wk_bf", [128, 8, D], BF16),
        ("wv_bf", [128, 8, D], BF16), ("wo_bf", [128, 8, D], BF16),
    ]
    offs, cur = {}, 0
    for name, shp, dt in a_specs:
        offs[name] = cur
        cur = al32(cur + int(np.prod(shp[1:])) * mybir.dt.size(dt))
    pool_bytes = cur
    b_specs = [
        ("gup_bf", [128, 8, 2 * D], BF16, offs["q_f"]),
        ("dwn_bf", [128, 8, D], BF16, offs["s_sb"]),
        ("z_sb", [128, 8, 2 * D], BF16, offs["wq_bf"]),
        ("h2_f", [128, 8, S], BF16, offs["wv_bf"]),
    ]
    assert offs["s_sb"] - offs["q_f"] >= 8 * 2 * D * 2
    assert offs["h_f"] - offs["s_sb"] >= 8 * D * 2
    assert offs["wv_bf"] - offs["wq_bf"] >= 8 * 2 * D * 2
    assert pool_bytes - offs["wv_bf"] >= 8 * S * 2
    pool = nc.alloc_sbuf_tensor("pool", [128, pool_bytes], U8)
    base = nc.lookup_mloc(pool).addr
    TT = {}
    for name, shp, dt in a_specs:
        TT[name] = nc.alloc_sbuf_tensor_at(name, shp, dt, offset=base + offs[name])
    for name, shp, dt, off in b_specs:
        TT[name] = nc.alloc_sbuf_tensor_at(name, shp, dt, offset=base + off)
    TT["ao_f2"] = nc.alloc_sbuf_tensor_at("ao_f2", [128, 8, 128], BF16,
                                          offset=base + offs["h_f"])
    q_f, k_f, k_t, v_t = TT["q_f"], TT["k_f"], TT["k_t"], TT["v_t"]
    s_sb, ao_t, ao_f = TT["s_sb"], TT["ao_t"], TT["ao_f"]
    ao_f2 = TT["ao_f2"]
    M32, M_bf, h_f = TT["M32"], TT["M_bf"], TT["h_f"]
    wq_bf, wk_bf, wv_bf, wo_bf = TT["wq_bf"], TT["wk_bf"], TT["wv_bf"], TT["wo_bf"]
    gup_bf, dwn_bf, z_sb, h2_f = TT["gup_bf"], TT["dwn_bf"], TT["z_sb"], TT["h2_f"]

    pb = [nc.alloc_psum_tensor(f"pb{i}", [128, 512], F32) for i in range(6)]
    tr = [nc.alloc_psum_tensor(f"tr{i}", [128, 1024], BF16) for i in range(2)]

    sems = {n: nc.alloc_semaphore(name=n)
            for n in ["dIN", "dW", "dOUT", "tS", "vS", "sS", "gS"]}

    dbg_d = {}
    if dbg:
        for name, shp, dt in [
            ("d_hf", [128, 8, 512], BF16), ("d_qf", [128, 8, 512], BF16),
            ("d_kf", [128, 8, 512], BF16), ("d_kt", [128, 4, D], BF16),
            ("d_vt", [128, 4, H * 65], BF16), ("d_aot", [128, 4, D], BF16),
            ("d_m32", [128, 8, 65], F32), ("d_xa", [128, NT, D], F32),
            ("d_h2f", [128, 8, S], BF16), ("d_routed", [128, NT, E], F32),
            ("d_z0", [128, 8, 2 * D], BF16),
        ]:
            dbg_d[name] = nc.dram_tensor(name, shp, dt, kind="ExternalOutput")

    def v_head(tc, h):
        return v_t[:, tc].rearrange("p (h c) -> p h c", c=65)[:, h, :]

    def walk(which, eng):
        c = {"dIN": 0, "dW": 0, "dOUT": 0, "tS": 0, "vS": 0, "sS": 0, "gS": 0}

        def op(engine, fn, inc=(), wait=()):
            if which == engine:
                for sn, v in wait:
                    if v > 0:
                        eng.wait_ge(sems[sn], v)
                ins = fn(eng)
                for sn, n in inc:
                    ins.then_inc(sems[sn], n)
            for sn, n in inc:
                c[sn] += n

        def tap(name, src_ap, wait):
            if not dbg:
                return
            op("sync", lambda e: e.dma_start(dbg_d[name].ap(), src_ap),
               wait=wait, inc=[("dOUT", 16)])
            tgt = c["dOUT"]
            if which in ("tensor", "vector", "scalar", "gpsimd"):
                eng.wait_ge(sems["dOUT"], tgt)

        op("vector", lambda e: e.memset(stat[:, 0, 11:12], 1.0), inc=[("vS", 1)])
        V_WARM = c["vS"]
        op("scalar", lambda e: e.activation(
            stat[:, 0, 10:11], stat[:, 0, 11:12], AF.Square),
            wait=[("vS", V_WARM)])
        op("scalar", lambda e: e.activation(
            stat[:, 0, 10:11], stat[:, 0, 11:12], AF.Ln,
            bias=stat[:, 0, 11:12]))
        op("scalar", lambda e: e.activation(
            stat[:, 0, 10:11], stat[:, 0, 11:12], AF.Exp, scale=-0.5))

        op("sync", lambda e: e.dma_start(
            x_t[:, 0:4, :],
            x_d.ap().rearrange("(c p) d -> p c d", p=128)[:, 0:4, :]),
            inc=[("dIN", 16)])
        IN_X = c["dIN"]
        for dst, src in [(mask_s[:, :], mask_d[:, :]), (idn_s[:, :], idn_d[:, :]),
                         (onesr_s[:, :], onesr_d[:, :]), (bqc_s[:, :], bqc_d[:, :]),
                         (bkc_s[:, :], bkc_d[:, :]), (epsc_s[:, :], epsc_d[:, :])]:
            op("sync", lambda e, d=dst, s=src: e.dma_start(d, s), inc=[("dIN", 16)])
        IN_CONST = c["dIN"]
        op("sync", lambda e: e.dma_start(
            x_t[:, 4:8, :],
            x_d.ap().rearrange("(c p) d -> p c d", p=128)[:, 4:8, :]),
            inc=[("dIN", 16)])
        IN_47 = c["dIN"]
        IN_ALL_BOX = [0]

        op("gpsimd", lambda e: e.memset(M32[:, :, :], 0.0), inc=[("gS", 1)])
        op("gpsimd", lambda e: e.memset(
            v_t[:, :, :].rearrange("p t (h c) -> p t h c", c=65)[:, :, :, 64:65], 1.0),
            inc=[("gS", 1)])
        MEMSET = c["gS"]
        for dst, src in [(bkr_s, bkr_d), (bvr_s, bvr_d), (bor_s, bor_d)]:
            op("gpsimd", lambda e, a=dst, b=src: e.dma_start(a[:, :], b[:, :]),
               inc=[("dW", 16)])
        DW_W = {}
        for nm, wbf, wd in [("v", wv_bf, wvT_d), ("k", wk_bf, wkT_d),
                            ("q", wq_bf, wqT_d), ("o", wo_bf, woT_d)]:
            for ch in range(2):
                op("gpsimd", lambda e, a=wbf, b=wd, ch=ch: e.dma_start(
                    a[:, ch * 4:(ch + 1) * 4, :],
                    b.ap().rearrange("(c p) n -> p c n", p=128)[
                        :, ch * 4:(ch + 1) * 4, :]),
                    wait=([("dIN", IN_X)] if (nm == "v" and ch == 0) else []),
                    inc=[("dW", 16)])
            DW_W[nm] = c["dW"]
        op("sync", lambda e: e.dma_start(
            x_t[:, 8:16, :],
            x_d.ap().rearrange("(c p) d -> p c d", p=128)[:, 8:16, :]),
            wait=[("dW", DW_W["k"])], inc=[("dIN", 16)])
        IN_ALL = c["dIN"]
        IN_ALL_BOX[0] = IN_ALL

        TR_FREE = [("sS", 0), ("sS", 0)]
        HF_EVAC_H = [0] * NT
        T_tr = [0] * NT
        HF_EVAC = [0] * NT
        ST_READY = {}
        V_HT = {}
        H_T_FREE = [0]
        LN2_JUNK = [0]
        LN2_VAR = {}
        VS1 = {}

        def ln_stats(tiles, xw):
            sts = [stat[:, tc] for tc in tiles]
            for i, tc in enumerate(tiles):
                op("vector", lambda e, st=sts[i], tc=tc: e.reduce_sum(
                    st[:, 0:1], x_t[:, tc, :], AX.X),
                    wait=(xw if i == 0 else []),
                    inc=([("vS", 1)] if i == 3 else []))
            v_s1 = c["vS"]
            for i, tc in enumerate(tiles):
                op("scalar", lambda e, st=sts[i], tc=tc: e.activation(
                    silu_sb[:, 0, :], x_t[:, tc, 0:512], AF.Square,
                    accum_out=st[:, 1:2]),
                    wait=(xw if i == 0 else []))
                op("scalar", lambda e, st=sts[i], tc=tc: e.activation(
                    silu_sb[:, 0, :], x_t[:, tc, 512:1024], AF.Square,
                    accum_out=st[:, 2:3]))
            for i in range(4):
                op("scalar", lambda e, st=sts[i]: e.mul(st[:, 4:5], st[:, 0:1], 1.0 / D),
                   wait=([("vS", v_s1)] if i == 0 else []))
            for i in range(4):
                op("scalar", lambda e, st=sts[i]: e.add(st[:, 3:4], st[:, 1:2], st[:, 2:3]))
            for i in range(4):
                op("scalar", lambda e, st=sts[i]: e.mul(st[:, 5:6], st[:, 4:5], st[:, 4:5]))
            for i in range(4):
                op("scalar", lambda e, st=sts[i]: e.mul(st[:, 6:7], st[:, 5:6], -1.0))
            for i in range(4):
                op("scalar", lambda e, st=sts[i]: e.mul(st[:, 11:12], st[:, 3:4], 1.0 / D))
            for i in range(4):
                op("scalar", lambda e, st=sts[i]: e.add(st[:, 7:8], st[:, 11:12], st[:, 6:7]))
            for i in range(4):
                op("scalar", lambda e, st=sts[i]: e.activation(
                    st[:, 8:9], st[:, 7:8], AF.Ln, bias=epsc_s[:, 0:1]))
            for i in range(4):
                op("scalar", lambda e, st=sts[i]: e.activation(
                    st[:, 9:10], st[:, 8:9], AF.Exp, scale=-0.5))
            for i in range(4):
                op("scalar", lambda e, st=sts[i]: e.mul(st[:, 10:11], st[:, 4:5], st[:, 9:10]),
                   inc=([("sS", 1)] if i == 3 else []))
            ST_READY[tiles[0]] = ("sS", c["sS"])

        def ln2_stats_v(tiles, sc):
            sts = [stat[:, tc] for tc in tiles]
            for i, tc in enumerate(tiles):
                op("vector", lambda e, st=sts[i], tc=tc: e.reduce_sum(
                    st[:, 0:1], x_t[:, tc, :], AX.X))
            for i, tc in enumerate(tiles):
                op("vector", lambda e, st=sts[i], tc=tc, i=i: e.scalar_tensor_tensor(
                    ao_t[:, i, :], x_t[:, tc, :], 1.0, x_t[:, tc, :],
                    ALU.mult, ALU.mult, accum_out=st[:, 1:2]),
                    wait=([("tS", H_T_FREE[0])] if i == 0 else []),
                    inc=([("vS", 1)] if i == 3 else []))
            LN2_JUNK[0] = c["vS"]
            for i in range(4):
                op("vector", lambda e, st=sts[i]: e.tensor_scalar(
                    st[:, 4:5], st[:, 0:1], 1.0 / D, None, ALU.mult))
            for i in range(4):
                op("vector", lambda e, st=sts[i]: e.tensor_mul(st[:, 5:6], st[:, 4:5], st[:, 4:5]))
            for i in range(4):
                op("vector", lambda e, st=sts[i]: e.tensor_scalar(
                    st[:, 11:12], st[:, 1:2], 1.0 / D, None, ALU.mult))
            for i in range(4):
                op("vector", lambda e, st=sts[i]: e.tensor_sub(st[:, 7:8], st[:, 11:12], st[:, 5:6]),
                   inc=([("vS", 1)] if i == 3 else []))
            LN2_VAR[tiles[0]] = c["vS"]

        def ln2_stats_fin(tiles):
            sts = [stat[:, tc] for tc in tiles]
            v_var = LN2_VAR[tiles[0]]
            for i in range(4):
                op("scalar", lambda e, st=sts[i]: e.activation(
                    st[:, 8:9], st[:, 7:8], AF.Ln, bias=epsc_s[:, 0:1]),
                    wait=([("vS", v_var)] if i == 0 else []))
            for i in range(4):
                op("scalar", lambda e, st=sts[i]: e.activation(
                    st[:, 9:10], st[:, 8:9], AF.Exp, scale=-0.5),
                    inc=([("sS", 1)] if i == 3 else []))
            s_exp = c["sS"]
            for i in range(4):
                op("vector", lambda e, st=sts[i]: e.tensor_mul(st[:, 10:11], st[:, 4:5], st[:, 9:10]),
                   wait=([("sS", s_exp)] if i == 0 else []),
                   inc=([("vS", 1)] if i == 3 else []))
            ST_READY[tiles[0]] = ("vS", c["vS"])

        def ln2_s_pieces(tiles):
            sts = [stat[:, tc] for tc in tiles]

            def sq(i0):
                for i in (i0, i0 + 1):
                    op("scalar", lambda e, st=sts[i], tc=tiles[i], i=i: e.activation(
                        ao_t[:, i, :], x_t[:, tc, :], AF.Square,
                        accum_out=st[:, 1:2]),
                        wait=([("vS", V_wo_evac[NSC - 1])] if i == 0 else []))

            def cp(i0):
                for i in (i0, i0 + 1):
                    op("scalar", lambda e, st=sts[i], tc=tiles[i], i=i: e.activation(
                        ao_t[:, i, :], x_t[:, tc, :], AF.Copy,
                        accum_out=st[:, 0:1]),
                        inc=([("sS", 1)] if i == 3 else []))
                if i0 == 2:
                    LN2_JUNK[0] = c["sS"]

            def mid1():
                for i in range(4):
                    op("scalar", lambda e, st=sts[i]: e.mul(st[:, 4:5], st[:, 0:1], 1.0 / D))
                for i in range(4):
                    op("scalar", lambda e, st=sts[i]: e.mul(st[:, 5:6], st[:, 4:5], st[:, 4:5]))

            def mid2():
                for i in range(4):
                    op("scalar", lambda e, st=sts[i]: e.mul(st[:, 6:7], st[:, 5:6], -1.0))
                for i in range(4):
                    op("scalar", lambda e, st=sts[i]: e.mul(st[:, 11:12], st[:, 1:2], 1.0 / D))

            def mid3():
                for i in range(4):
                    op("scalar", lambda e, st=sts[i]: e.add(st[:, 7:8], st[:, 11:12], st[:, 6:7]))

            def fin1():
                for i in range(4):
                    op("scalar", lambda e, st=sts[i]: e.activation(
                        st[:, 8:9], st[:, 7:8], AF.Ln, bias=epsc_s[:, 0:1]))

            def fin2():
                for i in range(4):
                    op("scalar", lambda e, st=sts[i]: e.activation(
                        st[:, 9:10], st[:, 8:9], AF.Exp, scale=-0.5))
                for i in range(4):
                    op("scalar", lambda e, st=sts[i]: e.mul(
                        st[:, 10:11], st[:, 4:5], st[:, 9:10]),
                       inc=([("sS", 1)] if i == 3 else []))
                ST_READY[tiles[0]] = ("sS", c["sS"])

            return {0: lambda: sq(0), 1: lambda: sq(2), 2: lambda: cp(0),
                    3: lambda: cp(2), 4: mid1, 5: mid2, 6: mid3,
                    7: lambda: (fin1(), fin2())}

        def ln_ts(tiles):
            sts = [stat[:, tc] for tc in tiles]
            for i, tc in enumerate(tiles):
                op("vector", lambda e, st=sts[i], tc=tc, i=i: e.tensor_scalar(
                    h_t[:, i, :], x_t[:, tc, :],
                    st[:, 9:10], st[:, 10:11], ALU.mult, ALU.subtract),
                    wait=([ST_READY[tiles[0]], ("tS", H_T_FREE[0])]
                          if i == 0 else []),
                    inc=([("vS", 1)] if i == 3 else []))
            V_HT[tiles[0]] = c["vS"]

        def ln_tr(tiles, dest_f, dst_cols, ev="scalar"):
            for i, tc in enumerate(tiles):
                trb = tr[i % 2]
                evi = ("scalar" if i % 2 == 0 else "vector") if ev == "mix" else ev
                es = "sS" if evi == "scalar" else "vS"
                for cc in range(8):
                    op("tensor", lambda e, cc=cc, i=i, trb=trb: e.transpose(
                        trb[:, cc * 128:(cc + 1) * 128],
                        h_t[:, i, cc * 128:(cc + 1) * 128], idn_s[:, :]),
                        wait=([("vS", V_HT[tiles[0]]), TR_FREE[i % 2],
                               ("dIN", IN_CONST)] if cc == 0 else []),
                        inc=([("tS", 1)] if cc == 7 else []))
                T_tr[tc] = c["tS"]
                cpf = ((lambda e, o, s: e.copy(o, s)) if evi == "scalar"
                       else (lambda e, o, s: e.tensor_copy(o, s)))
                for cc in range(8):
                    op(evi, lambda e, cc=cc, trb=trb, dc=dst_cols[i]: cpf(
                        e, dest_f[:, cc, dc * 128:(dc + 1) * 128],
                        trb[:, cc * 128:(cc + 1) * 128]),
                        wait=([("tS", T_tr[tc])] if cc == 0 else []),
                        inc=[(es, 1)])
                    if cc == 3:
                        HF_EVAC_H[tc] = (es, c[es])
                HF_EVAC[tc] = (es, c[es])
                TR_FREE[i % 2] = HF_EVAC[tc]
            H_T_FREE[0] = T_tr[tiles[3]]

        def ln_stats_pieces(tiles, xw):
            sts = [stat[:, tc] for tc in tiles]

            def pA():
                for i, tc in enumerate(tiles):
                    op("vector", lambda e, st=sts[i], tc=tc: e.reduce_sum(
                        st[:, 0:1], x_t[:, tc, :], AX.X),
                        wait=(xw if i == 0 else []),
                        inc=([("vS", 1)] if i == 3 else []))
                self_v = c["vS"]
                for i, tc in enumerate(tiles):
                    op("scalar", lambda e, st=sts[i], tc=tc: e.activation(
                        silu_sb[:, 0, :], x_t[:, tc, 0:512], AF.Square,
                        accum_out=st[:, 1:2]),
                        wait=(xw if i == 0 else []))
                    op("scalar", lambda e, st=sts[i], tc=tc: e.activation(
                        silu_sb[:, 0, :], x_t[:, tc, 512:1024], AF.Square,
                        accum_out=st[:, 2:3]))
                VS1[tiles[0]] = self_v

            def pB():
                for i in range(4):
                    op("scalar", lambda e, st=sts[i]: e.mul(st[:, 4:5], st[:, 0:1], 1.0 / D),
                       wait=([("vS", VS1[tiles[0]])] if i == 0 else []))
                for i in range(4):
                    op("scalar", lambda e, st=sts[i]: e.add(st[:, 3:4], st[:, 1:2], st[:, 2:3]))
                for i in range(4):
                    op("scalar", lambda e, st=sts[i]: e.mul(st[:, 5:6], st[:, 4:5], st[:, 4:5]))
                for i in range(4):
                    op("scalar", lambda e, st=sts[i]: e.mul(st[:, 6:7], st[:, 5:6], -1.0))

            def pC():
                for i in range(4):
                    op("scalar", lambda e, st=sts[i]: e.mul(st[:, 11:12], st[:, 3:4], 1.0 / D))
                for i in range(4):
                    op("scalar", lambda e, st=sts[i]: e.add(st[:, 7:8], st[:, 11:12], st[:, 6:7]))
                for i in range(4):
                    op("scalar", lambda e, st=sts[i]: e.activation(
                        st[:, 8:9], st[:, 7:8], AF.Ln, bias=epsc_s[:, 0:1]))
                for i in range(4):
                    op("scalar", lambda e, st=sts[i]: e.activation(
                        st[:, 9:10], st[:, 8:9], AF.Exp, scale=-0.5))

            def pD():
                for i in range(4):
                    op("scalar", lambda e, st=sts[i]: e.mul(
                        st[:, 10:11], st[:, 4:5], st[:, 9:10]),
                        inc=([("sS", 1)] if i == 3 else []))
                ST_READY[tiles[0]] = ("sS", c["sS"])
            return [pA, pB, pC, pD]

        def ln2v_pieces(tiles):
            sts = [stat[:, tc] for tc in tiles]

            def pE():
                for i, tc in enumerate(tiles):
                    op("vector", lambda e, st=sts[i], tc=tc: e.reduce_sum(
                        st[:, 0:1], x_t[:, tc, :], AX.X))
                for i, tc in enumerate(tiles):
                    op("vector", lambda e, st=sts[i], tc=tc, i=i: e.scalar_tensor_tensor(
                        h_t[:, i, :], x_t[:, tc, :], 1.0, x_t[:, tc, :],
                        ALU.mult, ALU.mult, accum_out=st[:, 1:2]),
                        wait=([("tS", H_T_FREE[0])] if i == 0 else []))

            def pF():
                for i in range(4):
                    op("vector", lambda e, st=sts[i]: e.tensor_scalar(
                        st[:, 4:5], st[:, 0:1], 1.0 / D, None, ALU.mult))
                for i in range(4):
                    op("vector", lambda e, st=sts[i]: e.tensor_mul(st[:, 5:6], st[:, 4:5], st[:, 4:5]))
                for i in range(4):
                    op("vector", lambda e, st=sts[i]: e.tensor_scalar(
                        st[:, 11:12], st[:, 1:2], 1.0 / D, None, ALU.mult))
                for i in range(4):
                    op("vector", lambda e, st=sts[i]: e.tensor_sub(
                        st[:, 7:8], st[:, 11:12], st[:, 5:6]),
                        inc=([("vS", 1)] if i == 3 else []))
                LN2_VAR[tiles[0]] = c["vS"]

            def pG():
                v_var = LN2_VAR[tiles[0]]
                for i in range(4):
                    op("scalar", lambda e, st=sts[i]: e.activation(
                        st[:, 8:9], st[:, 7:8], AF.Ln, bias=epsc_s[:, 0:1]),
                        wait=([("vS", v_var)] if i == 0 else []))
                for i in range(4):
                    op("scalar", lambda e, st=sts[i]: e.activation(
                        st[:, 9:10], st[:, 8:9], AF.Exp, scale=-0.5),
                        inc=([("sS", 1)] if i == 3 else []))
                s_exp = c["sS"]
                for i in range(4):
                    op("vector", lambda e, st=sts[i]: e.tensor_mul(
                        st[:, 10:11], st[:, 4:5], st[:, 9:10]),
                        wait=([("sS", s_exp)] if i == 0 else []),
                        inc=([("vS", 1)] if i == 3 else []))
                ST_READY[tiles[0]] = ("vS", c["vS"])
            return [pE, pF, pG]

        NB_FREE = [0] * H
        S_AO = [0] * H
        V_M32ADD = [0] * H
        S_mbf = [0] * H
        T_SCONS = [0] * H
        T_proj = [0] * NSC
        T_wo = [0] * NSC
        V_wo_evac = [0] * NSC
        V_PSUM23_FREE = 0
        T_attn_done = 0
        T_MID_BOX = [0]
        AOF_EVAC = [0] * 4
        WO_EVAC_V = [0] * (NSC * 8)

        PREV_SEVAC = [0] * 4
        V_RECV = [0] * H
        for sc in range(NSC):
            tc0 = sc * 4
            tiles = [tc0 + i for i in range(4)]
            last_sc = (sc == NSC - 1)
            if sc == 0:
                ln_stats(tiles, [("dIN", IN_X)])
                ln_ts(tiles)
            ln_tr(tiles, h_f, [0, 1, 2, 3], ev="mix")
            S_HF_W = [HF_EVAC[tc0 + 2], HF_EVAC[tc0 + 3]]
            S_hf = c["sS"]
            if sc == 0:
                tap("d_hf", h_f[:, :, :], [("sS", S_hf)])

            gidx = [0]
            PROJ_EVAC_S = [0] * 40
            pieces = {}
            pi = 6 if sc < NSC - 1 else 4
            step = 4 if sc < NSC - 1 else 3
            if sc + 1 < NSC:
                for p in ln_stats_pieces([tc0 + 4 + i for i in range(4)],
                                         [("dIN", IN_47 if sc == 0 else IN_ALL)]):
                    pieces[pi] = p
                    pi += step
            if sc > 0:
                for p in ln2v_pieces([tc0 - 4 + i for i in range(4)]):
                    pieces[pi] = p
                    pi += step

            def proj_group(mms, evac_fn, hf_wait=None, dw=None, hf_full=None):
                g = gidx[0]
                if g in pieces:
                    pieces.pop(g)()
                if sc > 0 and g < 2:
                    bank = pb[2 + g]
                else:
                    bank = pb[g % 2]
                w = (S_HF_W if hf_wait is None else [hf_wait]) + [
                     ("dW", DW_W["q"] if dw is None else dw), ("dIN", IN_CONST)]
                if sc > 0 and g < 2:
                    w.append(("vS", PREV_SEVAC[2 + g]))
                elif sc > 0 and g < 4:
                    w.append(("vS", V_wo_evac[sc - 1]))
                elif g >= 2:
                    w.append(("sS", PROJ_EVAC_S[g - 2]))
                n = len(mms)
                for i, (lf, rf) in enumerate(mms):
                    wi = w if i == 0 else (
                        [hf_full] if (hf_full is not None and i == 5) else [])
                    op("tensor", lambda e, lf=lf, rf=rf, i=i, n=n, bank=bank: e.matmul(
                        bank[:, 0:512], lf(), rf(),
                        start=(i == 0), stop=(i == n - 1)),
                        wait=wi,
                        inc=([("tS", 1)] if i == n - 1 else []))
                evac_fn(bank, c["tS"], g)
                PROJ_EVAC_S[g] = c["sS"]
                gidx[0] += 1

            def ev_act(dest_ap, bias_ap):
                def f(bank, td, g):
                    op("scalar", lambda e: e.activation(dest_ap, bank[:, 0:512],
                                                        AF.Exp, bias=bias_ap),
                       wait=[("tS", td)], inc=[("sS", 1)])
                return f

            def ev_exp(dest_ap):
                def f(bank, td, g):
                    op("scalar", lambda e: e.activation(dest_ap, bank[:, 0:512], AF.Exp),
                       wait=[("tS", td)], inc=[("sS", 1)])
                return f

            def ev_v(tcm, hh):
                def f(bank, td, g):
                    dst = v_t[:, tcm].rearrange("p (h c) -> p h c", c=65)[
                        :, hh * 8:(hh + 1) * 8, 0:64]
                    src = bank[:, 0:512].rearrange("p (h c) -> p h c", c=64)
                    op("scalar", lambda e: e.copy(dst, src),
                       wait=[("tS", td), ("gS", MEMSET)], inc=[("sS", 1)])
                return f

            wsel = [(wv_bf, bvr_s, "v")] if last_sc else \
                   [(wv_bf, bvr_s, "v"), (wk_bf, bkr_s, "k")]
            for tcm in range(4):
                for wbf, brow, evf in wsel:
                    for hh in range(2):
                        mms = [(lambda: onesr_s[:, :],
                                lambda hh=hh, brow=brow: brow[:, hh * 512:(hh + 1) * 512])]
                        mms += [(lambda dc=dc, tcm=tcm: h_f[:, dc, tcm * 128:(tcm + 1) * 128],
                                 lambda dc=dc, hh=hh, wbf=wbf: wbf[:, dc, hh * 512:(hh + 1) * 512])
                                for dc in range(8)]
                        if evf == "k":
                            proj_group(mms, ev_exp(k_t[:, tcm, hh * 512:(hh + 1) * 512]),
                                       hf_wait=HF_EVAC_H[tc0 + tcm], dw=DW_W["k"],
                                       hf_full=HF_EVAC[tc0 + tcm])
                        else:
                            proj_group(mms, ev_v(tcm, hh),
                                       hf_wait=HF_EVAC_H[tc0 + tcm], dw=DW_W["v"],
                                       hf_full=HF_EVAC[tc0 + tcm])
            for wbf, dest, bcol in [(wq_bf, q_f, bqc_s), (wk_bf, k_f, bkc_s)]:
                for oc in range(8):
                    mms = [(lambda dc=dc, wbf=wbf, oc=oc: wbf[:, dc, oc * 128:(oc + 1) * 128],
                            lambda dc=dc: h_f[:, dc, :]) for dc in range(8)]
                    proj_group(mms, ev_act(dest[:, oc, :], bcol[:, oc:oc + 1]))
            for g in sorted(pieces):
                pieces.pop(g)()
            T_proj[sc] = c["tS"]
            PROJ_LAST_S = c["sS"]
            if sc == 0:
                tap("d_qf", q_f[:, :, :], [("sS", PROJ_LAST_S)])
                tap("d_kf", k_f[:, :, :], [("sS", PROJ_LAST_S)])
                tap("d_kt", k_t[:, :, :], [("sS", PROJ_LAST_S)])
                tap("d_vt", v_t[:, :, :], [("sS", PROJ_LAST_S)])


            V_sevac = [[0] * 4 for _ in range(H)]
            T_SCONS_NEW = [0] * H

            def emit_scores(h):
                p64 = 64 * (h % 2)
                hc = h // 2
                for j in range(4):
                    nj = 512 - j * 128
                    sbk = pb[j]
                    w = [("sS", PROJ_LAST_S)]
                    if h >= 1:
                        w.append(("vS", V_sevac[h - 1][j]))
                    elif sc > 0:
                        w.append(("vS", V_wo_evac[sc - 1] if j < 2 else PREV_SEVAC[j]))
                    op("tensor", lambda e, j=j, nj=nj, sbk=sbk, p64=p64, hc=hc: e.matmul(
                        sbk[:, 0:nj],
                        k_f[p64:p64 + 64, hc, j * 128:(j + 1) * 128],
                        q_f[p64:p64 + 64, hc, j * 128:512],
                        start=True, stop=True), wait=w, inc=[("tS", 1)])
                    t_sj = c["tS"]
                    wv = [("tS", t_sj)]
                    if h >= 2:
                        wv.append(("tS", T_SCONS_NEW[h - 2]))
                    elif sc > 0:
                        wv.append(("tS", T_SCONS[H - 2 + h]))
                    op("vector", lambda e, j=j, nj=nj, sbk=sbk, h=h: e.tensor_tensor(
                        s_sb[:, h % 2, SOFF[j]:SOFF[j] + nj],
                        sbk[:, 0:nj], mask_s[:, 0:nj], ALU.mult),
                        wait=wv, inc=[("vS", 1)])
                    V_sevac[h][j] = c["vS"]

            emit_scores(0)
            for h in range(H):
                p64 = 64 * (h % 2)
                hc = h // 2
                nb = pb[4 + (h % 2)]
                if h + 1 < H:
                    emit_scores(h + 1)
                first_w = [("sS", S_AO[h - 2] if h >= 2 else S_AO[H - 2 + h]),
                           ("vS", V_RECV[h - 2] if h >= 2 else V_RECV[H - 2 + h])]
                nmm = 0
                if sc > 0:
                    for tcx in range(4):
                        w = (first_w + [("sS", S_mbf[h])]) if tcx == 0 else []
                        op("tensor", lambda e, tcx=tcx, nb=nb, p64=p64, hc=hc: e.matmul(
                            nb[:, tcx * 65:tcx * 65 + 65],
                            q_f[p64:p64 + 64, hc, tcx * 128:(tcx + 1) * 128],
                            M_bf[p64:p64 + 64, hc, :],
                            start=(tcx == 0), stop=False), wait=w)
                        nmm += 1
                intra_list = [(j, tcx) for j in range(4) for tcx in range(j, 4)]
                for j, tcx in intra_list[:-1]:
                    if tcx == j:
                        w = [("vS", V_sevac[h][j])]
                    else:
                        w = []
                    if nmm == 0:
                        w = first_w + w
                    op("tensor", lambda e, j=j, tcx=tcx, stf=(nmm == 0), nb=nb, h=h: e.matmul(
                        nb[:, tcx * 65:tcx * 65 + 65],
                        s_sb[:, h % 2,
                             SOFF[j] + (tcx - j) * 128:SOFF[j] + (tcx - j) * 128 + 128],
                        v_head(j, h),
                        start=stf, stop=False),
                        wait=w)
                    nmm += 1
                if not last_sc:
                    for j in range(4):
                        op("tensor", lambda e, j=j, nb=nb, p64=p64, h=h: e.matmul(
                            nb[p64:p64 + 64, 260:325],
                            k_t[:, j, h * 64:(h + 1) * 64],
                            v_head(j, h),
                            start=False, stop=False))
                op("tensor", lambda e, nb=nb, h=h: e.matmul(
                    nb[:, 3 * 65:3 * 65 + 65],
                    s_sb[:, h % 2, SOFF[3]:SOFF[3] + 128],
                    v_head(3, h),
                    start=False, stop=True),
                    wait=[("vS", V_sevac[h][3])], inc=[("tS", 1)])
                T_SCONS_NEW[h] = c["tS"]
                t_upd = c["tS"]
                op("vector", lambda e, nb=nb, h=h: e.reciprocal(
                    recip[:, h % 2, :],
                    nb[:, 0:260].rearrange("p (t c) -> p t c", c=65)[:, :, 64:65]
                      .rearrange("p t o -> p (t o)")),
                    wait=[("tS", t_upd),
                          ("sS", S_AO[h - 2] if h >= 2 else S_AO[H - 2 + h])],
                    inc=[("vS", 1)])
                v_recip = c["vS"]
                if not last_sc:
                    op("vector", lambda e, nb=nb, p64=p64, hc=hc: e.tensor_tensor(
                        M32[p64:p64 + 64, hc, :], M32[p64:p64 + 64, hc, :],
                        nb[p64:p64 + 64, 260:325], ALU.add),
                        inc=[("vS", 1)])
                    V_M32ADD[h] = c["vS"]
                V_RECV[h] = c["vS"]
                for tcx in range(4):
                    op("scalar", lambda e, tcx=tcx, nb=nb, h=h: e.activation(
                        ao_t[:, tcx, h * 64:(h + 1) * 64],
                        nb[:, tcx * 65:tcx * 65 + 64],
                        AF.Copy, scale=recip[:, h % 2, tcx:tcx + 1]),
                        wait=([("vS", v_recip)] if tcx == 0 else []),
                        inc=([("sS", 1)] if tcx == 3 else []))
                S_AO[h] = c["sS"]
                if not last_sc:
                    op("scalar", lambda e, p64=p64, hc=hc: e.copy(
                        M_bf[p64:p64 + 64, hc, :], M32[p64:p64 + 64, hc, :]),
                        wait=[("vS", V_M32ADD[h])], inc=[("sS", 1)])
                    S_mbf[h] = c["sS"]
                if sc == NSC - 1 and h == 7:
                    T_MID_BOX[0] = c["tS"]
            PREV_SEVAC = list(V_sevac[H - 1])
            T_SCONS = [0] * (H - 2) + [T_SCONS_NEW[H - 2], T_SCONS_NEW[H - 1]]
            S_AO_DONE = c["sS"]
            T_attn_done = c["tS"]
            if sc + 1 < NSC:
                ln_ts([tc0 + 4 + i for i in range(4)])
            if sc == 0:
                tap("d_aot", ao_t[:, :, :], [("sS", S_AO_DONE)])
                tap("d_m32", M32[:, :, :], [("vS", V_RECV[H - 1])])

            def a4_trs(tcm):
                trb = tr[tcm % 2]
                for cc in range(8):
                    w = [("sS", S_AO_DONE), TR_FREE[tcm % 2]] if cc == 0 else []
                    op("tensor", lambda e, cc=cc, tcm=tcm, trb=trb: e.transpose(
                        trb[:, cc * 128:(cc + 1) * 128],
                        ao_t[:, tcm, cc * 128:(cc + 1) * 128], idn_s[:, :]),
                        wait=w, inc=([("tS", 1)] if cc == 7 else []))
                t_tra = c["tS"]
                for cc in range(8):
                    op("scalar", lambda e, cc=cc, trb=trb,
                       dst=(ao_f if tcm % 2 == 0 else ao_f2): e.copy(
                        dst[:, cc, :], trb[:, cc * 128:(cc + 1) * 128]),
                        wait=([("tS", t_tra)] if cc == 0 else []),
                        inc=([("sS", 1)] if cc == 7 else []))
                AOF_EVAC[tcm] = c["sS"]
                TR_FREE[tcm % 2] = ("sS", AOF_EVAC[tcm])

            def a4_wo(tcm):
                for hh in range(2):
                    bank = pb[hh]
                    gix = sc * 8 + tcm * 2 + hh
                    w0 = [("sS", AOF_EVAC[tcm])]
                    if sc == 0 and gix == 0:
                        w0.append(("dW", DW_W["o"]))
                    if gix >= 2:
                        w0.append(("vS", WO_EVAC_V[gix - 2]))
                    else:
                        w0.append(("vS", V_sevac[H - 1][hh]))
                    op("tensor", lambda e, hh=hh, bank=bank: e.matmul(
                        bank[:, 0:512], onesr_s[:, :],
                        bor_s[:, hh * 512:(hh + 1) * 512],
                        start=True, stop=False), wait=w0)
                    for dc in range(8):
                        op("tensor", lambda e, dc=dc, hh=hh, bank=bank,
                           src=(ao_f if tcm % 2 == 0 else ao_f2): e.matmul(
                            bank[:, 0:512], src[:, dc, :],
                            wo_bf[:, dc, hh * 512:(hh + 1) * 512],
                            start=False, stop=(dc == 7)),
                            inc=([("tS", 1)] if dc == 7 else []))
                    t_wo = c["tS"]
                    op("vector", lambda e, tcm=tcm, hh=hh, bank=bank: e.tensor_tensor(
                        x_t[:, tc0 + tcm, hh * 512:(hh + 1) * 512],
                        x_t[:, tc0 + tcm, hh * 512:(hh + 1) * 512],
                        bank[:, 0:512], ALU.add),
                        wait=[("tS", t_wo)], inc=[("vS", 1)])
                    WO_EVAC_V[gix] = c["vS"]

            a4_trs(0)
            a4_trs(1)
            a4_wo(0)
            a4_trs(2)
            a4_wo(1)
            a4_trs(3)
            a4_wo(2)
            a4_wo(3)
            T_wo[sc] = c["tS"]
            V_wo_evac[sc] = c["vS"]
            if sc == NSC - 1:
                tap("d_xa", x_t[:, :, :], [("vS", V_wo_evac[sc])])


        op("gpsimd", lambda e: e.dma_start(
            gw_bf[:, :, :], gwT_d.ap().rearrange("(c p) e -> p c e", p=128)),
            wait=[("tS", T_proj[NSC - 1])], inc=[("dW", 16)])
        DW_GW = c["dW"]
        DW_GUP = [0] * E
        DW_DWN = [0] * E
        op("gpsimd", lambda e: e.dma_start(
            gup_bf[:, 4:6, :],
            gup_d[0].rearrange("(c p) q -> p c q", p=128)[:, 4:6, :]),
            wait=[("tS", T_MID_BOX[0])], inc=[("dW", 16)])
        op("gpsimd", lambda e: e.dma_start(
            gup_bf[:, 0:4, :],
            gup_d[0].rearrange("(c p) q -> p c q", p=128)[:, 0:4, :]),
            wait=[("tS", T_attn_done)], inc=[("dW", 16)])
        op("gpsimd", lambda e: e.dma_start(
            gup_bf[:, 6:8, :],
            gup_d[0].rearrange("(c p) q -> p c q", p=128)[:, 6:8, :]),
            inc=[("dW", 16)])
        DW_GUP[0] = c["dW"]


        T_tr = [0] * NT
        HF_EVAC = [0] * NT
        S_exp = [0] * 4
        V_z = [0] * (E * 32)
        V_de = [0] * (E * 32)
        V_out_h = [0] * (NT * 2)
        zi_box = [0]
        DW_GUP_CUR = [DW_GUP[0]]
        DW_GUP_CUR_A = [DW_GUP[0]]
        T_GATE_QP3 = [0]

        def emit_gate(e_i, tq, pieces=None):
            for qp in range(8):
                idx = tq * 8 + qp
                zi = zi_box[0]
                bp = (pb[2], pb[3]) if idx % 2 == 0 else (pb[0], pb[1])
                w = [("dW", DW_GUP_CUR_A[0] if qp < 4 else DW_GUP_CUR[0])]
                if e_i == 0:
                    w += [HF_EVAC[tq * 4 + 2], HF_EVAC[tq * 4 + 3]]
                else:
                    w += [("vS", H2F_V_BOX[0]), ("sS", H2F_S_BOX[0])]
                if zi >= 2:
                    w.append(("vS", V_z[zi - 2]))
                if e_i == 0 and idx == 0:
                    w.append(("vS", PREV_SEVAC[3]))
                if e_i == 0 and idx == 1:
                    w.append(("vS", V_wo_evac[NSC - 1]))
                for half, bank in enumerate(bp):
                    qb = qp * 128 + half * 1024
                    for dc in range(8):
                        op("tensor", lambda e, dc=dc, qb=qb, tq=tq, bank=bank: e.matmul(
                            bank[:, 0:512],
                            gup_bf[:, dc, qb:qb + 128],
                            h2_f[:, dc, tq * 512:(tq + 1) * 512],
                            start=(dc == 0), stop=(dc == 7)),
                            wait=(w if (half == 0 and dc == 0) else []),
                            inc=([("tS", 1)] if (half == 1 and dc == 7) else []))
                t_gu = c["tS"]
                if qp == 3:
                    T_GATE_QP3[0] = t_gu
                ws = [("tS", t_gu)]
                if zi >= 2:
                    ws.append(("vS", V_z[zi - 2]))
                op("scalar", lambda e, bp=bp, idx=idx: e.activation(
                    silu_sb[:, idx % 2, :], bp[0][:, 0:512], AF.Silu),
                    wait=ws, inc=[("sS", 1)])
                s_si = c["sS"]
                if pieces and qp in pieces:
                    pieces.pop(qp)()
                op("vector", lambda e, bp=bp, idx=idx, qp=qp, tq=tq: e.tensor_tensor(
                    z_sb[:, qp, tq * 512:(tq + 1) * 512],
                    silu_sb[:, idx % 2, :], bp[1][:, 0:512], ALU.mult),
                    wait=[("sS", s_si)], inc=[("vS", 1)])
                V_z[zi] = c["vS"]
                zi_box[0] = zi + 1

        H2F_DONE_BOX = [0]
        H2F_V_BOX = [0]
        H2F_S_BOX = [0]
        ln_ts([0, 1, 2, 3])
        for g4 in range(4):
            tiles = [g4 * 4 + i for i in range(4)]
            ln_tr(tiles, h2_f, tiles, ev="vector")
            rb = pb[4 + (g4 % 2)]
            w = [HF_EVAC[tiles[2]], HF_EVAC[tiles[3]], ("dW", DW_GW)]
            if g4 >= 2:
                w.append(("sS", S_exp[g4 - 2]))
            else:
                w.append(("sS", S_AO[H - 2 + g4]))
                w.append(("vS", V_RECV[H - 2 + g4]))
            nmm = 0
            for i, tc in enumerate(tiles):
                for dc in range(8):
                    op("tensor", lambda e, dc=dc, tc=tc, rb=rb, i=i, nmm=nmm: e.matmul(
                        rb[:, i * 8:(i + 1) * 8], h2_f[:, dc, tc * 128:(tc + 1) * 128],
                        gw_bf[:, dc, :], start=(nmm == 0), stop=(nmm == 31)),
                        wait=(w if nmm == 0 else []),
                        inc=([("tS", 1)] if nmm == 31 else []))
                    nmm += 1
            t_rout = c["tS"]
            for i in range(4):
                op("vector", lambda e, i=i, rb=rb, rs=rstat[:, g4 * 4 + i]: e.reduce_max(
                    rs[:, 0:1], rb[:, i * 8:(i + 1) * 8], AX.X, negate=True),
                    wait=([("tS", t_rout)] if i == 0 else []),
                    inc=([("vS", 1)] if i == 3 else []))
            v_nmx = c["vS"]
            for i in range(4):
                op("scalar", lambda e, i=i, rb=rb, rs=rstat[:, g4 * 4 + i]: e.activation(
                    rt_e[:, i, :], rb[:, i * 8:(i + 1) * 8], AF.Exp, bias=rs[:, 0:1]),
                    wait=([("vS", v_nmx)] if i == 0 else []),
                    inc=([("sS", 1)] if i == 3 else []))
            S_exp[g4] = c["sS"]

            def R(fn, wait_first=None, inc_last=False):
                for i in range(4):
                    op("vector", lambda e, i=i, rs=rstat[:, g4 * 4 + i],
                       ee=rt_e[:, i], mk=msk1[:, i], tp=tmp8[:, i]: fn(e, i, rs, ee, mk, tp),
                       wait=(wait_first if (i == 0 and wait_first) else []),
                       inc=([("vS", 1)] if (i == 3 and inc_last) else []))

            R(lambda e, i, rs, ee, mk, tp: e.reduce_sum(rs[:, 1:2], ee, AX.X),
              wait_first=[("sS", S_exp[g4])])
            R(lambda e, i, rs, ee, mk, tp: e.reciprocal(rs[:, 2:3], rs[:, 1:2]))
            R(lambda e, i, rs, ee, mk, tp: e.tensor_scalar(
                ee, ee, rs[:, 2:3], None, ALU.mult))
            R(lambda e, i, rs, ee, mk, tp: e.reduce_max(rs[:, 3:4], ee, AX.X))
            R(lambda e, i, rs, ee, mk, tp: e.tensor_scalar(
                mk, ee, rs[:, 3:4], None, ALU.is_equal))
            R(lambda e, i, rs, ee, mk, tp: e.tensor_mul(tp, ee, mk))
            R(lambda e, i, rs, ee, mk, tp: e.tensor_sub(tp, ee, tp))
            R(lambda e, i, rs, ee, mk, tp: e.reduce_max(rs[:, 4:5], tp, AX.X))
            R(lambda e, i, rs, ee, mk, tp: e.tensor_scalar(
                tp, tp, rs[:, 4:5], None, ALU.is_equal))
            R(lambda e, i, rs, ee, mk, tp: e.tensor_add(mk, mk, tp))
            R(lambda e, i, rs, ee, mk, tp: e.scalar_tensor_tensor(
                rs[:, 5:6], rs[:, 3:4], 1e-9, rs[:, 4:5], ALU.add, ALU.add))
            R(lambda e, i, rs, ee, mk, tp: e.reciprocal(rs[:, 6:7], rs[:, 5:6]))
            R(lambda e, i, rs, ee, mk, tp: e.tensor_mul(ee, ee, mk))
            R(lambda e, i, rs, ee, mk, tp, g4=g4: e.tensor_scalar(
                routed[:, g4 * 4 + i, :], ee, rs[:, 6:7], None, ALU.mult),
              inc_last=True)
            if g4 + 1 < 4:
                ln_ts([g4 * 4 + 4 + i for i in range(4)])
            emit_gate(0, g4, pieces=(
                ln2_s_pieces([(NSC - 1) * 4 + i for i in range(4)])
                if g4 == 0 else None))
            if g4 == 0:
                op("gpsimd", lambda e: e.dma_start(
                    dwn_bf[:, :, :], dwn_d[0].rearrange("(c p) q -> p c q", p=128)),
                    wait=[("tS", T_wo[NSC - 1]), ("sS", LN2_JUNK[0])],
                    inc=[("dW", 16)])
                DW_DWN[0] = c["dW"]
        H2F_DONE = c["sS"]
        H2F_V = c["vS"]
        H2F_DONE_BOX[0] = H2F_DONE
        H2F_V_BOX[0] = H2F_V
        H2F_S_BOX[0] = H2F_DONE
        V_ROUTED_ALL = c["vS"]
        tap("d_h2f", h2_f[:, :, :], [("sS", H2F_DONE)])
        tap("d_routed", routed[:, :, :], [("vS", V_ROUTED_ALL)])

        di = 0
        for e_i in range(E):
            if e_i > 0:
                for tq in range(4):
                    emit_gate(e_i, tq)
            T_gu_last = c["tS"]
            V_z_all = c["vS"]
            if e_i == 0:
                tap("d_z0", z_sb[:, :, :], [("vS", V_z_all)])
            if e_i + 1 < E:
                t_qp3 = T_GATE_QP3[0]
                for j, (lo, hi) in enumerate([(0, 512), (1024, 1536),
                                              (512, 1024), (1536, 2048)]):
                    op("gpsimd", lambda e, e_i=e_i, lo=lo, hi=hi: e.dma_start(
                        gup_bf[:, :, lo:hi],
                        gup_d[e_i + 1].rearrange(
                            "(c p) q -> p c q", p=128)[:, :, lo:hi]),
                        wait=([("tS", t_qp3)] if j == 0 else
                              ([("tS", T_gu_last)] if j == 2 else [])),
                        inc=[("dW", 16)])
                    if j == 1:
                        DW_GUP_A = c["dW"]
                DW_GUP[e_i + 1] = c["dW"]
                DW_GUP_CUR[0] = c["dW"]
                DW_GUP_CUR_A[0] = DW_GUP_A
            for tcm in range(NT):
                for hh in range(2):
                    idx = tcm * 2 + hh
                    bank = pb[4 + (idx % 2)]
                    w = [("dW", DW_DWN[e_i]),
                         ("vS", V_z[e_i * 32 + (tcm // 4) * 8 + 7])]
                    if di >= 2:
                        w.append(("vS", V_de[di - 2]))
                    if e_i == 0:
                        w.append(("vS", V_ROUTED_ALL))
                        if idx < 2:
                            w.append(("sS", S_exp[2 + idx]))
                    for qc in range(8):
                        op("tensor", lambda e, qc=qc, tcm=tcm, hh=hh, bank=bank: e.matmul(
                            bank[:, 0:512],
                            z_sb[:, qc, tcm * 128:(tcm + 1) * 128],
                            dwn_bf[:, qc, hh * 512:(hh + 1) * 512],
                            start=(qc == 0), stop=(qc == 7)),
                            wait=(w if qc == 0 else []),
                            inc=([("tS", 1)] if qc == 7 else []))
                    t_d = c["tS"]
                    op("vector",
                       lambda e, tcm=tcm, hh=hh, bank=bank, e_i=e_i: e.scalar_tensor_tensor(
                           x_t[:, tcm, hh * 512:(hh + 1) * 512],
                           bank[:, 0:512], routed[:, tcm, e_i:e_i + 1],
                           x_t[:, tcm, hh * 512:(hh + 1) * 512],
                           ALU.mult, ALU.add),
                       wait=[("tS", t_d)], inc=[("vS", 1)])
                    V_de[di] = c["vS"]
                    di += 1
                    if e_i == E - 1:
                        V_out_h[tcm * 2 + hh] = c["vS"]
            T_dwn_last = c["tS"]
            if e_i + 1 < E:
                op("gpsimd", lambda e, e_i=e_i: e.dma_start(
                    dwn_bf[:, :, :],
                    dwn_d[e_i + 1].rearrange("(c p) q -> p c q", p=128)),
                    wait=[("tS", T_dwn_last)], inc=[("dW", 16)])
                DW_DWN[e_i + 1] = c["dW"]

        for tcm in range(NT):
            for hh in range(2):
                op("sync" if hh == 0 else "scalar",
                   lambda e, tcm=tcm, hh=hh: e.dma_start(
                    y_d[tcm * 128:(tcm + 1) * 128, hh * 512:(hh + 1) * 512],
                    x_t[:, tcm, hh * 512:(hh + 1) * 512]),
                   wait=[("vS", V_out_h[tcm * 2 + hh])], inc=[("dOUT", 16)])
        if which == "sync":
            eng.wait_ge(sems["dOUT"], c["dOUT"])

    with nc.Block() as block:
        @block.sync
        def _(e):
            walk("sync", e)

        @block.gpsimd
        def _(e):
            walk("gpsimd", e)

        @block.tensor
        def _(e):
            walk("tensor", e)

        @block.vector
        def _(e):
            walk("vector", e)

        @block.scalar
        def _(e):
            walk("scalar", e)

    return nc


_NC_CACHE = [None]


def kernel(**inputs):
    x = np.ascontiguousarray(np.asarray(inputs["L0_attn_x"], dtype=np.float32))
    wqT = np.ascontiguousarray(np.asarray(inputs["L0_attn_Wq"], dtype=np.float32).T)
    wkT = np.ascontiguousarray(np.asarray(inputs["L0_attn_Wk"], dtype=np.float32).T)
    wvT = np.ascontiguousarray(np.asarray(inputs["L0_attn_Wv"], dtype=np.float32).T)
    woT = np.ascontiguousarray(np.asarray(inputs["L0_attn_Wo"], dtype=np.float32).T)
    bq = np.asarray(inputs["L0_attn_bq"], dtype=np.float32)
    bk = np.asarray(inputs["L0_attn_bk"], dtype=np.float32)
    bv = np.asarray(inputs["L0_attn_bv"], dtype=np.float32)
    bo = np.asarray(inputs["L0_attn_bo"], dtype=np.float32)
    gw = np.asarray(inputs["L0_ffn_gate_w"], dtype=np.float32)
    gup = np.ascontiguousarray(np.asarray(inputs["L0_ffn_gate_up_proj"], dtype=np.float32))
    dwn = np.ascontiguousarray(np.asarray(inputs["L0_ffn_down_proj"], dtype=np.float32))

    common = dict(
        wqT=wqT, wkT=wkT, wvT=wvT, woT=woT,
        bq_col=np.ascontiguousarray(bq.reshape(8, 128).T),
        bk_col=np.ascontiguousarray(bk.reshape(8, 128).T),
        bk_row=np.ascontiguousarray(bk.reshape(1, D)),
        bv_row=np.ascontiguousarray(bv.reshape(1, D)),
        bo_row=np.ascontiguousarray(bo.reshape(1, D)),
        gwT=np.ascontiguousarray(gw.T), gup=gup, dwn=dwn,
        mask=np.concatenate([np.triu(np.ones((128, 128), np.float32)),
                             np.ones((128, 384), np.float32)], axis=1),
        idn=np.eye(128, dtype=np.float32).astype(ml_dtypes.bfloat16),
        ones_row=np.ones((1, 128), np.float32).astype(ml_dtypes.bfloat16),
        epsc=np.full((128, 1), 1e-5, np.float32),
    )
    if _NC_CACHE[0] is None:
        _NC_CACHE[0] = build_nc()
    nc = _NC_CACHE[0]
    in_maps = [dict(x=np.ascontiguousarray(x[b]), **common) for b in range(B)]
    res = run_bass_kernel_spmd(nc, in_maps, core_ids=list(range(B)))
    return np.stack([r["y"] for r in res.results], axis=0)


# version 46
# speedup vs baseline: 1.0053x; 1.0031x over previous
import numpy as np
import ml_dtypes
import concourse.bass as bass
import concourse.mybir as mybir
from concourse.bass_utils import run_bass_kernel_spmd

F32 = mybir.dt.float32
BF16 = mybir.dt.bfloat16
U8 = mybir.dt.uint8
AF = mybir.ActivationFunctionType
ALU = mybir.AluOpType
AX = mybir.AxisListType

B = 8
S = 2048
D = 1024
NT = 16
NSC = 4
H = 16
E = 8
SOFF = [0, 512, 896, 1152]


def build_nc(dbg=False):
    nc = bass.Bass("TRN2", target_bir_lowering=False,
                   detect_race_conditions=False, num_swdge_queues=1)

    x_d = nc.dram_tensor("x", [S, D], F32, kind="ExternalInput")
    wqT_d = nc.dram_tensor("wqT", [D, D], F32, kind="ExternalInput")
    wkT_d = nc.dram_tensor("wkT", [D, D], F32, kind="ExternalInput")
    wvT_d = nc.dram_tensor("wvT", [D, D], F32, kind="ExternalInput")
    woT_d = nc.dram_tensor("woT", [D, D], F32, kind="ExternalInput")
    bqc_d = nc.dram_tensor("bq_col", [128, 8], F32, kind="ExternalInput")
    bkc_d = nc.dram_tensor("bk_col", [128, 8], F32, kind="ExternalInput")
    bkr_d = nc.dram_tensor("bk_row", [1, D], F32, kind="ExternalInput")
    bvr_d = nc.dram_tensor("bv_row", [1, D], F32, kind="ExternalInput")
    bor_d = nc.dram_tensor("bo_row", [1, D], F32, kind="ExternalInput")
    gwT_d = nc.dram_tensor("gwT", [D, E], F32, kind="ExternalInput")
    gup_d = nc.dram_tensor("gup", [E, D, 2 * D], F32, kind="ExternalInput")
    dwn_d = nc.dram_tensor("dwn", [E, D, D], F32, kind="ExternalInput")
    mask_d = nc.dram_tensor("mask", [128, 512], F32, kind="ExternalInput")
    idn_d = nc.dram_tensor("idn", [128, 128], BF16, kind="ExternalInput")
    onesr_d = nc.dram_tensor("ones_row", [1, 128], BF16, kind="ExternalInput")
    epsc_d = nc.dram_tensor("epsc", [128, 1], F32, kind="ExternalInput")
    y_d = nc.dram_tensor("y", [S, D], F32, kind="ExternalOutput")

    x_t = nc.alloc_sbuf_tensor("x_t", [128, NT, D], F32)
    h_t = nc.alloc_sbuf_tensor("h_t", [128, 4, D], BF16)
    stat = nc.alloc_sbuf_tensor("stat", [128, NT, 12], F32)
    rstat = nc.alloc_sbuf_tensor("rstat", [128, 16, 8], F32)
    recip = nc.alloc_sbuf_tensor("recip", [128, 2, 4], F32)
    rt_e = nc.alloc_sbuf_tensor("rt_e", [128, 4, 8], F32)
    msk1 = nc.alloc_sbuf_tensor("msk1", [128, 4, 8], F32)
    tmp8 = nc.alloc_sbuf_tensor("tmp8", [128, 4, 8], F32)
    silu_sb = nc.alloc_sbuf_tensor("silu_sb", [128, 2, 512], BF16)
    mask_s = nc.alloc_sbuf_tensor("mask_s", [128, 512], F32)
    idn_s = nc.alloc_sbuf_tensor("idn_s", [128, 128], BF16)
    onesr_s = nc.alloc_sbuf_tensor("onesr_s", [1, 128], BF16)
    bqc_s = nc.alloc_sbuf_tensor("bqc_s", [128, 8], F32)
    bkc_s = nc.alloc_sbuf_tensor("bkc_s", [128, 8], F32)
    bkr_s = nc.alloc_sbuf_tensor("bkr_s", [1, D], BF16)
    bvr_s = nc.alloc_sbuf_tensor("bvr_s", [1, D], BF16)
    bor_s = nc.alloc_sbuf_tensor("bor_s", [1, D], BF16)
    epsc_s = nc.alloc_sbuf_tensor("epsc_s", [128, 1], F32)
    routed = nc.alloc_sbuf_tensor("routed", [128, NT, E], F32)
    gw_bf = nc.alloc_sbuf_tensor("gw_bf", [128, 8, E], BF16)

    def al32(v):
        return (v + 31) & ~31

    a_specs = [
        ("q_f", [128, 8, 512], BF16), ("k_f", [128, 8, 512], BF16),
        ("k_t", [128, 4, D], BF16), ("v_t", [128, 4, H * 65], BF16),
        ("s_sb", [128, 2, 1280], BF16), ("ao_t", [128, 4, D], BF16),
        ("ao_f", [128, 8, 128], BF16), ("M32", [128, 8, 65], F32),
        ("M_bf", [128, 8, 65], BF16), ("h_f", [128, 8, 512], BF16),
        ("wq_bf", [128, 8, D], BF16), ("wk_bf", [128, 8, D], BF16),
        ("wv_bf", [128, 8, D], BF16), ("wo_bf", [128, 8, D], BF16),
    ]
    offs, cur = {}, 0
    for name, shp, dt in a_specs:
        offs[name] = cur
        cur = al32(cur + int(np.prod(shp[1:])) * mybir.dt.size(dt))
    pool_bytes = cur
    b_specs = [
        ("gup_bf", [128, 8, 2 * D], BF16, offs["q_f"]),
        ("dwn_bf", [128, 8, D], BF16, offs["s_sb"]),
        ("z_sb", [128, 8, 2 * D], BF16, offs["wq_bf"]),
        ("h2_f", [128, 8, S], BF16, offs["wv_bf"]),
    ]
    assert offs["s_sb"] - offs["q_f"] >= 8 * 2 * D * 2
    assert offs["h_f"] - offs["s_sb"] >= 8 * D * 2
    assert offs["wv_bf"] - offs["wq_bf"] >= 8 * 2 * D * 2
    assert pool_bytes - offs["wv_bf"] >= 8 * S * 2
    pool = nc.alloc_sbuf_tensor("pool", [128, pool_bytes], U8)
    base = nc.lookup_mloc(pool).addr
    TT = {}
    for name, shp, dt in a_specs:
        TT[name] = nc.alloc_sbuf_tensor_at(name, shp, dt, offset=base + offs[name])
    for name, shp, dt, off in b_specs:
        TT[name] = nc.alloc_sbuf_tensor_at(name, shp, dt, offset=base + off)
    TT["ao_f2"] = nc.alloc_sbuf_tensor_at("ao_f2", [128, 8, 128], BF16,
                                          offset=base + offs["h_f"])
    q_f, k_f, k_t, v_t = TT["q_f"], TT["k_f"], TT["k_t"], TT["v_t"]
    s_sb, ao_t, ao_f = TT["s_sb"], TT["ao_t"], TT["ao_f"]
    ao_f2 = TT["ao_f2"]
    M32, M_bf, h_f = TT["M32"], TT["M_bf"], TT["h_f"]
    wq_bf, wk_bf, wv_bf, wo_bf = TT["wq_bf"], TT["wk_bf"], TT["wv_bf"], TT["wo_bf"]
    gup_bf, dwn_bf, z_sb, h2_f = TT["gup_bf"], TT["dwn_bf"], TT["z_sb"], TT["h2_f"]

    pb = [nc.alloc_psum_tensor(f"pb{i}", [128, 512], F32) for i in range(6)]
    tr = [nc.alloc_psum_tensor(f"tr{i}", [128, 1024], BF16) for i in range(2)]

    sems = {n: nc.alloc_semaphore(name=n)
            for n in ["dIN", "dW", "dOUT", "tS", "vS", "sS", "gS"]}

    dbg_d = {}
    if dbg:
        for name, shp, dt in [
            ("d_hf", [128, 8, 512], BF16), ("d_qf", [128, 8, 512], BF16),
            ("d_kf", [128, 8, 512], BF16), ("d_kt", [128, 4, D], BF16),
            ("d_vt", [128, 4, H * 65], BF16), ("d_aot", [128, 4, D], BF16),
            ("d_m32", [128, 8, 65], F32), ("d_xa", [128, NT, D], F32),
            ("d_h2f", [128, 8, S], BF16), ("d_routed", [128, NT, E], F32),
            ("d_z0", [128, 8, 2 * D], BF16),
        ]:
            dbg_d[name] = nc.dram_tensor(name, shp, dt, kind="ExternalOutput")

    def v_head(tc, h):
        return v_t[:, tc].rearrange("p (h c) -> p h c", c=65)[:, h, :]

    def walk(which, eng):
        c = {"dIN": 0, "dW": 0, "dOUT": 0, "tS": 0, "vS": 0, "sS": 0, "gS": 0}

        def op(engine, fn, inc=(), wait=()):
            if which == engine:
                for sn, v in wait:
                    if v > 0:
                        eng.wait_ge(sems[sn], v)
                ins = fn(eng)
                for sn, n in inc:
                    ins.then_inc(sems[sn], n)
            for sn, n in inc:
                c[sn] += n

        def tap(name, src_ap, wait):
            if not dbg:
                return
            op("sync", lambda e: e.dma_start(dbg_d[name].ap(), src_ap),
               wait=wait, inc=[("dOUT", 16)])
            tgt = c["dOUT"]
            if which in ("tensor", "vector", "scalar", "gpsimd"):
                eng.wait_ge(sems["dOUT"], tgt)

        op("vector", lambda e: e.memset(stat[:, 0, 11:12], 1.0), inc=[("vS", 1)])
        V_WARM = c["vS"]
        op("scalar", lambda e: e.activation(
            stat[:, 0, 10:11], stat[:, 0, 11:12], AF.Square),
            wait=[("vS", V_WARM)])
        op("scalar", lambda e: e.activation(
            stat[:, 0, 10:11], stat[:, 0, 11:12], AF.Ln,
            bias=stat[:, 0, 11:12]))
        op("scalar", lambda e: e.activation(
            stat[:, 0, 10:11], stat[:, 0, 11:12], AF.Exp, scale=-0.5))

        op("sync", lambda e: e.dma_start(
            x_t[:, 0:2, :],
            x_d.ap().rearrange("(c p) d -> p c d", p=128)[:, 0:2, :]),
            inc=[("dIN", 16)])
        IN_X01 = c["dIN"]
        op("sync", lambda e: e.dma_start(
            x_t[:, 2:4, :],
            x_d.ap().rearrange("(c p) d -> p c d", p=128)[:, 2:4, :]),
            inc=[("dIN", 16)])
        IN_X = c["dIN"]
        for dst, src in [(mask_s[:, :], mask_d[:, :]), (idn_s[:, :], idn_d[:, :]),
                         (onesr_s[:, :], onesr_d[:, :]), (bqc_s[:, :], bqc_d[:, :]),
                         (bkc_s[:, :], bkc_d[:, :]), (epsc_s[:, :], epsc_d[:, :])]:
            op("sync", lambda e, d=dst, s=src: e.dma_start(d, s), inc=[("dIN", 16)])
        IN_CONST = c["dIN"]
        op("sync", lambda e: e.dma_start(
            x_t[:, 4:8, :],
            x_d.ap().rearrange("(c p) d -> p c d", p=128)[:, 4:8, :]),
            inc=[("dIN", 16)])
        IN_47 = c["dIN"]
        IN_ALL_BOX = [0]

        op("gpsimd", lambda e: e.memset(M32[:, :, :], 0.0), inc=[("gS", 1)])
        op("gpsimd", lambda e: e.memset(
            v_t[:, :, :].rearrange("p t (h c) -> p t h c", c=65)[:, :, :, 64:65], 1.0),
            inc=[("gS", 1)])
        MEMSET = c["gS"]
        for dst, src in [(bkr_s, bkr_d), (bvr_s, bvr_d), (bor_s, bor_d)]:
            op("gpsimd", lambda e, a=dst, b=src: e.dma_start(a[:, :], b[:, :]),
               inc=[("dW", 16)])
        DW_W = {}
        for nm, wbf, wd in [("v", wv_bf, wvT_d), ("k", wk_bf, wkT_d),
                            ("q", wq_bf, wqT_d), ("o", wo_bf, woT_d)]:
            for ch in range(2):
                op("gpsimd", lambda e, a=wbf, b=wd, ch=ch: e.dma_start(
                    a[:, ch * 4:(ch + 1) * 4, :],
                    b.ap().rearrange("(c p) n -> p c n", p=128)[
                        :, ch * 4:(ch + 1) * 4, :]),
                    wait=([("dIN", IN_X01)] if (nm == "v" and ch == 0) else []),
                    inc=[("dW", 16)])
            DW_W[nm] = c["dW"]
        op("sync", lambda e: e.dma_start(
            x_t[:, 8:16, :],
            x_d.ap().rearrange("(c p) d -> p c d", p=128)[:, 8:16, :]),
            wait=[("dW", DW_W["k"])], inc=[("dIN", 16)])
        IN_ALL = c["dIN"]
        IN_ALL_BOX[0] = IN_ALL

        TR_FREE = [("sS", 0), ("sS", 0)]
        T_tr = [0] * NT
        HF_EVAC = [0] * NT
        ST_READY = {}
        V_HT = {}
        H_T_FREE = [0]
        LN2_JUNK = [0]
        LN2_VAR = {}
        VS1 = {}

        def ln_stats(tiles, xw, xw2=None):
            sts = [stat[:, tc] for tc in tiles]
            for i, tc in enumerate(tiles):
                op("vector", lambda e, st=sts[i], tc=tc: e.reduce_sum(
                    st[:, 0:1], x_t[:, tc, :], AX.X),
                    wait=(xw if i == 0 else (xw2 or []) if i == 2 else []),
                    inc=([("vS", 1)] if i == 3 else []))
            v_s1 = c["vS"]
            for i, tc in enumerate(tiles):
                op("scalar", lambda e, st=sts[i], tc=tc: e.activation(
                    silu_sb[:, 0, :], x_t[:, tc, 0:512], AF.Square,
                    accum_out=st[:, 1:2]),
                    wait=(xw if i == 0 else (xw2 or []) if i == 2 else []))
                op("scalar", lambda e, st=sts[i], tc=tc: e.activation(
                    silu_sb[:, 0, :], x_t[:, tc, 512:1024], AF.Square,
                    accum_out=st[:, 2:3]))
            for i in range(4):
                op("scalar", lambda e, st=sts[i]: e.mul(st[:, 4:5], st[:, 0:1], 1.0 / D),
                   wait=([("vS", v_s1)] if i == 0 else []))
            for i in range(4):
                op("scalar", lambda e, st=sts[i]: e.add(st[:, 3:4], st[:, 1:2], st[:, 2:3]))
            for i in range(4):
                op("scalar", lambda e, st=sts[i]: e.mul(st[:, 5:6], st[:, 4:5], st[:, 4:5]))
            for i in range(4):
                op("scalar", lambda e, st=sts[i]: e.mul(st[:, 6:7], st[:, 5:6], -1.0))
            for i in range(4):
                op("scalar", lambda e, st=sts[i]: e.mul(st[:, 11:12], st[:, 3:4], 1.0 / D))
            for i in range(4):
                op("scalar", lambda e, st=sts[i]: e.add(st[:, 7:8], st[:, 11:12], st[:, 6:7]))
            for i in range(4):
                op("scalar", lambda e, st=sts[i]: e.activation(
                    st[:, 8:9], st[:, 7:8], AF.Ln, bias=epsc_s[:, 0:1]))
            for i in range(4):
                op("scalar", lambda e, st=sts[i]: e.activation(
                    st[:, 9:10], st[:, 8:9], AF.Exp, scale=-0.5))
            for i in range(4):
                op("scalar", lambda e, st=sts[i]: e.mul(st[:, 10:11], st[:, 4:5], st[:, 9:10]),
                   inc=([("sS", 1)] if i == 3 else []))
            ST_READY[tiles[0]] = ("sS", c["sS"])

        def ln2_stats_v(tiles, sc):
            sts = [stat[:, tc] for tc in tiles]
            for i, tc in enumerate(tiles):
                op("vector", lambda e, st=sts[i], tc=tc: e.reduce_sum(
                    st[:, 0:1], x_t[:, tc, :], AX.X))
            for i, tc in enumerate(tiles):
                op("vector", lambda e, st=sts[i], tc=tc, i=i: e.scalar_tensor_tensor(
                    ao_t[:, i, :], x_t[:, tc, :], 1.0, x_t[:, tc, :],
                    ALU.mult, ALU.mult, accum_out=st[:, 1:2]),
                    wait=([("tS", H_T_FREE[0])] if i == 0 else []),
                    inc=([("vS", 1)] if i == 3 else []))
            LN2_JUNK[0] = c["vS"]
            for i in range(4):
                op("vector", lambda e, st=sts[i]: e.tensor_scalar(
                    st[:, 4:5], st[:, 0:1], 1.0 / D, None, ALU.mult))
            for i in range(4):
                op("vector", lambda e, st=sts[i]: e.tensor_mul(st[:, 5:6], st[:, 4:5], st[:, 4:5]))
            for i in range(4):
                op("vector", lambda e, st=sts[i]: e.tensor_scalar(
                    st[:, 11:12], st[:, 1:2], 1.0 / D, None, ALU.mult))
            for i in range(4):
                op("vector", lambda e, st=sts[i]: e.tensor_sub(st[:, 7:8], st[:, 11:12], st[:, 5:6]),
                   inc=([("vS", 1)] if i == 3 else []))
            LN2_VAR[tiles[0]] = c["vS"]

        def ln2_stats_fin(tiles):
            sts = [stat[:, tc] for tc in tiles]
            v_var = LN2_VAR[tiles[0]]
            for i in range(4):
                op("scalar", lambda e, st=sts[i]: e.activation(
                    st[:, 8:9], st[:, 7:8], AF.Ln, bias=epsc_s[:, 0:1]),
                    wait=([("vS", v_var)] if i == 0 else []))
            for i in range(4):
                op("scalar", lambda e, st=sts[i]: e.activation(
                    st[:, 9:10], st[:, 8:9], AF.Exp, scale=-0.5),
                    inc=([("sS", 1)] if i == 3 else []))
            s_exp = c["sS"]
            for i in range(4):
                op("vector", lambda e, st=sts[i]: e.tensor_mul(st[:, 10:11], st[:, 4:5], st[:, 9:10]),
                   wait=([("sS", s_exp)] if i == 0 else []),
                   inc=([("vS", 1)] if i == 3 else []))
            ST_READY[tiles[0]] = ("vS", c["vS"])

        def ln2_s_pieces(tiles):
            sts = [stat[:, tc] for tc in tiles]

            def sq(i0):
                for i in (i0, i0 + 1):
                    op("scalar", lambda e, st=sts[i], tc=tiles[i], i=i: e.activation(
                        ao_t[:, i, :], x_t[:, tc, :], AF.Square,
                        accum_out=st[:, 1:2]),
                        wait=([("vS", V_wo_evac[NSC - 1])] if i == 0 else []))

            def cp(i0):
                for i in (i0, i0 + 1):
                    op("scalar", lambda e, st=sts[i], tc=tiles[i], i=i: e.activation(
                        ao_t[:, i, :], x_t[:, tc, :], AF.Copy,
                        accum_out=st[:, 0:1]),
                        inc=([("sS", 1)] if i == 3 else []))
                if i0 == 2:
                    LN2_JUNK[0] = c["sS"]

            def mid1():
                for i in range(4):
                    op("scalar", lambda e, st=sts[i]: e.mul(st[:, 4:5], st[:, 0:1], 1.0 / D))
                for i in range(4):
                    op("scalar", lambda e, st=sts[i]: e.mul(st[:, 5:6], st[:, 4:5], st[:, 4:5]))

            def mid2():
                for i in range(4):
                    op("scalar", lambda e, st=sts[i]: e.mul(st[:, 6:7], st[:, 5:6], -1.0))
                for i in range(4):
                    op("scalar", lambda e, st=sts[i]: e.mul(st[:, 11:12], st[:, 1:2], 1.0 / D))

            def mid3():
                for i in range(4):
                    op("scalar", lambda e, st=sts[i]: e.add(st[:, 7:8], st[:, 11:12], st[:, 6:7]))

            def fin1():
                for i in range(4):
                    op("scalar", lambda e, st=sts[i]: e.activation(
                        st[:, 8:9], st[:, 7:8], AF.Ln, bias=epsc_s[:, 0:1]))

            def fin2():
                for i in range(4):
                    op("scalar", lambda e, st=sts[i]: e.activation(
                        st[:, 9:10], st[:, 8:9], AF.Exp, scale=-0.5))
                for i in range(4):
                    op("scalar", lambda e, st=sts[i]: e.mul(
                        st[:, 10:11], st[:, 4:5], st[:, 9:10]),
                       inc=([("sS", 1)] if i == 3 else []))
                ST_READY[tiles[0]] = ("sS", c["sS"])

            return {0: lambda: sq(0), 1: lambda: sq(2), 2: lambda: cp(0),
                    3: lambda: cp(2), 4: mid1, 5: mid2, 6: mid3,
                    7: lambda: (fin1(), fin2())}

        def ln_ts(tiles):
            sts = [stat[:, tc] for tc in tiles]
            for i, tc in enumerate(tiles):
                op("vector", lambda e, st=sts[i], tc=tc, i=i: e.tensor_scalar(
                    h_t[:, i, :], x_t[:, tc, :],
                    st[:, 9:10], st[:, 10:11], ALU.mult, ALU.subtract),
                    wait=([ST_READY[tiles[0]], ("tS", H_T_FREE[0])]
                          if i == 0 else []),
                    inc=([("vS", 1)] if i == 3 else []))
            V_HT[tiles[0]] = c["vS"]

        def ln_tr(tiles, dest_f, dst_cols, ev="scalar"):
            for i, tc in enumerate(tiles):
                trb = tr[i % 2]
                evi = ("scalar" if i % 2 == 0 else "vector") if ev == "mix" else ev
                es = "sS" if evi == "scalar" else "vS"
                for cc in range(8):
                    op("tensor", lambda e, cc=cc, i=i, trb=trb: e.transpose(
                        trb[:, cc * 128:(cc + 1) * 128],
                        h_t[:, i, cc * 128:(cc + 1) * 128], idn_s[:, :]),
                        wait=([("vS", V_HT[tiles[0]]), TR_FREE[i % 2],
                               ("dIN", IN_CONST)] if cc == 0 else []),
                        inc=([("tS", 1)] if cc == 7 else []))
                T_tr[tc] = c["tS"]
                cpf = ((lambda e, o, s: e.copy(o, s)) if evi == "scalar"
                       else (lambda e, o, s: e.tensor_copy(o, s)))
                for cc in range(8):
                    op(evi, lambda e, cc=cc, trb=trb, dc=dst_cols[i]: cpf(
                        e, dest_f[:, cc, dc * 128:(dc + 1) * 128],
                        trb[:, cc * 128:(cc + 1) * 128]),
                        wait=([("tS", T_tr[tc])] if cc == 0 else []),
                        inc=([(es, 1)] if cc == 7 else []))
                HF_EVAC[tc] = (es, c[es])
                TR_FREE[i % 2] = HF_EVAC[tc]
            H_T_FREE[0] = T_tr[tiles[3]]

        def ln_stats_pieces(tiles, xw):
            sts = [stat[:, tc] for tc in tiles]

            def pA():
                for i, tc in enumerate(tiles):
                    op("vector", lambda e, st=sts[i], tc=tc: e.reduce_sum(
                        st[:, 0:1], x_t[:, tc, :], AX.X),
                        wait=(xw if i == 0 else []),
                        inc=([("vS", 1)] if i == 3 else []))
                self_v = c["vS"]
                for i, tc in enumerate(tiles):
                    op("scalar", lambda e, st=sts[i], tc=tc: e.activation(
                        silu_sb[:, 0, :], x_t[:, tc, 0:512], AF.Square,
                        accum_out=st[:, 1:2]),
                        wait=(xw if i == 0 else []))
                    op("scalar", lambda e, st=sts[i], tc=tc: e.activation(
                        silu_sb[:, 0, :], x_t[:, tc, 512:1024], AF.Square,
                        accum_out=st[:, 2:3]))
                VS1[tiles[0]] = self_v

            def pB():
                for i in range(4):
                    op("scalar", lambda e, st=sts[i]: e.mul(st[:, 4:5], st[:, 0:1], 1.0 / D),
                       wait=([("vS", VS1[tiles[0]])] if i == 0 else []))
                for i in range(4):
                    op("scalar", lambda e, st=sts[i]: e.add(st[:, 3:4], st[:, 1:2], st[:, 2:3]))
                for i in range(4):
                    op("scalar", lambda e, st=sts[i]: e.mul(st[:, 5:6], st[:, 4:5], st[:, 4:5]))
                for i in range(4):
                    op("scalar", lambda e, st=sts[i]: e.mul(st[:, 6:7], st[:, 5:6], -1.0))

            def pC():
                for i in range(4):
                    op("scalar", lambda e, st=sts[i]: e.mul(st[:, 11:12], st[:, 3:4], 1.0 / D))
                for i in range(4):
                    op("scalar", lambda e, st=sts[i]: e.add(st[:, 7:8], st[:, 11:12], st[:, 6:7]))
                for i in range(4):
                    op("scalar", lambda e, st=sts[i]: e.activation(
                        st[:, 8:9], st[:, 7:8], AF.Ln, bias=epsc_s[:, 0:1]))
                for i in range(4):
                    op("scalar", lambda e, st=sts[i]: e.activation(
                        st[:, 9:10], st[:, 8:9], AF.Exp, scale=-0.5))

            def pD():
                for i in range(4):
                    op("scalar", lambda e, st=sts[i]: e.mul(
                        st[:, 10:11], st[:, 4:5], st[:, 9:10]),
                        inc=([("sS", 1)] if i == 3 else []))
                ST_READY[tiles[0]] = ("sS", c["sS"])
            return [pA, pB, pC, pD]

        def ln2v_pieces(tiles):
            sts = [stat[:, tc] for tc in tiles]

            def pE():
                for i, tc in enumerate(tiles):
                    op("vector", lambda e, st=sts[i], tc=tc: e.reduce_sum(
                        st[:, 0:1], x_t[:, tc, :], AX.X))
                for i, tc in enumerate(tiles):
                    op("vector", lambda e, st=sts[i], tc=tc, i=i: e.scalar_tensor_tensor(
                        h_t[:, i, :], x_t[:, tc, :], 1.0, x_t[:, tc, :],
                        ALU.mult, ALU.mult, accum_out=st[:, 1:2]),
                        wait=([("tS", H_T_FREE[0])] if i == 0 else []))

            def pF():
                for i in range(4):
                    op("vector", lambda e, st=sts[i]: e.tensor_scalar(
                        st[:, 4:5], st[:, 0:1], 1.0 / D, None, ALU.mult))
                for i in range(4):
                    op("vector", lambda e, st=sts[i]: e.tensor_mul(st[:, 5:6], st[:, 4:5], st[:, 4:5]))
                for i in range(4):
                    op("vector", lambda e, st=sts[i]: e.tensor_scalar(
                        st[:, 11:12], st[:, 1:2], 1.0 / D, None, ALU.mult))
                for i in range(4):
                    op("vector", lambda e, st=sts[i]: e.tensor_sub(
                        st[:, 7:8], st[:, 11:12], st[:, 5:6]),
                        inc=([("vS", 1)] if i == 3 else []))
                LN2_VAR[tiles[0]] = c["vS"]

            def pG():
                v_var = LN2_VAR[tiles[0]]
                for i in range(4):
                    op("scalar", lambda e, st=sts[i]: e.activation(
                        st[:, 8:9], st[:, 7:8], AF.Ln, bias=epsc_s[:, 0:1]),
                        wait=([("vS", v_var)] if i == 0 else []))
                for i in range(4):
                    op("scalar", lambda e, st=sts[i]: e.activation(
                        st[:, 9:10], st[:, 8:9], AF.Exp, scale=-0.5),
                        inc=([("sS", 1)] if i == 3 else []))
                s_exp = c["sS"]
                for i in range(4):
                    op("vector", lambda e, st=sts[i]: e.tensor_mul(
                        st[:, 10:11], st[:, 4:5], st[:, 9:10]),
                        wait=([("sS", s_exp)] if i == 0 else []),
                        inc=([("vS", 1)] if i == 3 else []))
                ST_READY[tiles[0]] = ("vS", c["vS"])
            return [pE, pF, pG]

        NB_FREE = [0] * H
        S_AO = [0] * H
        V_M32ADD = [0] * H
        S_mbf = [0] * H
        T_SCONS = [0] * H
        T_proj = [0] * NSC
        T_wo = [0] * NSC
        V_wo_evac = [0] * NSC
        V_PSUM23_FREE = 0
        T_attn_done = 0
        T_MID_BOX = [0]
        AOF_EVAC = [0] * 4
        WO_EVAC_V = [0] * (NSC * 8)

        PREV_SEVAC = [0] * 4
        V_RECV = [0] * H
        for sc in range(NSC):
            tc0 = sc * 4
            tiles = [tc0 + i for i in range(4)]
            last_sc = (sc == NSC - 1)
            if sc == 0:
                ln_stats(tiles, [("dIN", IN_X01)], xw2=[("dIN", IN_X)])
                ln_ts(tiles)
            ln_tr(tiles, h_f, [0, 1, 2, 3], ev="mix")
            S_HF_W = [HF_EVAC[tc0 + 2], HF_EVAC[tc0 + 3]]
            S_hf = c["sS"]
            if sc == 0:
                tap("d_hf", h_f[:, :, :], [("sS", S_hf)])

            gidx = [0]
            PROJ_EVAC_S = [0] * 40
            pieces = {}
            pi = 6 if sc < NSC - 1 else 4
            step = 4 if sc < NSC - 1 else 3
            if sc + 1 < NSC:
                for p in ln_stats_pieces([tc0 + 4 + i for i in range(4)],
                                         [("dIN", IN_47 if sc == 0 else IN_ALL)]):
                    pieces[pi] = p
                    pi += step
            if sc > 0:
                for p in ln2v_pieces([tc0 - 4 + i for i in range(4)]):
                    pieces[pi] = p
                    pi += step

            def proj_group(mms, evac_fn, hf_wait=None, dw=None):
                g = gidx[0]
                if g in pieces:
                    pieces.pop(g)()
                if sc > 0 and g < 2:
                    bank = pb[2 + g]
                else:
                    bank = pb[g % 2]
                w = (S_HF_W if hf_wait is None else [hf_wait]) + [
                     ("dW", DW_W["q"] if dw is None else dw), ("dIN", IN_CONST)]
                if sc > 0 and g < 2:
                    w.append(("vS", PREV_SEVAC[2 + g]))
                elif sc > 0 and g < 4:
                    w.append(("vS", V_wo_evac[sc - 1]))
                elif g >= 2:
                    w.append(("sS", PROJ_EVAC_S[g - 2]))
                n = len(mms)
                for i, (lf, rf) in enumerate(mms):
                    op("tensor", lambda e, lf=lf, rf=rf, i=i, n=n, bank=bank: e.matmul(
                        bank[:, 0:512], lf(), rf(),
                        start=(i == 0), stop=(i == n - 1)),
                        wait=(w if i == 0 else []),
                        inc=([("tS", 1)] if i == n - 1 else []))
                evac_fn(bank, c["tS"], g)
                PROJ_EVAC_S[g] = c["sS"]
                gidx[0] += 1

            def ev_act(dest_ap, bias_ap):
                def f(bank, td, g):
                    op("scalar", lambda e: e.activation(dest_ap, bank[:, 0:512],
                                                        AF.Exp, bias=bias_ap),
                       wait=[("tS", td)], inc=[("sS", 1)])
                return f

            def ev_exp(dest_ap):
                def f(bank, td, g):
                    op("scalar", lambda e: e.activation(dest_ap, bank[:, 0:512], AF.Exp),
                       wait=[("tS", td)], inc=[("sS", 1)])
                return f

            def ev_v(tcm, hh):
                def f(bank, td, g):
                    dst = v_t[:, tcm].rearrange("p (h c) -> p h c", c=65)[
                        :, hh * 8:(hh + 1) * 8, 0:64]
                    src = bank[:, 0:512].rearrange("p (h c) -> p h c", c=64)
                    op("scalar", lambda e: e.copy(dst, src),
                       wait=[("tS", td), ("gS", MEMSET)], inc=[("sS", 1)])
                return f

            wsel = [(wv_bf, bvr_s, "v")] if last_sc else \
                   [(wv_bf, bvr_s, "v"), (wk_bf, bkr_s, "k")]
            for tcm in range(4):
                for wbf, brow, evf in wsel:
                    for hh in range(2):
                        mms = [(lambda: onesr_s[:, :],
                                lambda hh=hh, brow=brow: brow[:, hh * 512:(hh + 1) * 512])]
                        mms += [(lambda dc=dc, tcm=tcm: h_f[:, dc, tcm * 128:(tcm + 1) * 128],
                                 lambda dc=dc, hh=hh, wbf=wbf: wbf[:, dc, hh * 512:(hh + 1) * 512])
                                for dc in range(8)]
                        if evf == "k":
                            proj_group(mms, ev_exp(k_t[:, tcm, hh * 512:(hh + 1) * 512]),
                                       hf_wait=HF_EVAC[tc0 + tcm], dw=DW_W["k"])
                        else:
                            proj_group(mms, ev_v(tcm, hh),
                                       hf_wait=HF_EVAC[tc0 + tcm], dw=DW_W["v"])
            for wbf, dest, bcol in [(wq_bf, q_f, bqc_s), (wk_bf, k_f, bkc_s)]:
                for oc in range(8):
                    mms = [(lambda dc=dc, wbf=wbf, oc=oc: wbf[:, dc, oc * 128:(oc + 1) * 128],
                            lambda dc=dc: h_f[:, dc, :]) for dc in range(8)]
                    proj_group(mms, ev_act(dest[:, oc, :], bcol[:, oc:oc + 1]))
            for g in sorted(pieces):
                pieces.pop(g)()
            T_proj[sc] = c["tS"]
            PROJ_LAST_S = c["sS"]
            if sc == 0:
                tap("d_qf", q_f[:, :, :], [("sS", PROJ_LAST_S)])
                tap("d_kf", k_f[:, :, :], [("sS", PROJ_LAST_S)])
                tap("d_kt", k_t[:, :, :], [("sS", PROJ_LAST_S)])
                tap("d_vt", v_t[:, :, :], [("sS", PROJ_LAST_S)])


            V_sevac = [[0] * 4 for _ in range(H)]
            T_SCONS_NEW = [0] * H

            def emit_scores(h):
                p64 = 64 * (h % 2)
                hc = h // 2
                for j in range(4):
                    nj = 512 - j * 128
                    sbk = pb[j]
                    w = [("sS", PROJ_LAST_S)]
                    if h >= 1:
                        w.append(("vS", V_sevac[h - 1][j]))
                    elif sc > 0:
                        w.append(("vS", V_wo_evac[sc - 1] if j < 2 else PREV_SEVAC[j]))
                    op("tensor", lambda e, j=j, nj=nj, sbk=sbk, p64=p64, hc=hc: e.matmul(
                        sbk[:, 0:nj],
                        k_f[p64:p64 + 64, hc, j * 128:(j + 1) * 128],
                        q_f[p64:p64 + 64, hc, j * 128:512],
                        start=True, stop=True), wait=w, inc=[("tS", 1)])
                    t_sj = c["tS"]
                    wv = [("tS", t_sj)]
                    if h >= 2:
                        wv.append(("tS", T_SCONS_NEW[h - 2]))
                    elif sc > 0:
                        wv.append(("tS", T_SCONS[H - 2 + h]))
                    op("vector", lambda e, j=j, nj=nj, sbk=sbk, h=h: e.tensor_tensor(
                        s_sb[:, h % 2, SOFF[j]:SOFF[j] + nj],
                        sbk[:, 0:nj], mask_s[:, 0:nj], ALU.mult),
                        wait=wv, inc=[("vS", 1)])
                    V_sevac[h][j] = c["vS"]

            emit_scores(0)
            for h in range(H):
                p64 = 64 * (h % 2)
                hc = h // 2
                nb = pb[4 + (h % 2)]
                if h + 1 < H:
                    emit_scores(h + 1)
                first_w = [("sS", S_AO[h - 2] if h >= 2 else S_AO[H - 2 + h]),
                           ("vS", V_RECV[h - 2] if h >= 2 else V_RECV[H - 2 + h])]
                nmm = 0
                if sc > 0:
                    for tcx in range(4):
                        w = (first_w + [("sS", S_mbf[h])]) if tcx == 0 else []
                        op("tensor", lambda e, tcx=tcx, nb=nb, p64=p64, hc=hc: e.matmul(
                            nb[:, tcx * 65:tcx * 65 + 65],
                            q_f[p64:p64 + 64, hc, tcx * 128:(tcx + 1) * 128],
                            M_bf[p64:p64 + 64, hc, :],
                            start=(tcx == 0), stop=False), wait=w)
                        nmm += 1
                intra_list = [(j, tcx) for j in range(4) for tcx in range(j, 4)]
                for j, tcx in intra_list[:-1]:
                    if tcx == j:
                        w = [("vS", V_sevac[h][j])]
                    else:
                        w = []
                    if nmm == 0:
                        w = first_w + w
                    op("tensor", lambda e, j=j, tcx=tcx, stf=(nmm == 0), nb=nb, h=h: e.matmul(
                        nb[:, tcx * 65:tcx * 65 + 65],
                        s_sb[:, h % 2,
                             SOFF[j] + (tcx - j) * 128:SOFF[j] + (tcx - j) * 128 + 128],
                        v_head(j, h),
                        start=stf, stop=False),
                        wait=w)
                    nmm += 1
                if not last_sc:
                    for j in range(4):
                        op("tensor", lambda e, j=j, nb=nb, p64=p64, h=h: e.matmul(
                            nb[p64:p64 + 64, 260:325],
                            k_t[:, j, h * 64:(h + 1) * 64],
                            v_head(j, h),
                            start=False, stop=False))
                op("tensor", lambda e, nb=nb, h=h: e.matmul(
                    nb[:, 3 * 65:3 * 65 + 65],
                    s_sb[:, h % 2, SOFF[3]:SOFF[3] + 128],
                    v_head(3, h),
                    start=False, stop=True),
                    wait=[("vS", V_sevac[h][3])], inc=[("tS", 1)])
                T_SCONS_NEW[h] = c["tS"]
                t_upd = c["tS"]
                op("vector", lambda e, nb=nb, h=h: e.reciprocal(
                    recip[:, h % 2, :],
                    nb[:, 0:260].rearrange("p (t c) -> p t c", c=65)[:, :, 64:65]
                      .rearrange("p t o -> p (t o)")),
                    wait=[("tS", t_upd),
                          ("sS", S_AO[h - 2] if h >= 2 else S_AO[H - 2 + h])],
                    inc=[("vS", 1)])
                v_recip = c["vS"]
                if not last_sc:
                    op("vector", lambda e, nb=nb, p64=p64, hc=hc: e.tensor_tensor(
                        M32[p64:p64 + 64, hc, :], M32[p64:p64 + 64, hc, :],
                        nb[p64:p64 + 64, 260:325], ALU.add),
                        inc=[("vS", 1)])
                    V_M32ADD[h] = c["vS"]
                V_RECV[h] = c["vS"]
                for tcx in range(4):
                    op("scalar", lambda e, tcx=tcx, nb=nb, h=h: e.activation(
                        ao_t[:, tcx, h * 64:(h + 1) * 64],
                        nb[:, tcx * 65:tcx * 65 + 64],
                        AF.Copy, scale=recip[:, h % 2, tcx:tcx + 1]),
                        wait=([("vS", v_recip)] if tcx == 0 else []),
                        inc=([("sS", 1)] if tcx == 3 else []))
                S_AO[h] = c["sS"]
                if not last_sc:
                    op("scalar", lambda e, p64=p64, hc=hc: e.copy(
                        M_bf[p64:p64 + 64, hc, :], M32[p64:p64 + 64, hc, :]),
                        wait=[("vS", V_M32ADD[h])], inc=[("sS", 1)])
                    S_mbf[h] = c["sS"]
                if sc == NSC - 1 and h == 7:
                    T_MID_BOX[0] = c["tS"]
            PREV_SEVAC = list(V_sevac[H - 1])
            T_SCONS = [0] * (H - 2) + [T_SCONS_NEW[H - 2], T_SCONS_NEW[H - 1]]
            S_AO_DONE = c["sS"]
            T_attn_done = c["tS"]
            if sc + 1 < NSC:
                ln_ts([tc0 + 4 + i for i in range(4)])
            if sc == 0:
                tap("d_aot", ao_t[:, :, :], [("sS", S_AO_DONE)])
                tap("d_m32", M32[:, :, :], [("vS", V_RECV[H - 1])])

            def a4_trs(tcm):
                trb = tr[tcm % 2]
                for cc in range(8):
                    w = [("sS", S_AO_DONE), TR_FREE[tcm % 2]] if cc == 0 else []
                    op("tensor", lambda e, cc=cc, tcm=tcm, trb=trb: e.transpose(
                        trb[:, cc * 128:(cc + 1) * 128],
                        ao_t[:, tcm, cc * 128:(cc + 1) * 128], idn_s[:, :]),
                        wait=w, inc=([("tS", 1)] if cc == 7 else []))
                t_tra = c["tS"]
                for cc in range(8):
                    op("scalar", lambda e, cc=cc, trb=trb,
                       dst=(ao_f if tcm % 2 == 0 else ao_f2): e.copy(
                        dst[:, cc, :], trb[:, cc * 128:(cc + 1) * 128]),
                        wait=([("tS", t_tra)] if cc == 0 else []),
                        inc=([("sS", 1)] if cc == 7 else []))
                AOF_EVAC[tcm] = c["sS"]
                TR_FREE[tcm % 2] = ("sS", AOF_EVAC[tcm])

            def a4_wo(tcm):
                for hh in range(2):
                    bank = pb[hh]
                    gix = sc * 8 + tcm * 2 + hh
                    w0 = [("sS", AOF_EVAC[tcm])]
                    if sc == 0 and gix == 0:
                        w0.append(("dW", DW_W["o"]))
                    if gix >= 2:
                        w0.append(("vS", WO_EVAC_V[gix - 2]))
                    else:
                        w0.append(("vS", V_sevac[H - 1][hh]))
                    op("tensor", lambda e, hh=hh, bank=bank: e.matmul(
                        bank[:, 0:512], onesr_s[:, :],
                        bor_s[:, hh * 512:(hh + 1) * 512],
                        start=True, stop=False), wait=w0)
                    for dc in range(8):
                        op("tensor", lambda e, dc=dc, hh=hh, bank=bank,
                           src=(ao_f if tcm % 2 == 0 else ao_f2): e.matmul(
                            bank[:, 0:512], src[:, dc, :],
                            wo_bf[:, dc, hh * 512:(hh + 1) * 512],
                            start=False, stop=(dc == 7)),
                            inc=([("tS", 1)] if dc == 7 else []))
                    t_wo = c["tS"]
                    op("vector", lambda e, tcm=tcm, hh=hh, bank=bank: e.tensor_tensor(
                        x_t[:, tc0 + tcm, hh * 512:(hh + 1) * 512],
                        x_t[:, tc0 + tcm, hh * 512:(hh + 1) * 512],
                        bank[:, 0:512], ALU.add),
                        wait=[("tS", t_wo)], inc=[("vS", 1)])
                    WO_EVAC_V[gix] = c["vS"]

            a4_trs(0)
            a4_trs(1)
            a4_wo(0)
            a4_trs(2)
            a4_wo(1)
            a4_trs(3)
            a4_wo(2)
            a4_wo(3)
            T_wo[sc] = c["tS"]
            V_wo_evac[sc] = c["vS"]
            if sc == NSC - 1:
                tap("d_xa", x_t[:, :, :], [("vS", V_wo_evac[sc])])


        op("gpsimd", lambda e: e.dma_start(
            gw_bf[:, :, :], gwT_d.ap().rearrange("(c p) e -> p c e", p=128)),
            wait=[("tS", T_proj[NSC - 1])], inc=[("dW", 16)])
        DW_GW = c["dW"]
        DW_GUP = [0] * E
        DW_DWN = [0] * E
        op("gpsimd", lambda e: e.dma_start(
            gup_bf[:, 4:6, :],
            gup_d[0].rearrange("(c p) q -> p c q", p=128)[:, 4:6, :]),
            wait=[("tS", T_MID_BOX[0])], inc=[("dW", 16)])
        op("gpsimd", lambda e: e.dma_start(
            gup_bf[:, 0:4, :],
            gup_d[0].rearrange("(c p) q -> p c q", p=128)[:, 0:4, :]),
            wait=[("tS", T_attn_done)], inc=[("dW", 16)])
        op("gpsimd", lambda e: e.dma_start(
            gup_bf[:, 6:8, :],
            gup_d[0].rearrange("(c p) q -> p c q", p=128)[:, 6:8, :]),
            inc=[("dW", 16)])
        DW_GUP[0] = c["dW"]


        T_tr = [0] * NT
        HF_EVAC = [0] * NT
        S_exp = [0] * 4
        V_z = [0] * (E * 32)
        V_de = [0] * (E * 32)
        V_out_h = [0] * (NT * 2)
        zi_box = [0]
        DW_GUP_CUR = [DW_GUP[0]]
        DW_GUP_CUR_A = [DW_GUP[0]]
        T_GATE_QP3 = [0]

        def emit_gate(e_i, tq, pieces=None):
            for qp in range(8):
                idx = tq * 8 + qp
                zi = zi_box[0]
                bp = (pb[2], pb[3]) if idx % 2 == 0 else (pb[0], pb[1])
                w = [("dW", DW_GUP_CUR_A[0] if qp < 4 else DW_GUP_CUR[0])]
                if e_i == 0:
                    w += [HF_EVAC[tq * 4 + 2], HF_EVAC[tq * 4 + 3]]
                else:
                    w += [("vS", H2F_V_BOX[0]), ("sS", H2F_S_BOX[0])]
                if zi >= 2:
                    w.append(("vS", V_z[zi - 2]))
                if e_i == 0 and idx == 0:
                    w.append(("vS", PREV_SEVAC[3]))
                if e_i == 0 and idx == 1:
                    w.append(("vS", V_wo_evac[NSC - 1]))
                for half, bank in enumerate(bp):
                    qb = qp * 128 + half * 1024
                    for dc in range(8):
                        op("tensor", lambda e, dc=dc, qb=qb, tq=tq, bank=bank: e.matmul(
                            bank[:, 0:512],
                            gup_bf[:, dc, qb:qb + 128],
                            h2_f[:, dc, tq * 512:(tq + 1) * 512],
                            start=(dc == 0), stop=(dc == 7)),
                            wait=(w if (half == 0 and dc == 0) else []),
                            inc=([("tS", 1)] if (half == 1 and dc == 7) else []))
                t_gu = c["tS"]
                if qp == 3:
                    T_GATE_QP3[0] = t_gu
                ws = [("tS", t_gu)]
                if zi >= 2:
                    ws.append(("vS", V_z[zi - 2]))
                op("scalar", lambda e, bp=bp, idx=idx: e.activation(
                    silu_sb[:, idx % 2, :], bp[0][:, 0:512], AF.Silu),
                    wait=ws, inc=[("sS", 1)])
                s_si = c["sS"]
                if pieces and qp in pieces:
                    pieces.pop(qp)()
                op("vector", lambda e, bp=bp, idx=idx, qp=qp, tq=tq: e.tensor_tensor(
                    z_sb[:, qp, tq * 512:(tq + 1) * 512],
                    silu_sb[:, idx % 2, :], bp[1][:, 0:512], ALU.mult),
                    wait=[("sS", s_si)], inc=[("vS", 1)])
                V_z[zi] = c["vS"]
                zi_box[0] = zi + 1

        H2F_DONE_BOX = [0]
        H2F_V_BOX = [0]
        H2F_S_BOX = [0]
        ln_ts([0, 1, 2, 3])
        for g4 in range(4):
            tiles = [g4 * 4 + i for i in range(4)]
            ln_tr(tiles, h2_f, tiles, ev="vector")
            rb = pb[4 + (g4 % 2)]
            w = [HF_EVAC[tiles[2]], HF_EVAC[tiles[3]], ("dW", DW_GW)]
            if g4 >= 2:
                w.append(("sS", S_exp[g4 - 2]))
            else:
                w.append(("sS", S_AO[H - 2 + g4]))
                w.append(("vS", V_RECV[H - 2 + g4]))
            nmm = 0
            for i, tc in enumerate(tiles):
                for dc in range(8):
                    op("tensor", lambda e, dc=dc, tc=tc, rb=rb, i=i, nmm=nmm: e.matmul(
                        rb[:, i * 8:(i + 1) * 8], h2_f[:, dc, tc * 128:(tc + 1) * 128],
                        gw_bf[:, dc, :], start=(nmm == 0), stop=(nmm == 31)),
                        wait=(w if nmm == 0 else []),
                        inc=([("tS", 1)] if nmm == 31 else []))
                    nmm += 1
            t_rout = c["tS"]
            for i in range(4):
                op("vector", lambda e, i=i, rb=rb, rs=rstat[:, g4 * 4 + i]: e.reduce_max(
                    rs[:, 0:1], rb[:, i * 8:(i + 1) * 8], AX.X, negate=True),
                    wait=([("tS", t_rout)] if i == 0 else []),
                    inc=([("vS", 1)] if i == 3 else []))
            v_nmx = c["vS"]
            for i in range(4):
                op("scalar", lambda e, i=i, rb=rb, rs=rstat[:, g4 * 4 + i]: e.activation(
                    rt_e[:, i, :], rb[:, i * 8:(i + 1) * 8], AF.Exp, bias=rs[:, 0:1]),
                    wait=([("vS", v_nmx)] if i == 0 else []),
                    inc=([("sS", 1)] if i == 3 else []))
            S_exp[g4] = c["sS"]

            def R(fn, wait_first=None, inc_last=False):
                for i in range(4):
                    op("vector", lambda e, i=i, rs=rstat[:, g4 * 4 + i],
                       ee=rt_e[:, i], mk=msk1[:, i], tp=tmp8[:, i]: fn(e, i, rs, ee, mk, tp),
                       wait=(wait_first if (i == 0 and wait_first) else []),
                       inc=([("vS", 1)] if (i == 3 and inc_last) else []))

            R(lambda e, i, rs, ee, mk, tp: e.reduce_sum(rs[:, 1:2], ee, AX.X),
              wait_first=[("sS", S_exp[g4])])
            R(lambda e, i, rs, ee, mk, tp: e.reciprocal(rs[:, 2:3], rs[:, 1:2]))
            R(lambda e, i, rs, ee, mk, tp: e.tensor_scalar(
                ee, ee, rs[:, 2:3], None, ALU.mult))
            R(lambda e, i, rs, ee, mk, tp: e.reduce_max(rs[:, 3:4], ee, AX.X))
            R(lambda e, i, rs, ee, mk, tp: e.tensor_scalar(
                mk, ee, rs[:, 3:4], None, ALU.is_equal))
            R(lambda e, i, rs, ee, mk, tp: e.tensor_mul(tp, ee, mk))
            R(lambda e, i, rs, ee, mk, tp: e.tensor_sub(tp, ee, tp))
            R(lambda e, i, rs, ee, mk, tp: e.reduce_max(rs[:, 4:5], tp, AX.X))
            R(lambda e, i, rs, ee, mk, tp: e.tensor_scalar(
                tp, tp, rs[:, 4:5], None, ALU.is_equal))
            R(lambda e, i, rs, ee, mk, tp: e.tensor_add(mk, mk, tp))
            R(lambda e, i, rs, ee, mk, tp: e.scalar_tensor_tensor(
                rs[:, 5:6], rs[:, 3:4], 1e-9, rs[:, 4:5], ALU.add, ALU.add))
            R(lambda e, i, rs, ee, mk, tp: e.reciprocal(rs[:, 6:7], rs[:, 5:6]))
            R(lambda e, i, rs, ee, mk, tp: e.tensor_mul(ee, ee, mk))
            R(lambda e, i, rs, ee, mk, tp, g4=g4: e.tensor_scalar(
                routed[:, g4 * 4 + i, :], ee, rs[:, 6:7], None, ALU.mult),
              inc_last=True)
            if g4 + 1 < 4:
                ln_ts([g4 * 4 + 4 + i for i in range(4)])
            emit_gate(0, g4, pieces=(
                ln2_s_pieces([(NSC - 1) * 4 + i for i in range(4)])
                if g4 == 0 else None))
            if g4 == 0:
                op("gpsimd", lambda e: e.dma_start(
                    dwn_bf[:, :, :], dwn_d[0].rearrange("(c p) q -> p c q", p=128)),
                    wait=[("tS", T_wo[NSC - 1]), ("sS", LN2_JUNK[0])],
                    inc=[("dW", 16)])
                DW_DWN[0] = c["dW"]
        H2F_DONE = c["sS"]
        H2F_V = c["vS"]
        H2F_DONE_BOX[0] = H2F_DONE
        H2F_V_BOX[0] = H2F_V
        H2F_S_BOX[0] = H2F_DONE
        V_ROUTED_ALL = c["vS"]
        tap("d_h2f", h2_f[:, :, :], [("sS", H2F_DONE)])
        tap("d_routed", routed[:, :, :], [("vS", V_ROUTED_ALL)])

        di = 0
        for e_i in range(E):
            if e_i > 0:
                for tq in range(4):
                    emit_gate(e_i, tq)
            T_gu_last = c["tS"]
            V_z_all = c["vS"]
            if e_i == 0:
                tap("d_z0", z_sb[:, :, :], [("vS", V_z_all)])
            if e_i + 1 < E:
                t_qp3 = T_GATE_QP3[0]
                for j, (lo, hi) in enumerate([(0, 512), (1024, 1536),
                                              (512, 1024), (1536, 2048)]):
                    op("gpsimd", lambda e, e_i=e_i, lo=lo, hi=hi: e.dma_start(
                        gup_bf[:, :, lo:hi],
                        gup_d[e_i + 1].rearrange(
                            "(c p) q -> p c q", p=128)[:, :, lo:hi]),
                        wait=([("tS", t_qp3)] if j == 0 else
                              ([("tS", T_gu_last)] if j == 2 else [])),
                        inc=[("dW", 16)])
                    if j == 1:
                        DW_GUP_A = c["dW"]
                DW_GUP[e_i + 1] = c["dW"]
                DW_GUP_CUR[0] = c["dW"]
                DW_GUP_CUR_A[0] = DW_GUP_A
            for tcm in range(NT):
                for hh in range(2):
                    idx = tcm * 2 + hh
                    bank = pb[4 + (idx % 2)]
                    w = [("dW", DW_DWN[e_i]),
                         ("vS", V_z[e_i * 32 + (tcm // 4) * 8 + 7])]
                    if di >= 2:
                        w.append(("vS", V_de[di - 2]))
                    if e_i == 0:
                        w.append(("vS", V_ROUTED_ALL))
                        if idx < 2:
                            w.append(("sS", S_exp[2 + idx]))
                    for qc in range(8):
                        op("tensor", lambda e, qc=qc, tcm=tcm, hh=hh, bank=bank: e.matmul(
                            bank[:, 0:512],
                            z_sb[:, qc, tcm * 128:(tcm + 1) * 128],
                            dwn_bf[:, qc, hh * 512:(hh + 1) * 512],
                            start=(qc == 0), stop=(qc == 7)),
                            wait=(w if qc == 0 else []),
                            inc=([("tS", 1)] if qc == 7 else []))
                    t_d = c["tS"]
                    op("vector",
                       lambda e, tcm=tcm, hh=hh, bank=bank, e_i=e_i: e.scalar_tensor_tensor(
                           x_t[:, tcm, hh * 512:(hh + 1) * 512],
                           bank[:, 0:512], routed[:, tcm, e_i:e_i + 1],
                           x_t[:, tcm, hh * 512:(hh + 1) * 512],
                           ALU.mult, ALU.add),
                       wait=[("tS", t_d)], inc=[("vS", 1)])
                    V_de[di] = c["vS"]
                    di += 1
                    if e_i == E - 1:
                        V_out_h[tcm * 2 + hh] = c["vS"]
            T_dwn_last = c["tS"]
            if e_i + 1 < E:
                op("gpsimd", lambda e, e_i=e_i: e.dma_start(
                    dwn_bf[:, :, :],
                    dwn_d[e_i + 1].rearrange("(c p) q -> p c q", p=128)),
                    wait=[("tS", T_dwn_last)], inc=[("dW", 16)])
                DW_DWN[e_i + 1] = c["dW"]

        for tcm in range(NT):
            for hh in range(2):
                op("sync" if hh == 0 else "scalar",
                   lambda e, tcm=tcm, hh=hh: e.dma_start(
                    y_d[tcm * 128:(tcm + 1) * 128, hh * 512:(hh + 1) * 512],
                    x_t[:, tcm, hh * 512:(hh + 1) * 512]),
                   wait=[("vS", V_out_h[tcm * 2 + hh])], inc=[("dOUT", 16)])
        if which == "sync":
            eng.wait_ge(sems["dOUT"], c["dOUT"])

    with nc.Block() as block:
        @block.sync
        def _(e):
            walk("sync", e)

        @block.gpsimd
        def _(e):
            walk("gpsimd", e)

        @block.tensor
        def _(e):
            walk("tensor", e)

        @block.vector
        def _(e):
            walk("vector", e)

        @block.scalar
        def _(e):
            walk("scalar", e)

    return nc


_NC_CACHE = [None]


def kernel(**inputs):
    x = np.ascontiguousarray(np.asarray(inputs["L0_attn_x"], dtype=np.float32))
    wqT = np.ascontiguousarray(np.asarray(inputs["L0_attn_Wq"], dtype=np.float32).T)
    wkT = np.ascontiguousarray(np.asarray(inputs["L0_attn_Wk"], dtype=np.float32).T)
    wvT = np.ascontiguousarray(np.asarray(inputs["L0_attn_Wv"], dtype=np.float32).T)
    woT = np.ascontiguousarray(np.asarray(inputs["L0_attn_Wo"], dtype=np.float32).T)
    bq = np.asarray(inputs["L0_attn_bq"], dtype=np.float32)
    bk = np.asarray(inputs["L0_attn_bk"], dtype=np.float32)
    bv = np.asarray(inputs["L0_attn_bv"], dtype=np.float32)
    bo = np.asarray(inputs["L0_attn_bo"], dtype=np.float32)
    gw = np.asarray(inputs["L0_ffn_gate_w"], dtype=np.float32)
    gup = np.ascontiguousarray(np.asarray(inputs["L0_ffn_gate_up_proj"], dtype=np.float32))
    dwn = np.ascontiguousarray(np.asarray(inputs["L0_ffn_down_proj"], dtype=np.float32))

    common = dict(
        wqT=wqT, wkT=wkT, wvT=wvT, woT=woT,
        bq_col=np.ascontiguousarray(bq.reshape(8, 128).T),
        bk_col=np.ascontiguousarray(bk.reshape(8, 128).T),
        bk_row=np.ascontiguousarray(bk.reshape(1, D)),
        bv_row=np.ascontiguousarray(bv.reshape(1, D)),
        bo_row=np.ascontiguousarray(bo.reshape(1, D)),
        gwT=np.ascontiguousarray(gw.T), gup=gup, dwn=dwn,
        mask=np.concatenate([np.triu(np.ones((128, 128), np.float32)),
                             np.ones((128, 384), np.float32)], axis=1),
        idn=np.eye(128, dtype=np.float32).astype(ml_dtypes.bfloat16),
        ones_row=np.ones((1, 128), np.float32).astype(ml_dtypes.bfloat16),
        epsc=np.full((128, 1), 1e-5, np.float32),
    )
    if _NC_CACHE[0] is None:
        _NC_CACHE[0] = build_nc()
    nc = _NC_CACHE[0]
    in_maps = [dict(x=np.ascontiguousarray(x[b]), **common) for b in range(B)]
    res = run_bass_kernel_spmd(nc, in_maps, core_ids=list(range(B)))
    return np.stack([r["y"] for r in res.results], axis=0)


# version 47
# speedup vs baseline: 1.0108x; 1.0055x over previous
import numpy as np
import ml_dtypes
import concourse.bass as bass
import concourse.mybir as mybir
from concourse.bass_utils import run_bass_kernel_spmd

F32 = mybir.dt.float32
BF16 = mybir.dt.bfloat16
U8 = mybir.dt.uint8
AF = mybir.ActivationFunctionType
ALU = mybir.AluOpType
AX = mybir.AxisListType

B = 8
S = 2048
D = 1024
NT = 16
NSC = 4
H = 16
E = 8
SOFF = [0, 512, 896, 1152]


def build_nc(dbg=False):
    nc = bass.Bass("TRN2", target_bir_lowering=False,
                   detect_race_conditions=False, num_swdge_queues=1)

    x_d = nc.dram_tensor("x", [S, D], F32, kind="ExternalInput")
    wqT_d = nc.dram_tensor("wqT", [D, D], F32, kind="ExternalInput")
    wkT_d = nc.dram_tensor("wkT", [D, D], F32, kind="ExternalInput")
    wvT_d = nc.dram_tensor("wvT", [D, D], F32, kind="ExternalInput")
    woT_d = nc.dram_tensor("woT", [D, D], F32, kind="ExternalInput")
    bqc_d = nc.dram_tensor("bq_col", [128, 8], F32, kind="ExternalInput")
    bkc_d = nc.dram_tensor("bk_col", [128, 8], F32, kind="ExternalInput")
    bkr_d = nc.dram_tensor("bk_row", [1, D], F32, kind="ExternalInput")
    bvr_d = nc.dram_tensor("bv_row", [1, D], F32, kind="ExternalInput")
    bor_d = nc.dram_tensor("bo_row", [1, D], F32, kind="ExternalInput")
    gwT_d = nc.dram_tensor("gwT", [D, E], F32, kind="ExternalInput")
    gup_d = nc.dram_tensor("gup", [E, D, 2 * D], F32, kind="ExternalInput")
    dwn_d = nc.dram_tensor("dwn", [E, D, D], F32, kind="ExternalInput")
    mask_d = nc.dram_tensor("mask", [128, 512], F32, kind="ExternalInput")
    idn_d = nc.dram_tensor("idn", [128, 128], BF16, kind="ExternalInput")
    onesr_d = nc.dram_tensor("ones_row", [1, 128], BF16, kind="ExternalInput")
    epsc_d = nc.dram_tensor("epsc", [128, 1], F32, kind="ExternalInput")
    y_d = nc.dram_tensor("y", [S, D], F32, kind="ExternalOutput")

    x_t = nc.alloc_sbuf_tensor("x_t", [128, NT, D], F32)
    h_t = nc.alloc_sbuf_tensor("h_t", [128, 4, D], BF16)
    stat = nc.alloc_sbuf_tensor("stat", [128, NT, 12], F32)
    rstat = nc.alloc_sbuf_tensor("rstat", [128, 16, 8], F32)
    recip = nc.alloc_sbuf_tensor("recip", [128, 2, 4], F32)
    rt_e = nc.alloc_sbuf_tensor("rt_e", [128, 4, 8], F32)
    msk1 = nc.alloc_sbuf_tensor("msk1", [128, 4, 8], F32)
    tmp8 = nc.alloc_sbuf_tensor("tmp8", [128, 4, 8], F32)
    silu_sb = nc.alloc_sbuf_tensor("silu_sb", [128, 2, 512], BF16)
    mask_s = nc.alloc_sbuf_tensor("mask_s", [128, 512], F32)
    idn_s = nc.alloc_sbuf_tensor("idn_s", [128, 128], BF16)
    onesr_s = nc.alloc_sbuf_tensor("onesr_s", [1, 128], BF16)
    bqc_s = nc.alloc_sbuf_tensor("bqc_s", [128, 8], F32)
    bkc_s = nc.alloc_sbuf_tensor("bkc_s", [128, 8], F32)
    bkr_s = nc.alloc_sbuf_tensor("bkr_s", [1, D], BF16)
    bvr_s = nc.alloc_sbuf_tensor("bvr_s", [1, D], BF16)
    bor_s = nc.alloc_sbuf_tensor("bor_s", [1, D], BF16)
    epsc_s = nc.alloc_sbuf_tensor("epsc_s", [128, 1], F32)
    routed = nc.alloc_sbuf_tensor("routed", [128, NT, E], F32)
    gw_bf = nc.alloc_sbuf_tensor("gw_bf", [128, 8, E], BF16)

    def al32(v):
        return (v + 31) & ~31

    a_specs = [
        ("q_f", [128, 8, 512], BF16), ("k_f", [128, 8, 512], BF16),
        ("k_t", [128, 4, D], BF16), ("v_t", [128, 4, H * 65], BF16),
        ("s_sb", [128, 2, 1280], BF16), ("ao_t", [128, 4, D], BF16),
        ("ao_f", [128, 8, 128], BF16), ("M32", [128, 8, 65], F32),
        ("M_bf", [128, 8, 65], BF16), ("h_f", [128, 8, 512], BF16),
        ("wq_bf", [128, 8, D], BF16), ("wk_bf", [128, 8, D], BF16),
        ("wv_bf", [128, 8, D], BF16), ("wo_bf", [128, 8, D], BF16),
    ]
    offs, cur = {}, 0
    for name, shp, dt in a_specs:
        offs[name] = cur
        cur = al32(cur + int(np.prod(shp[1:])) * mybir.dt.size(dt))
    pool_bytes = cur
    b_specs = [
        ("gup_bf", [128, 8, 2 * D], BF16, offs["q_f"]),
        ("dwn_bf", [128, 8, D], BF16, offs["s_sb"]),
        ("z_sb", [128, 8, 2 * D], BF16, offs["wq_bf"]),
        ("h2_f", [128, 8, S], BF16, offs["wv_bf"]),
    ]
    assert offs["s_sb"] - offs["q_f"] >= 8 * 2 * D * 2
    assert offs["h_f"] - offs["s_sb"] >= 8 * D * 2
    assert offs["wv_bf"] - offs["wq_bf"] >= 8 * 2 * D * 2
    assert pool_bytes - offs["wv_bf"] >= 8 * S * 2
    pool = nc.alloc_sbuf_tensor("pool", [128, pool_bytes], U8)
    base = nc.lookup_mloc(pool).addr
    TT = {}
    for name, shp, dt in a_specs:
        TT[name] = nc.alloc_sbuf_tensor_at(name, shp, dt, offset=base + offs[name])
    for name, shp, dt, off in b_specs:
        TT[name] = nc.alloc_sbuf_tensor_at(name, shp, dt, offset=base + off)
    TT["ao_f2"] = nc.alloc_sbuf_tensor_at("ao_f2", [128, 8, 128], BF16,
                                          offset=base + offs["h_f"])
    q_f, k_f, k_t, v_t = TT["q_f"], TT["k_f"], TT["k_t"], TT["v_t"]
    s_sb, ao_t, ao_f = TT["s_sb"], TT["ao_t"], TT["ao_f"]
    ao_f2 = TT["ao_f2"]
    M32, M_bf, h_f = TT["M32"], TT["M_bf"], TT["h_f"]
    wq_bf, wk_bf, wv_bf, wo_bf = TT["wq_bf"], TT["wk_bf"], TT["wv_bf"], TT["wo_bf"]
    gup_bf, dwn_bf, z_sb, h2_f = TT["gup_bf"], TT["dwn_bf"], TT["z_sb"], TT["h2_f"]

    pb = [nc.alloc_psum_tensor(f"pb{i}", [128, 512], F32) for i in range(6)]
    tr = [nc.alloc_psum_tensor(f"tr{i}", [128, 1024], BF16) for i in range(2)]

    sems = {n: nc.alloc_semaphore(name=n)
            for n in ["dIN", "dW", "dOUT", "tS", "vS", "sS", "gS"]}

    dbg_d = {}
    if dbg:
        for name, shp, dt in [
            ("d_hf", [128, 8, 512], BF16), ("d_qf", [128, 8, 512], BF16),
            ("d_kf", [128, 8, 512], BF16), ("d_kt", [128, 4, D], BF16),
            ("d_vt", [128, 4, H * 65], BF16), ("d_aot", [128, 4, D], BF16),
            ("d_m32", [128, 8, 65], F32), ("d_xa", [128, NT, D], F32),
            ("d_h2f", [128, 8, S], BF16), ("d_routed", [128, NT, E], F32),
            ("d_z0", [128, 8, 2 * D], BF16),
        ]:
            dbg_d[name] = nc.dram_tensor(name, shp, dt, kind="ExternalOutput")

    def v_head(tc, h):
        return v_t[:, tc].rearrange("p (h c) -> p h c", c=65)[:, h, :]

    def walk(which, eng):
        c = {"dIN": 0, "dW": 0, "dOUT": 0, "tS": 0, "vS": 0, "sS": 0, "gS": 0}
        last_w = {}

        def op(engine, fn, inc=(), wait=()):
            if which == engine:
                for sn, v in wait:
                    if v > 0 and v > last_w.get(sn, 0):
                        eng.wait_ge(sems[sn], v)
                        last_w[sn] = v
                ins = fn(eng)
                for sn, n in inc:
                    ins.then_inc(sems[sn], n)
            for sn, n in inc:
                c[sn] += n

        def tap(name, src_ap, wait):
            if not dbg:
                return
            op("sync", lambda e: e.dma_start(dbg_d[name].ap(), src_ap),
               wait=wait, inc=[("dOUT", 16)])
            tgt = c["dOUT"]
            if which in ("tensor", "vector", "scalar", "gpsimd"):
                eng.wait_ge(sems["dOUT"], tgt)

        op("vector", lambda e: e.memset(stat[:, 0, 11:12], 1.0), inc=[("vS", 1)])
        V_WARM = c["vS"]
        op("scalar", lambda e: e.activation(
            stat[:, 0, 10:11], stat[:, 0, 11:12], AF.Square),
            wait=[("vS", V_WARM)])
        op("scalar", lambda e: e.activation(
            stat[:, 0, 10:11], stat[:, 0, 11:12], AF.Ln,
            bias=stat[:, 0, 11:12]))
        op("scalar", lambda e: e.activation(
            stat[:, 0, 10:11], stat[:, 0, 11:12], AF.Exp, scale=-0.5))

        op("sync", lambda e: e.dma_start(
            x_t[:, 0:2, :],
            x_d.ap().rearrange("(c p) d -> p c d", p=128)[:, 0:2, :]),
            inc=[("dIN", 16)])
        IN_X01 = c["dIN"]
        op("sync", lambda e: e.dma_start(
            x_t[:, 2:4, :],
            x_d.ap().rearrange("(c p) d -> p c d", p=128)[:, 2:4, :]),
            inc=[("dIN", 16)])
        IN_X = c["dIN"]
        for dst, src in [(mask_s[:, :], mask_d[:, :]), (idn_s[:, :], idn_d[:, :]),
                         (onesr_s[:, :], onesr_d[:, :]), (bqc_s[:, :], bqc_d[:, :]),
                         (bkc_s[:, :], bkc_d[:, :]), (epsc_s[:, :], epsc_d[:, :])]:
            op("sync", lambda e, d=dst, s=src: e.dma_start(d, s), inc=[("dIN", 16)])
        IN_CONST = c["dIN"]
        op("sync", lambda e: e.dma_start(
            x_t[:, 4:8, :],
            x_d.ap().rearrange("(c p) d -> p c d", p=128)[:, 4:8, :]),
            inc=[("dIN", 16)])
        IN_47 = c["dIN"]
        IN_ALL_BOX = [0]

        op("gpsimd", lambda e: e.memset(M32[:, :, :], 0.0), inc=[("gS", 1)])
        op("gpsimd", lambda e: e.memset(
            v_t[:, :, :].rearrange("p t (h c) -> p t h c", c=65)[:, :, :, 64:65], 1.0),
            inc=[("gS", 1)])
        MEMSET = c["gS"]
        for dst, src in [(bkr_s, bkr_d), (bvr_s, bvr_d), (bor_s, bor_d)]:
            op("gpsimd", lambda e, a=dst, b=src: e.dma_start(a[:, :], b[:, :]),
               inc=[("dW", 16)])
        DW_W = {}
        for nm, wbf, wd in [("v", wv_bf, wvT_d), ("k", wk_bf, wkT_d),
                            ("q", wq_bf, wqT_d), ("o", wo_bf, woT_d)]:
            for ch in range(2):
                op("gpsimd", lambda e, a=wbf, b=wd, ch=ch: e.dma_start(
                    a[:, ch * 4:(ch + 1) * 4, :],
                    b.ap().rearrange("(c p) n -> p c n", p=128)[
                        :, ch * 4:(ch + 1) * 4, :]),
                    wait=([("dIN", IN_X01)] if (nm == "v" and ch == 0) else []),
                    inc=[("dW", 16)])
            DW_W[nm] = c["dW"]
        op("sync", lambda e: e.dma_start(
            x_t[:, 8:16, :],
            x_d.ap().rearrange("(c p) d -> p c d", p=128)[:, 8:16, :]),
            wait=[("dW", DW_W["k"])], inc=[("dIN", 16)])
        IN_ALL = c["dIN"]
        IN_ALL_BOX[0] = IN_ALL

        TR_FREE = [("sS", 0), ("sS", 0)]
        T_tr = [0] * NT
        HF_EVAC = [0] * NT
        ST_READY = {}
        V_HT = {}
        H_T_FREE = [0]
        LN2_JUNK = [0]
        LN2_VAR = {}
        VS1 = {}

        def ln_stats(tiles, xw):
            sts = [stat[:, tc] for tc in tiles]
            for i, tc in enumerate(tiles):
                op("vector", lambda e, st=sts[i], tc=tc: e.reduce_sum(
                    st[:, 0:1], x_t[:, tc, :], AX.X),
                    wait=(xw if i == 0 else []),
                    inc=([("vS", 1)] if i == 3 else []))
            v_s1 = c["vS"]
            for i, tc in enumerate(tiles):
                op("scalar", lambda e, st=sts[i], tc=tc: e.activation(
                    silu_sb[:, 0, :], x_t[:, tc, 0:512], AF.Square,
                    accum_out=st[:, 1:2]),
                    wait=(xw if i == 0 else []))
                op("scalar", lambda e, st=sts[i], tc=tc: e.activation(
                    silu_sb[:, 0, :], x_t[:, tc, 512:1024], AF.Square,
                    accum_out=st[:, 2:3]))
            for i in range(4):
                op("scalar", lambda e, st=sts[i]: e.mul(st[:, 4:5], st[:, 0:1], 1.0 / D),
                   wait=([("vS", v_s1)] if i == 0 else []))
            for i in range(4):
                op("scalar", lambda e, st=sts[i]: e.add(st[:, 3:4], st[:, 1:2], st[:, 2:3]))
            for i in range(4):
                op("scalar", lambda e, st=sts[i]: e.mul(st[:, 5:6], st[:, 4:5], st[:, 4:5]))
            for i in range(4):
                op("scalar", lambda e, st=sts[i]: e.mul(st[:, 6:7], st[:, 5:6], -1.0))
            for i in range(4):
                op("scalar", lambda e, st=sts[i]: e.mul(st[:, 11:12], st[:, 3:4], 1.0 / D))
            for i in range(4):
                op("scalar", lambda e, st=sts[i]: e.add(st[:, 7:8], st[:, 11:12], st[:, 6:7]))
            for i in range(4):
                op("scalar", lambda e, st=sts[i]: e.activation(
                    st[:, 8:9], st[:, 7:8], AF.Ln, bias=epsc_s[:, 0:1]))
            for i in range(4):
                op("scalar", lambda e, st=sts[i]: e.activation(
                    st[:, 9:10], st[:, 8:9], AF.Exp, scale=-0.5))
            for i in range(4):
                op("scalar", lambda e, st=sts[i]: e.mul(st[:, 10:11], st[:, 4:5], st[:, 9:10]),
                   inc=([("sS", 1)] if i == 3 else []))
            ST_READY[tiles[0]] = ("sS", c["sS"])

        def ln2_stats_v(tiles, sc):
            sts = [stat[:, tc] for tc in tiles]
            for i, tc in enumerate(tiles):
                op("vector", lambda e, st=sts[i], tc=tc: e.reduce_sum(
                    st[:, 0:1], x_t[:, tc, :], AX.X))
            for i, tc in enumerate(tiles):
                op("vector", lambda e, st=sts[i], tc=tc, i=i: e.scalar_tensor_tensor(
                    ao_t[:, i, :], x_t[:, tc, :], 1.0, x_t[:, tc, :],
                    ALU.mult, ALU.mult, accum_out=st[:, 1:2]),
                    wait=([("tS", H_T_FREE[0])] if i == 0 else []),
                    inc=([("vS", 1)] if i == 3 else []))
            LN2_JUNK[0] = c["vS"]
            for i in range(4):
                op("vector", lambda e, st=sts[i]: e.tensor_scalar(
                    st[:, 4:5], st[:, 0:1], 1.0 / D, None, ALU.mult))
            for i in range(4):
                op("vector", lambda e, st=sts[i]: e.tensor_mul(st[:, 5:6], st[:, 4:5], st[:, 4:5]))
            for i in range(4):
                op("vector", lambda e, st=sts[i]: e.tensor_scalar(
                    st[:, 11:12], st[:, 1:2], 1.0 / D, None, ALU.mult))
            for i in range(4):
                op("vector", lambda e, st=sts[i]: e.tensor_sub(st[:, 7:8], st[:, 11:12], st[:, 5:6]),
                   inc=([("vS", 1)] if i == 3 else []))
            LN2_VAR[tiles[0]] = c["vS"]

        def ln2_stats_fin(tiles):
            sts = [stat[:, tc] for tc in tiles]
            v_var = LN2_VAR[tiles[0]]
            for i in range(4):
                op("scalar", lambda e, st=sts[i]: e.activation(
                    st[:, 8:9], st[:, 7:8], AF.Ln, bias=epsc_s[:, 0:1]),
                    wait=([("vS", v_var)] if i == 0 else []))
            for i in range(4):
                op("scalar", lambda e, st=sts[i]: e.activation(
                    st[:, 9:10], st[:, 8:9], AF.Exp, scale=-0.5),
                    inc=([("sS", 1)] if i == 3 else []))
            s_exp = c["sS"]
            for i in range(4):
                op("vector", lambda e, st=sts[i]: e.tensor_mul(st[:, 10:11], st[:, 4:5], st[:, 9:10]),
                   wait=([("sS", s_exp)] if i == 0 else []),
                   inc=([("vS", 1)] if i == 3 else []))
            ST_READY[tiles[0]] = ("vS", c["vS"])

        def ln2_s_pieces(tiles):
            sts = [stat[:, tc] for tc in tiles]

            def sq(i0):
                for i in (i0, i0 + 1):
                    op("scalar", lambda e, st=sts[i], tc=tiles[i], i=i: e.activation(
                        ao_t[:, i, :], x_t[:, tc, :], AF.Square,
                        accum_out=st[:, 1:2]),
                        wait=([("vS", V_wo_evac[NSC - 1])] if i == 0 else []))

            def cp(i0):
                for i in (i0, i0 + 1):
                    op("scalar", lambda e, st=sts[i], tc=tiles[i], i=i: e.activation(
                        ao_t[:, i, :], x_t[:, tc, :], AF.Copy,
                        accum_out=st[:, 0:1]),
                        inc=([("sS", 1)] if i == 3 else []))
                if i0 == 2:
                    LN2_JUNK[0] = c["sS"]

            def mid1():
                for i in range(4):
                    op("scalar", lambda e, st=sts[i]: e.mul(st[:, 4:5], st[:, 0:1], 1.0 / D))
                for i in range(4):
                    op("scalar", lambda e, st=sts[i]: e.mul(st[:, 5:6], st[:, 4:5], st[:, 4:5]))

            def mid2():
                for i in range(4):
                    op("scalar", lambda e, st=sts[i]: e.mul(st[:, 6:7], st[:, 5:6], -1.0))
                for i in range(4):
                    op("scalar", lambda e, st=sts[i]: e.mul(st[:, 11:12], st[:, 1:2], 1.0 / D))

            def mid3():
                for i in range(4):
                    op("scalar", lambda e, st=sts[i]: e.add(st[:, 7:8], st[:, 11:12], st[:, 6:7]))

            def fin1():
                for i in range(4):
                    op("scalar", lambda e, st=sts[i]: e.activation(
                        st[:, 8:9], st[:, 7:8], AF.Ln, bias=epsc_s[:, 0:1]))

            def fin2():
                for i in range(4):
                    op("scalar", lambda e, st=sts[i]: e.activation(
                        st[:, 9:10], st[:, 8:9], AF.Exp, scale=-0.5))
                for i in range(4):
                    op("scalar", lambda e, st=sts[i]: e.mul(
                        st[:, 10:11], st[:, 4:5], st[:, 9:10]),
                       inc=([("sS", 1)] if i == 3 else []))
                ST_READY[tiles[0]] = ("sS", c["sS"])

            return {0: lambda: sq(0), 1: lambda: sq(2), 2: lambda: cp(0),
                    3: lambda: cp(2), 4: mid1, 5: mid2, 6: mid3,
                    7: lambda: (fin1(), fin2())}

        def ln_ts(tiles):
            sts = [stat[:, tc] for tc in tiles]
            for i, tc in enumerate(tiles):
                op("vector", lambda e, st=sts[i], tc=tc, i=i: e.tensor_scalar(
                    h_t[:, i, :], x_t[:, tc, :],
                    st[:, 9:10], st[:, 10:11], ALU.mult, ALU.subtract),
                    wait=([ST_READY[tiles[0]], ("tS", H_T_FREE[0])]
                          if i == 0 else []),
                    inc=([("vS", 1)] if i == 3 else []))
            V_HT[tiles[0]] = c["vS"]

        def ln_tr(tiles, dest_f, dst_cols, ev="scalar"):
            for i, tc in enumerate(tiles):
                trb = tr[i % 2]
                evi = ("scalar" if i % 2 == 0 else "vector") if ev == "mix" else ev
                es = "sS" if evi == "scalar" else "vS"
                for cc in range(8):
                    op("tensor", lambda e, cc=cc, i=i, trb=trb: e.transpose(
                        trb[:, cc * 128:(cc + 1) * 128],
                        h_t[:, i, cc * 128:(cc + 1) * 128], idn_s[:, :]),
                        wait=([("vS", V_HT[tiles[0]]), TR_FREE[i % 2],
                               ("dIN", IN_CONST)] if cc == 0 else []),
                        inc=([("tS", 1)] if cc == 7 else []))
                T_tr[tc] = c["tS"]
                cpf = ((lambda e, o, s: e.copy(o, s)) if evi == "scalar"
                       else (lambda e, o, s: e.tensor_copy(o, s)))
                for cc in range(8):
                    op(evi, lambda e, cc=cc, trb=trb, dc=dst_cols[i]: cpf(
                        e, dest_f[:, cc, dc * 128:(dc + 1) * 128],
                        trb[:, cc * 128:(cc + 1) * 128]),
                        wait=([("tS", T_tr[tc])] if cc == 0 else []),
                        inc=([(es, 1)] if cc == 7 else []))
                HF_EVAC[tc] = (es, c[es])
                TR_FREE[i % 2] = HF_EVAC[tc]
            H_T_FREE[0] = T_tr[tiles[3]]

        def ln_stats_pieces(tiles, xw):
            sts = [stat[:, tc] for tc in tiles]

            def pA():
                for i, tc in enumerate(tiles):
                    op("vector", lambda e, st=sts[i], tc=tc: e.reduce_sum(
                        st[:, 0:1], x_t[:, tc, :], AX.X),
                        wait=(xw if i == 0 else []),
                        inc=([("vS", 1)] if i == 3 else []))
                self_v = c["vS"]
                for i, tc in enumerate(tiles):
                    op("scalar", lambda e, st=sts[i], tc=tc: e.activation(
                        silu_sb[:, 0, :], x_t[:, tc, 0:512], AF.Square,
                        accum_out=st[:, 1:2]),
                        wait=(xw if i == 0 else []))
                    op("scalar", lambda e, st=sts[i], tc=tc: e.activation(
                        silu_sb[:, 0, :], x_t[:, tc, 512:1024], AF.Square,
                        accum_out=st[:, 2:3]))
                VS1[tiles[0]] = self_v

            def pB():
                for i in range(4):
                    op("scalar", lambda e, st=sts[i]: e.mul(st[:, 4:5], st[:, 0:1], 1.0 / D),
                       wait=([("vS", VS1[tiles[0]])] if i == 0 else []))
                for i in range(4):
                    op("scalar", lambda e, st=sts[i]: e.add(st[:, 3:4], st[:, 1:2], st[:, 2:3]))
                for i in range(4):
                    op("scalar", lambda e, st=sts[i]: e.mul(st[:, 5:6], st[:, 4:5], st[:, 4:5]))
                for i in range(4):
                    op("scalar", lambda e, st=sts[i]: e.mul(st[:, 6:7], st[:, 5:6], -1.0))

            def pC():
                for i in range(4):
                    op("scalar", lambda e, st=sts[i]: e.mul(st[:, 11:12], st[:, 3:4], 1.0 / D))
                for i in range(4):
                    op("scalar", lambda e, st=sts[i]: e.add(st[:, 7:8], st[:, 11:12], st[:, 6:7]))
                for i in range(4):
                    op("scalar", lambda e, st=sts[i]: e.activation(
                        st[:, 8:9], st[:, 7:8], AF.Ln, bias=epsc_s[:, 0:1]))
                for i in range(4):
                    op("scalar", lambda e, st=sts[i]: e.activation(
                        st[:, 9:10], st[:, 8:9], AF.Exp, scale=-0.5))

            def pD():
                for i in range(4):
                    op("scalar", lambda e, st=sts[i]: e.mul(
                        st[:, 10:11], st[:, 4:5], st[:, 9:10]),
                        inc=([("sS", 1)] if i == 3 else []))
                ST_READY[tiles[0]] = ("sS", c["sS"])
            return [pA, pB, pC, pD]

        def ln2v_pieces(tiles):
            sts = [stat[:, tc] for tc in tiles]

            def pE():
                for i, tc in enumerate(tiles):
                    op("vector", lambda e, st=sts[i], tc=tc: e.reduce_sum(
                        st[:, 0:1], x_t[:, tc, :], AX.X))
                for i, tc in enumerate(tiles):
                    op("vector", lambda e, st=sts[i], tc=tc, i=i: e.scalar_tensor_tensor(
                        h_t[:, i, :], x_t[:, tc, :], 1.0, x_t[:, tc, :],
                        ALU.mult, ALU.mult, accum_out=st[:, 1:2]),
                        wait=([("tS", H_T_FREE[0])] if i == 0 else []))

            def pF():
                for i in range(4):
                    op("vector", lambda e, st=sts[i]: e.tensor_scalar(
                        st[:, 4:5], st[:, 0:1], 1.0 / D, None, ALU.mult))
                for i in range(4):
                    op("vector", lambda e, st=sts[i]: e.tensor_mul(st[:, 5:6], st[:, 4:5], st[:, 4:5]))
                for i in range(4):
                    op("vector", lambda e, st=sts[i]: e.tensor_scalar(
                        st[:, 11:12], st[:, 1:2], 1.0 / D, None, ALU.mult))
                for i in range(4):
                    op("vector", lambda e, st=sts[i]: e.tensor_sub(
                        st[:, 7:8], st[:, 11:12], st[:, 5:6]),
                        inc=([("vS", 1)] if i == 3 else []))
                LN2_VAR[tiles[0]] = c["vS"]

            def pG():
                v_var = LN2_VAR[tiles[0]]
                for i in range(4):
                    op("scalar", lambda e, st=sts[i]: e.activation(
                        st[:, 8:9], st[:, 7:8], AF.Ln, bias=epsc_s[:, 0:1]),
                        wait=([("vS", v_var)] if i == 0 else []))
                for i in range(4):
                    op("scalar", lambda e, st=sts[i]: e.activation(
                        st[:, 9:10], st[:, 8:9], AF.Exp, scale=-0.5),
                        inc=([("sS", 1)] if i == 3 else []))
                s_exp = c["sS"]
                for i in range(4):
                    op("vector", lambda e, st=sts[i]: e.tensor_mul(
                        st[:, 10:11], st[:, 4:5], st[:, 9:10]),
                        wait=([("sS", s_exp)] if i == 0 else []),
                        inc=([("vS", 1)] if i == 3 else []))
                ST_READY[tiles[0]] = ("vS", c["vS"])
            return [pE, pF, pG]

        NB_FREE = [0] * H
        S_AO = [0] * H
        V_M32ADD = [0] * H
        S_mbf = [0] * H
        T_SCONS = [0] * H
        T_proj = [0] * NSC
        T_wo = [0] * NSC
        V_wo_evac = [0] * NSC
        V_PSUM23_FREE = 0
        T_attn_done = 0
        T_MID_BOX = [0]
        AOF_EVAC = [0] * 4
        WO_EVAC_V = [0] * (NSC * 8)

        PREV_SEVAC = [0] * 4
        V_RECV = [0] * H
        for sc in range(NSC):
            tc0 = sc * 4
            tiles = [tc0 + i for i in range(4)]
            last_sc = (sc == NSC - 1)
            if sc == 0:
                ln_stats(tiles, [("dIN", IN_X)])
                ln_ts(tiles)
            ln_tr(tiles, h_f, [0, 1, 2, 3], ev="mix")
            S_HF_W = [HF_EVAC[tc0 + 2], HF_EVAC[tc0 + 3]]
            S_hf = c["sS"]
            if sc == 0:
                tap("d_hf", h_f[:, :, :], [("sS", S_hf)])

            gidx = [0]
            PROJ_EVAC_S = [0] * 40
            pieces = {}
            pi = 6 if sc < NSC - 1 else 4
            step = 4 if sc < NSC - 1 else 3
            if sc + 1 < NSC:
                for p in ln_stats_pieces([tc0 + 4 + i for i in range(4)],
                                         [("dIN", IN_47 if sc == 0 else IN_ALL)]):
                    pieces[pi] = p
                    pi += step
            if sc > 0:
                for p in ln2v_pieces([tc0 - 4 + i for i in range(4)]):
                    pieces[pi] = p
                    pi += step

            def proj_group(mms, evac_fn, hf_wait=None, dw=None):
                g = gidx[0]
                if g in pieces:
                    pieces.pop(g)()
                if sc > 0 and g < 2:
                    bank = pb[2 + g]
                else:
                    bank = pb[g % 2]
                w = (S_HF_W if hf_wait is None else [hf_wait]) + [
                     ("dW", DW_W["q"] if dw is None else dw), ("dIN", IN_CONST)]
                if sc > 0 and g < 2:
                    w.append(("vS", PREV_SEVAC[2 + g]))
                elif sc > 0 and g < 4:
                    w.append(("vS", V_wo_evac[sc - 1]))
                elif g >= 2:
                    w.append(("sS", PROJ_EVAC_S[g - 2]))
                n = len(mms)
                for i, (lf, rf) in enumerate(mms):
                    op("tensor", lambda e, lf=lf, rf=rf, i=i, n=n, bank=bank: e.matmul(
                        bank[:, 0:512], lf(), rf(),
                        start=(i == 0), stop=(i == n - 1)),
                        wait=(w if i == 0 else []),
                        inc=([("tS", 1)] if i == n - 1 else []))
                evac_fn(bank, c["tS"], g)
                PROJ_EVAC_S[g] = c["sS"]
                gidx[0] += 1

            def ev_act(dest_ap, bias_ap):
                def f(bank, td, g):
                    op("scalar", lambda e: e.activation(dest_ap, bank[:, 0:512],
                                                        AF.Exp, bias=bias_ap),
                       wait=[("tS", td)], inc=[("sS", 1)])
                return f

            def ev_exp(dest_ap):
                def f(bank, td, g):
                    op("scalar", lambda e: e.activation(dest_ap, bank[:, 0:512], AF.Exp),
                       wait=[("tS", td)], inc=[("sS", 1)])
                return f

            def ev_v(tcm, hh):
                def f(bank, td, g):
                    dst = v_t[:, tcm].rearrange("p (h c) -> p h c", c=65)[
                        :, hh * 8:(hh + 1) * 8, 0:64]
                    src = bank[:, 0:512].rearrange("p (h c) -> p h c", c=64)
                    op("scalar", lambda e: e.copy(dst, src),
                       wait=[("tS", td), ("gS", MEMSET)], inc=[("sS", 1)])
                return f

            wsel = [(wv_bf, bvr_s, "v")] if last_sc else \
                   [(wv_bf, bvr_s, "v"), (wk_bf, bkr_s, "k")]
            for tcm in range(4):
                for wbf, brow, evf in wsel:
                    for hh in range(2):
                        mms = [(lambda: onesr_s[:, :],
                                lambda hh=hh, brow=brow: brow[:, hh * 512:(hh + 1) * 512])]
                        mms += [(lambda dc=dc, tcm=tcm: h_f[:, dc, tcm * 128:(tcm + 1) * 128],
                                 lambda dc=dc, hh=hh, wbf=wbf: wbf[:, dc, hh * 512:(hh + 1) * 512])
                                for dc in range(8)]
                        if evf == "k":
                            proj_group(mms, ev_exp(k_t[:, tcm, hh * 512:(hh + 1) * 512]),
                                       hf_wait=HF_EVAC[tc0 + tcm], dw=DW_W["k"])
                        else:
                            proj_group(mms, ev_v(tcm, hh),
                                       hf_wait=HF_EVAC[tc0 + tcm], dw=DW_W["v"])
            for wbf, dest, bcol in [(wq_bf, q_f, bqc_s), (wk_bf, k_f, bkc_s)]:
                for oc in range(8):
                    mms = [(lambda dc=dc, wbf=wbf, oc=oc: wbf[:, dc, oc * 128:(oc + 1) * 128],
                            lambda dc=dc: h_f[:, dc, :]) for dc in range(8)]
                    proj_group(mms, ev_act(dest[:, oc, :], bcol[:, oc:oc + 1]))
            for g in sorted(pieces):
                pieces.pop(g)()
            T_proj[sc] = c["tS"]
            PROJ_LAST_S = c["sS"]
            if sc == 0:
                tap("d_qf", q_f[:, :, :], [("sS", PROJ_LAST_S)])
                tap("d_kf", k_f[:, :, :], [("sS", PROJ_LAST_S)])
                tap("d_kt", k_t[:, :, :], [("sS", PROJ_LAST_S)])
                tap("d_vt", v_t[:, :, :], [("sS", PROJ_LAST_S)])


            V_sevac = [[0] * 4 for _ in range(H)]
            T_SCONS_NEW = [0] * H

            def emit_scores(h):
                p64 = 64 * (h % 2)
                hc = h // 2
                for j in range(4):
                    nj = 512 - j * 128
                    sbk = pb[j]
                    w = [("sS", PROJ_LAST_S)]
                    if h >= 1:
                        w.append(("vS", V_sevac[h - 1][j]))
                    elif sc > 0:
                        w.append(("vS", V_wo_evac[sc - 1] if j < 2 else PREV_SEVAC[j]))
                    op("tensor", lambda e, j=j, nj=nj, sbk=sbk, p64=p64, hc=hc: e.matmul(
                        sbk[:, 0:nj],
                        k_f[p64:p64 + 64, hc, j * 128:(j + 1) * 128],
                        q_f[p64:p64 + 64, hc, j * 128:512],
                        start=True, stop=True), wait=w, inc=[("tS", 1)])
                    t_sj = c["tS"]
                    wv = [("tS", t_sj)]
                    if h >= 2:
                        wv.append(("tS", T_SCONS_NEW[h - 2]))
                    elif sc > 0:
                        wv.append(("tS", T_SCONS[H - 2 + h]))
                    op("vector", lambda e, j=j, nj=nj, sbk=sbk, h=h: e.tensor_tensor(
                        s_sb[:, h % 2, SOFF[j]:SOFF[j] + nj],
                        sbk[:, 0:nj], mask_s[:, 0:nj], ALU.mult),
                        wait=wv, inc=[("vS", 1)])
                    V_sevac[h][j] = c["vS"]

            emit_scores(0)
            for h in range(H):
                p64 = 64 * (h % 2)
                hc = h // 2
                nb = pb[4 + (h % 2)]
                if h + 1 < H:
                    emit_scores(h + 1)
                first_w = [("sS", S_AO[h - 2] if h >= 2 else S_AO[H - 2 + h]),
                           ("vS", V_RECV[h - 2] if h >= 2 else V_RECV[H - 2 + h])]
                nmm = 0
                if sc > 0:
                    for tcx in range(4):
                        w = (first_w + [("sS", S_mbf[h])]) if tcx == 0 else []
                        op("tensor", lambda e, tcx=tcx, nb=nb, p64=p64, hc=hc: e.matmul(
                            nb[:, tcx * 65:tcx * 65 + 65],
                            q_f[p64:p64 + 64, hc, tcx * 128:(tcx + 1) * 128],
                            M_bf[p64:p64 + 64, hc, :],
                            start=(tcx == 0), stop=False), wait=w)
                        nmm += 1
                intra_list = [(j, tcx) for j in range(4) for tcx in range(j, 4)]
                for j, tcx in intra_list[:-1]:
                    if tcx == j:
                        w = [("vS", V_sevac[h][j])]
                    else:
                        w = []
                    if nmm == 0:
                        w = first_w + w
                    op("tensor", lambda e, j=j, tcx=tcx, stf=(nmm == 0), nb=nb, h=h: e.matmul(
                        nb[:, tcx * 65:tcx * 65 + 65],
                        s_sb[:, h % 2,
                             SOFF[j] + (tcx - j) * 128:SOFF[j] + (tcx - j) * 128 + 128],
                        v_head(j, h),
                        start=stf, stop=False),
                        wait=w)
                    nmm += 1
                if not last_sc:
                    for j in range(4):
                        op("tensor", lambda e, j=j, nb=nb, p64=p64, h=h: e.matmul(
                            nb[p64:p64 + 64, 260:325],
                            k_t[:, j, h * 64:(h + 1) * 64],
                            v_head(j, h),
                            start=False, stop=False))
                op("tensor", lambda e, nb=nb, h=h: e.matmul(
                    nb[:, 3 * 65:3 * 65 + 65],
                    s_sb[:, h % 2, SOFF[3]:SOFF[3] + 128],
                    v_head(3, h),
                    start=False, stop=True),
                    wait=[("vS", V_sevac[h][3])], inc=[("tS", 1)])
                T_SCONS_NEW[h] = c["tS"]
                t_upd = c["tS"]
                op("vector", lambda e, nb=nb, h=h: e.reciprocal(
                    recip[:, h % 2, :],
                    nb[:, 0:260].rearrange("p (t c) -> p t c", c=65)[:, :, 64:65]
                      .rearrange("p t o -> p (t o)")),
                    wait=[("tS", t_upd),
                          ("sS", S_AO[h - 2] if h >= 2 else S_AO[H - 2 + h])],
                    inc=[("vS", 1)])
                v_recip = c["vS"]
                if not last_sc:
                    op("vector", lambda e, nb=nb, p64=p64, hc=hc: e.tensor_tensor(
                        M32[p64:p64 + 64, hc, :], M32[p64:p64 + 64, hc, :],
                        nb[p64:p64 + 64, 260:325], ALU.add),
                        inc=[("vS", 1)])
                    V_M32ADD[h] = c["vS"]
                V_RECV[h] = c["vS"]
                for tcx in range(4):
                    op("scalar", lambda e, tcx=tcx, nb=nb, h=h: e.activation(
                        ao_t[:, tcx, h * 64:(h + 1) * 64],
                        nb[:, tcx * 65:tcx * 65 + 64],
                        AF.Copy, scale=recip[:, h % 2, tcx:tcx + 1]),
                        wait=([("vS", v_recip)] if tcx == 0 else []),
                        inc=([("sS", 1)] if tcx == 3 else []))
                S_AO[h] = c["sS"]
                if not last_sc:
                    op("scalar", lambda e, p64=p64, hc=hc: e.copy(
                        M_bf[p64:p64 + 64, hc, :], M32[p64:p64 + 64, hc, :]),
                        wait=[("vS", V_M32ADD[h])], inc=[("sS", 1)])
                    S_mbf[h] = c["sS"]
                if sc == NSC - 1 and h == 7:
                    T_MID_BOX[0] = c["tS"]
            PREV_SEVAC = list(V_sevac[H - 1])
            T_SCONS = [0] * (H - 2) + [T_SCONS_NEW[H - 2], T_SCONS_NEW[H - 1]]
            S_AO_DONE = c["sS"]
            T_attn_done = c["tS"]
            if sc + 1 < NSC:
                ln_ts([tc0 + 4 + i for i in range(4)])
            if sc == 0:
                tap("d_aot", ao_t[:, :, :], [("sS", S_AO_DONE)])
                tap("d_m32", M32[:, :, :], [("vS", V_RECV[H - 1])])

            def a4_trs(tcm):
                trb = tr[tcm % 2]
                for cc in range(8):
                    w = [("sS", S_AO_DONE), TR_FREE[tcm % 2]] if cc == 0 else []
                    op("tensor", lambda e, cc=cc, tcm=tcm, trb=trb: e.transpose(
                        trb[:, cc * 128:(cc + 1) * 128],
                        ao_t[:, tcm, cc * 128:(cc + 1) * 128], idn_s[:, :]),
                        wait=w, inc=([("tS", 1)] if cc == 7 else []))
                t_tra = c["tS"]
                for cc in range(8):
                    op("scalar", lambda e, cc=cc, trb=trb,
                       dst=(ao_f if tcm % 2 == 0 else ao_f2): e.copy(
                        dst[:, cc, :], trb[:, cc * 128:(cc + 1) * 128]),
                        wait=([("tS", t_tra)] if cc == 0 else []),
                        inc=([("sS", 1)] if cc == 7 else []))
                AOF_EVAC[tcm] = c["sS"]
                TR_FREE[tcm % 2] = ("sS", AOF_EVAC[tcm])

            def a4_wo(tcm):
                for hh in range(2):
                    bank = pb[hh]
                    gix = sc * 8 + tcm * 2 + hh
                    w0 = [("sS", AOF_EVAC[tcm])]
                    if sc == 0 and gix == 0:
                        w0.append(("dW", DW_W["o"]))
                    if gix >= 2:
                        w0.append(("vS", WO_EVAC_V[gix - 2]))
                    else:
                        w0.append(("vS", V_sevac[H - 1][hh]))
                    op("tensor", lambda e, hh=hh, bank=bank: e.matmul(
                        bank[:, 0:512], onesr_s[:, :],
                        bor_s[:, hh * 512:(hh + 1) * 512],
                        start=True, stop=False), wait=w0)
                    for dc in range(8):
                        op("tensor", lambda e, dc=dc, hh=hh, bank=bank,
                           src=(ao_f if tcm % 2 == 0 else ao_f2): e.matmul(
                            bank[:, 0:512], src[:, dc, :],
                            wo_bf[:, dc, hh * 512:(hh + 1) * 512],
                            start=False, stop=(dc == 7)),
                            inc=([("tS", 1)] if dc == 7 else []))
                    t_wo = c["tS"]
                    op("vector", lambda e, tcm=tcm, hh=hh, bank=bank: e.tensor_tensor(
                        x_t[:, tc0 + tcm, hh * 512:(hh + 1) * 512],
                        x_t[:, tc0 + tcm, hh * 512:(hh + 1) * 512],
                        bank[:, 0:512], ALU.add),
                        wait=[("tS", t_wo)], inc=[("vS", 1)])
                    WO_EVAC_V[gix] = c["vS"]

            a4_trs(0)
            a4_trs(1)
            a4_wo(0)
            a4_trs(2)
            a4_wo(1)
            a4_trs(3)
            a4_wo(2)
            a4_wo(3)
            T_wo[sc] = c["tS"]
            V_wo_evac[sc] = c["vS"]
            if sc == NSC - 1:
                tap("d_xa", x_t[:, :, :], [("vS", V_wo_evac[sc])])


        op("gpsimd", lambda e: e.dma_start(
            gw_bf[:, :, :], gwT_d.ap().rearrange("(c p) e -> p c e", p=128)),
            wait=[("tS", T_proj[NSC - 1])], inc=[("dW", 16)])
        DW_GW = c["dW"]
        DW_GUP = [0] * E
        DW_DWN = [0] * E
        op("gpsimd", lambda e: e.dma_start(
            gup_bf[:, 4:6, :],
            gup_d[0].rearrange("(c p) q -> p c q", p=128)[:, 4:6, :]),
            wait=[("tS", T_MID_BOX[0])], inc=[("dW", 16)])
        op("gpsimd", lambda e: e.dma_start(
            gup_bf[:, 0:4, :],
            gup_d[0].rearrange("(c p) q -> p c q", p=128)[:, 0:4, :]),
            wait=[("tS", T_attn_done)], inc=[("dW", 16)])
        op("gpsimd", lambda e: e.dma_start(
            gup_bf[:, 6:8, :],
            gup_d[0].rearrange("(c p) q -> p c q", p=128)[:, 6:8, :]),
            inc=[("dW", 16)])
        DW_GUP[0] = c["dW"]


        T_tr = [0] * NT
        HF_EVAC = [0] * NT
        S_exp = [0] * 4
        V_z = [0] * (E * 32)
        V_de = [0] * (E * 32)
        V_out_h = [0] * (NT * 2)
        zi_box = [0]
        DW_GUP_CUR = [DW_GUP[0]]
        DW_GUP_CUR_A = [DW_GUP[0]]
        T_GATE_QP3 = [0]

        def emit_gate(e_i, tq, pieces=None):
            for qp in range(8):
                idx = tq * 8 + qp
                zi = zi_box[0]
                bp = (pb[2], pb[3]) if idx % 2 == 0 else (pb[0], pb[1])
                w = [("dW", DW_GUP_CUR_A[0] if qp < 4 else DW_GUP_CUR[0])]
                if e_i == 0:
                    w += [HF_EVAC[tq * 4 + 2], HF_EVAC[tq * 4 + 3]]
                else:
                    w += [("vS", H2F_V_BOX[0]), ("sS", H2F_S_BOX[0])]
                if zi >= 2:
                    w.append(("vS", V_z[zi - 2]))
                if e_i == 0 and idx == 0:
                    w.append(("vS", PREV_SEVAC[3]))
                if e_i == 0 and idx == 1:
                    w.append(("vS", V_wo_evac[NSC - 1]))
                for half, bank in enumerate(bp):
                    qb = qp * 128 + half * 1024
                    for dc in range(8):
                        op("tensor", lambda e, dc=dc, qb=qb, tq=tq, bank=bank: e.matmul(
                            bank[:, 0:512],
                            gup_bf[:, dc, qb:qb + 128],
                            h2_f[:, dc, tq * 512:(tq + 1) * 512],
                            start=(dc == 0), stop=(dc == 7)),
                            wait=(w if (half == 0 and dc == 0) else []),
                            inc=([("tS", 1)] if (half == 1 and dc == 7) else []))
                t_gu = c["tS"]
                if qp == 3:
                    T_GATE_QP3[0] = t_gu
                ws = [("tS", t_gu)]
                if zi >= 2:
                    ws.append(("vS", V_z[zi - 2]))
                op("scalar", lambda e, bp=bp, idx=idx: e.activation(
                    silu_sb[:, idx % 2, :], bp[0][:, 0:512], AF.Silu),
                    wait=ws, inc=[("sS", 1)])
                s_si = c["sS"]
                if pieces and qp in pieces:
                    pieces.pop(qp)()
                op("vector", lambda e, bp=bp, idx=idx, qp=qp, tq=tq: e.tensor_tensor(
                    z_sb[:, qp, tq * 512:(tq + 1) * 512],
                    silu_sb[:, idx % 2, :], bp[1][:, 0:512], ALU.mult),
                    wait=[("sS", s_si)], inc=[("vS", 1)])
                V_z[zi] = c["vS"]
                zi_box[0] = zi + 1

        H2F_DONE_BOX = [0]
        H2F_V_BOX = [0]
        H2F_S_BOX = [0]
        ln_ts([0, 1, 2, 3])
        for g4 in range(4):
            tiles = [g4 * 4 + i for i in range(4)]
            ln_tr(tiles, h2_f, tiles, ev="vector")
            rb = pb[4 + (g4 % 2)]
            w = [HF_EVAC[tiles[2]], HF_EVAC[tiles[3]], ("dW", DW_GW)]
            if g4 >= 2:
                w.append(("sS", S_exp[g4 - 2]))
            else:
                w.append(("sS", S_AO[H - 2 + g4]))
                w.append(("vS", V_RECV[H - 2 + g4]))
            nmm = 0
            for i, tc in enumerate(tiles):
                for dc in range(8):
                    op("tensor", lambda e, dc=dc, tc=tc, rb=rb, i=i, nmm=nmm: e.matmul(
                        rb[:, i * 8:(i + 1) * 8], h2_f[:, dc, tc * 128:(tc + 1) * 128],
                        gw_bf[:, dc, :], start=(nmm == 0), stop=(nmm == 31)),
                        wait=(w if nmm == 0 else []),
                        inc=([("tS", 1)] if nmm == 31 else []))
                    nmm += 1
            t_rout = c["tS"]
            for i in range(4):
                op("vector", lambda e, i=i, rb=rb, rs=rstat[:, g4 * 4 + i]: e.reduce_max(
                    rs[:, 0:1], rb[:, i * 8:(i + 1) * 8], AX.X, negate=True),
                    wait=([("tS", t_rout)] if i == 0 else []),
                    inc=([("vS", 1)] if i == 3 else []))
            v_nmx = c["vS"]
            for i in range(4):
                op("scalar", lambda e, i=i, rb=rb, rs=rstat[:, g4 * 4 + i]: e.activation(
                    rt_e[:, i, :], rb[:, i * 8:(i + 1) * 8], AF.Exp, bias=rs[:, 0:1]),
                    wait=([("vS", v_nmx)] if i == 0 else []),
                    inc=([("sS", 1)] if i == 3 else []))
            S_exp[g4] = c["sS"]

            def R(fn, wait_first=None, inc_last=False):
                for i in range(4):
                    op("vector", lambda e, i=i, rs=rstat[:, g4 * 4 + i],
                       ee=rt_e[:, i], mk=msk1[:, i], tp=tmp8[:, i]: fn(e, i, rs, ee, mk, tp),
                       wait=(wait_first if (i == 0 and wait_first) else []),
                       inc=([("vS", 1)] if (i == 3 and inc_last) else []))

            R(lambda e, i, rs, ee, mk, tp: e.reduce_sum(rs[:, 1:2], ee, AX.X),
              wait_first=[("sS", S_exp[g4])])
            R(lambda e, i, rs, ee, mk, tp: e.reciprocal(rs[:, 2:3], rs[:, 1:2]))
            R(lambda e, i, rs, ee, mk, tp: e.tensor_scalar(
                ee, ee, rs[:, 2:3], None, ALU.mult))
            R(lambda e, i, rs, ee, mk, tp: e.reduce_max(rs[:, 3:4], ee, AX.X))
            R(lambda e, i, rs, ee, mk, tp: e.tensor_scalar(
                mk, ee, rs[:, 3:4], None, ALU.is_equal))
            R(lambda e, i, rs, ee, mk, tp: e.tensor_mul(tp, ee, mk))
            R(lambda e, i, rs, ee, mk, tp: e.tensor_sub(tp, ee, tp))
            R(lambda e, i, rs, ee, mk, tp: e.reduce_max(rs[:, 4:5], tp, AX.X))
            R(lambda e, i, rs, ee, mk, tp: e.tensor_scalar(
                tp, tp, rs[:, 4:5], None, ALU.is_equal))
            R(lambda e, i, rs, ee, mk, tp: e.tensor_add(mk, mk, tp))
            R(lambda e, i, rs, ee, mk, tp: e.scalar_tensor_tensor(
                rs[:, 5:6], rs[:, 3:4], 1e-9, rs[:, 4:5], ALU.add, ALU.add))
            R(lambda e, i, rs, ee, mk, tp: e.reciprocal(rs[:, 6:7], rs[:, 5:6]))
            R(lambda e, i, rs, ee, mk, tp: e.tensor_mul(ee, ee, mk))
            R(lambda e, i, rs, ee, mk, tp, g4=g4: e.tensor_scalar(
                routed[:, g4 * 4 + i, :], ee, rs[:, 6:7], None, ALU.mult),
              inc_last=True)
            if g4 + 1 < 4:
                ln_ts([g4 * 4 + 4 + i for i in range(4)])
            emit_gate(0, g4, pieces=(
                ln2_s_pieces([(NSC - 1) * 4 + i for i in range(4)])
                if g4 == 0 else None))
            if g4 == 0:
                op("gpsimd", lambda e: e.dma_start(
                    dwn_bf[:, :, :], dwn_d[0].rearrange("(c p) q -> p c q", p=128)),
                    wait=[("tS", T_wo[NSC - 1]), ("sS", LN2_JUNK[0])],
                    inc=[("dW", 16)])
                DW_DWN[0] = c["dW"]
        H2F_DONE = c["sS"]
        H2F_V = c["vS"]
        H2F_DONE_BOX[0] = H2F_DONE
        H2F_V_BOX[0] = H2F_V
        H2F_S_BOX[0] = H2F_DONE
        V_ROUTED_ALL = c["vS"]
        tap("d_h2f", h2_f[:, :, :], [("sS", H2F_DONE)])
        tap("d_routed", routed[:, :, :], [("vS", V_ROUTED_ALL)])

        di = 0
        for e_i in range(E):
            if e_i > 0:
                for tq in range(4):
                    emit_gate(e_i, tq)
            T_gu_last = c["tS"]
            V_z_all = c["vS"]
            if e_i == 0:
                tap("d_z0", z_sb[:, :, :], [("vS", V_z_all)])
            if e_i + 1 < E:
                t_qp3 = T_GATE_QP3[0]
                for j, (lo, hi) in enumerate([(0, 512), (1024, 1536),
                                              (512, 1024), (1536, 2048)]):
                    op("gpsimd", lambda e, e_i=e_i, lo=lo, hi=hi: e.dma_start(
                        gup_bf[:, :, lo:hi],
                        gup_d[e_i + 1].rearrange(
                            "(c p) q -> p c q", p=128)[:, :, lo:hi]),
                        wait=([("tS", t_qp3)] if j == 0 else
                              ([("tS", T_gu_last)] if j == 2 else [])),
                        inc=[("dW", 16)])
                    if j == 1:
                        DW_GUP_A = c["dW"]
                DW_GUP[e_i + 1] = c["dW"]
                DW_GUP_CUR[0] = c["dW"]
                DW_GUP_CUR_A[0] = DW_GUP_A
            for tcm in range(NT):
                for hh in range(2):
                    idx = tcm * 2 + hh
                    bank = pb[4 + (idx % 2)]
                    w = [("dW", DW_DWN[e_i]),
                         ("vS", V_z[e_i * 32 + (tcm // 4) * 8 + 7])]
                    if di >= 2:
                        w.append(("vS", V_de[di - 2]))
                    if e_i == 0:
                        w.append(("vS", V_ROUTED_ALL))
                        if idx < 2:
                            w.append(("sS", S_exp[2 + idx]))
                    for qc in range(8):
                        op("tensor", lambda e, qc=qc, tcm=tcm, hh=hh, bank=bank: e.matmul(
                            bank[:, 0:512],
                            z_sb[:, qc, tcm * 128:(tcm + 1) * 128],
                            dwn_bf[:, qc, hh * 512:(hh + 1) * 512],
                            start=(qc == 0), stop=(qc == 7)),
                            wait=(w if qc == 0 else []),
                            inc=([("tS", 1)] if qc == 7 else []))
                    t_d = c["tS"]
                    op("vector",
                       lambda e, tcm=tcm, hh=hh, bank=bank, e_i=e_i: e.scalar_tensor_tensor(
                           x_t[:, tcm, hh * 512:(hh + 1) * 512],
                           bank[:, 0:512], routed[:, tcm, e_i:e_i + 1],
                           x_t[:, tcm, hh * 512:(hh + 1) * 512],
                           ALU.mult, ALU.add),
                       wait=[("tS", t_d)], inc=[("vS", 1)])
                    V_de[di] = c["vS"]
                    di += 1
                    if e_i == E - 1:
                        V_out_h[tcm * 2 + hh] = c["vS"]
            T_dwn_last = c["tS"]
            if e_i + 1 < E:
                op("gpsimd", lambda e, e_i=e_i: e.dma_start(
                    dwn_bf[:, :, :],
                    dwn_d[e_i + 1].rearrange("(c p) q -> p c q", p=128)),
                    wait=[("tS", T_dwn_last)], inc=[("dW", 16)])
                DW_DWN[e_i + 1] = c["dW"]

        for tcm in range(NT):
            for hh in range(2):
                op("sync" if hh == 0 else "scalar",
                   lambda e, tcm=tcm, hh=hh: e.dma_start(
                    y_d[tcm * 128:(tcm + 1) * 128, hh * 512:(hh + 1) * 512],
                    x_t[:, tcm, hh * 512:(hh + 1) * 512]),
                   wait=[("vS", V_out_h[tcm * 2 + hh])], inc=[("dOUT", 16)])
        if which == "sync":
            eng.wait_ge(sems["dOUT"], c["dOUT"])

    with nc.Block() as block:
        @block.sync
        def _(e):
            walk("sync", e)

        @block.gpsimd
        def _(e):
            walk("gpsimd", e)

        @block.tensor
        def _(e):
            walk("tensor", e)

        @block.vector
        def _(e):
            walk("vector", e)

        @block.scalar
        def _(e):
            walk("scalar", e)

    return nc


_NC_CACHE = [None]


def kernel(**inputs):
    x = np.ascontiguousarray(np.asarray(inputs["L0_attn_x"], dtype=np.float32))
    wqT = np.ascontiguousarray(np.asarray(inputs["L0_attn_Wq"], dtype=np.float32).T)
    wkT = np.ascontiguousarray(np.asarray(inputs["L0_attn_Wk"], dtype=np.float32).T)
    wvT = np.ascontiguousarray(np.asarray(inputs["L0_attn_Wv"], dtype=np.float32).T)
    woT = np.ascontiguousarray(np.asarray(inputs["L0_attn_Wo"], dtype=np.float32).T)
    bq = np.asarray(inputs["L0_attn_bq"], dtype=np.float32)
    bk = np.asarray(inputs["L0_attn_bk"], dtype=np.float32)
    bv = np.asarray(inputs["L0_attn_bv"], dtype=np.float32)
    bo = np.asarray(inputs["L0_attn_bo"], dtype=np.float32)
    gw = np.asarray(inputs["L0_ffn_gate_w"], dtype=np.float32)
    gup = np.ascontiguousarray(np.asarray(inputs["L0_ffn_gate_up_proj"], dtype=np.float32))
    dwn = np.ascontiguousarray(np.asarray(inputs["L0_ffn_down_proj"], dtype=np.float32))

    common = dict(
        wqT=wqT, wkT=wkT, wvT=wvT, woT=woT,
        bq_col=np.ascontiguousarray(bq.reshape(8, 128).T),
        bk_col=np.ascontiguousarray(bk.reshape(8, 128).T),
        bk_row=np.ascontiguousarray(bk.reshape(1, D)),
        bv_row=np.ascontiguousarray(bv.reshape(1, D)),
        bo_row=np.ascontiguousarray(bo.reshape(1, D)),
        gwT=np.ascontiguousarray(gw.T), gup=gup, dwn=dwn,
        mask=np.concatenate([np.triu(np.ones((128, 128), np.float32)),
                             np.ones((128, 384), np.float32)], axis=1),
        idn=np.eye(128, dtype=np.float32).astype(ml_dtypes.bfloat16),
        ones_row=np.ones((1, 128), np.float32).astype(ml_dtypes.bfloat16),
        epsc=np.full((128, 1), 1e-5, np.float32),
    )
    if _NC_CACHE[0] is None:
        _NC_CACHE[0] = build_nc()
    nc = _NC_CACHE[0]
    in_maps = [dict(x=np.ascontiguousarray(x[b]), **common) for b in range(B)]
    res = run_bass_kernel_spmd(nc, in_maps, core_ids=list(range(B)))
    return np.stack([r["y"] for r in res.results], axis=0)
